# version 02
# baseline (speedup 1.0000x reference)
import numpy as np
import ml_dtypes
from contextlib import ExitStack
import concourse.bass as bass
import concourse.mybir as mybir
from concourse.bass_utils import run_bass_kernel_spmd
from concourse.alu_op_type import AluOpType as ALU

F32 = mybir.dt.float32
BF16 = mybir.dt.bfloat16
AF = mybir.ActivationFunctionType
AX = mybir.AxisListType.X

NCORES = 8
T = 4096
D = 1024
FF = 4096
NBLK = 4
S = 2048
NH = 16
NE = 8
CAP = 4915
LNE_A = 1e-6
LNE_T = 1e-5
ENGS = ('sync', 'gpsimd', 'scalar', 'vector', 'tensor')


class _Rec:
    def __getattr__(self, name):
        def call(*a, **k):
            return (name, a, k)
        return call


_REC = _Rec()


class P:
    def __init__(self, nc, es):
        self.nc = nc
        self.ops = []
        self.ctr = {}
        self.sems = {}
        for s in ('LD', 'GD', 'TE', 'VE', 'SE', 'GE', 'BAR', 'CC'):
            self.sems[s] = es.enter_context(nc.semaphore(s))
            self.ctr[s] = 0

    def emit(self, eng, fn):
        name, a, k = fn(_REC)
        self.ops.append((eng, name, a, k, None, 0))

    def cur(self, sem):
        return self.ctr[sem]

    def wait(self, eng, sem, val):
        h = self.sems[sem]
        self.ops.append((eng, 'wait_ge', (h, val), {}, None, 0))

    def op(self, eng, fn, sem=None, n=1):
        name, a, k = fn(_REC)
        if eng in ('vector', 'scalar', 'gpsimd') and sem in ('VE', 'SE', 'GE'):
            prev = getattr(self, '_last_' + eng, None)
            if prev is not None:
                self.ops.append((eng, 'wait_ge', (self.sems[prev[0]], prev[1]), {}, None, 0))
        if sem is None:
            self.ops.append((eng, name, a, k, None, 0))
            return None
        self.ctr[sem] += n
        self.ops.append((eng, name, a, k, self.sems[sem], n))
        if eng in ('vector', 'scalar', 'gpsimd') and sem in ('VE', 'SE', 'GE'):
            setattr(self, '_last_' + eng, (sem, self.ctr[sem]))
        return self.ctr[sem]

    def dma(self, eng, out, in_, **kw):
        sem = {'sync': 'LD', 'gpsimd': 'GD', 'scalar': 'LD'}[eng]
        return self.op(eng, lambda e: e.dma_start(out=out, in_=in_, **kw), sem, 16)

    def barrier(self, engines=ENGS):
        if 'tensor' in engines:
            self.wait('sync', 'TE', self.ctr['TE'])
        if 'vector' in engines:
            self.wait('sync', 'VE', self.ctr['VE'])
        if 'scalar' in engines:
            self.wait('sync', 'SE', self.ctr['SE'])
        if 'gpsimd' in engines:
            self.wait('sync', 'GE', self.ctr['GE'])
            self.wait('sync', 'GD', self.ctr['GD'])
        self.wait('sync', 'LD', self.ctr['LD'])
        self.ctr['BAR'] += 1
        self.ops.append(('sync', 'nop', (), {}, self.sems['BAR'], 1))
        tgt = self.ctr['BAR']
        for eng in engines:
            if eng != 'sync':
                self.wait(eng, 'BAR', tgt)


def build_program(stage=None):
    nc = bass.Bass("TRN2", target_bir_lowering=False)
    ges = ExitStack()

    def din(name, shape, dt=F32):
        return nc.dram_tensor(name, shape, dt, kind="ExternalInput").ap()
    xin = din("xin", [T, D])
    aWq = din("aWq", [NBLK, D, D]); aWk = din("aWk", [NBLK, D, D])
    aWv = din("aWv", [NBLK, D, D]); aWo = din("aWo", [NBLK, D, D])
    aW1 = din("aW1", [NBLK, D, FF]); aW2 = din("aW2", [NBLK, FF, D])
    abq = din("abq", [NBLK, D]); abk = din("abk", [NBLK, D]); abv = din("abv", [NBLK, D])
    abo = din("abo", [NBLK, D]); ab1 = din("ab1", [NBLK, FF]); ab2 = din("ab2", [NBLK, D])
    aln1s = din("aln1s", [NBLK, D]); aln1b = din("aln1b", [NBLK, D])
    aln2s = din("aln2s", [NBLK, D]); aln2b = din("aln2b", [NBLK, D])
    fWq = din("fWq", [1024, D]); fWk = din("fWk", [1024, D]); fWv = din("fWv", [1024, D])
    fbq = din("fbq", [1024]); fbk = din("fbk", [1024]); fbv = din("fbv", [1024])
    fWe = din("fWe", [NE, D, D]); fbe = din("fbe", [NE, D])
    fWsw = din("fWsw", [NE, D]); fbsw = din("fbsw", [NE])
    gam = din("gam", [D]); bet = din("bet", [D])
    c_id = din("c_id", [128, 128]); c_sl = din("c_sl", [128, 128])
    c_mask = din("c_mask", [128, 4 * 512], BF16); c_io8 = din("c_io8", [128, 8])
    c_rm8 = din("c_rm8", [128, 8]); c_rmask = din("c_rmask", [8, 1])
    c_e127 = din("c_e127", [128, 1])
    out = nc.dram_tensor("out", [T, D], F32, kind="ExternalOutput").ap()

    uA = nc.dram_tensor("uA", [D, T], F32).ap()
    uB = nc.dram_tensor("uB", [D, T], F32).ap()
    x1T = nc.dram_tensor("x1T", [D, T], F32).ap()
    xmid = nc.dram_tensor("xmid", [T, D], F32).ap()
    u2tok = nc.dram_tensor("u2tok", [T, D], F32).ap()
    u2T = nc.dram_tensor("u2T", [D, T], F32).ap()
    hidT = nc.dram_tensor("hidT", [D, T], BF16).ap()
    agin = nc.dram_tensor("agin", [1, 8], F32)
    agout = nc.dram_tensor("agout", [8, 8], F32)

    p = P(nc, ges)
    gsb = lambda n, s, d=F32: ges.enter_context(nc.sbuf_tensor(n, s, d))

    ident = gsb("ident", [128, 128])
    identb = gsb("identb", [128, 128], BF16)
    sltri = gsb("sltri", [128, 128])
    allonesb = gsb("allonesb", [128, 512], BF16)
    io8 = gsb("io8", [128, 8]); rm8 = gsb("rm8", [128, 8]); rmask = gsb("rmask", [8, 1])
    ones128 = gsb("ones128", [128, 1]); onesrow = gsb("onesrow", [1, 128])
    e127 = gsb("e127", [128, 1])
    vq = gsb("vq", [128, 32]); vk = gsb("vk", [128, 32]); vv = gsb("vv", [128, 32])
    vo = gsb("vo", [128, 32]); v1 = gsb("v1", [128, 128]); v2 = gsb("v2", [128, 32])
    vnq = gsb("vnq", [128, 32]); vnk = gsb("vnk", [128, 32])
    vl1s = gsb("vl1s", [128, 32]); vl1b = gsb("vl1b", [128, 32])
    vl2s = gsb("vl2s", [128, 32]); vl2b = gsb("vl2b", [128, 32])
    fvq = gsb("fvq", [128, 8]); fvk = gsb("fvk", [128, 8])
    fbsw_row = gsb("fbsw_row", [1, 8]); wswT = gsb("wswT", [128, 64])
    scr = [gsb(f"scr{i}", [128, 512]) for i in range(3)]
    abrow = gsb("abrow", [1, 1024]); strow = gsb("strow", [1, 1024])
    r128a = gsb("r128a", [128, 1]); r128b = gsb("r128b", [128, 1])
    r128c = gsb("r128c", [128, 1]); r128d = gsb("r128d", [128, 1])
    kvacc = gsb("kvacc", [128, 16]); ksacc = gsb("ksacc", [128, 16])
    oh_all = gsb("oh_all", [128, 256]); rank_all = gsb("rank_all", [128, 256])
    rp_all = gsb("rp_all", [128, 32])
    carry = gsb("carry", [1, 8]); offs_row = gsb("offs_row", [1, 8])
    cnt8 = gsb("cnt8", [8, 8]); offcol = gsb("offcol", [8, 1])

    bank = [ges.enter_context(nc.psum_tensor(f"bank{i}", [128, 512], F32)) for i in range(7)]
    pbf = ges.enter_context(nc.psum_tensor("pbf", [128, 1024], BF16))

    p.dma('sync', ident[:], c_id)
    p.dma('sync', sltri[:], c_sl)
    p.dma('sync', io8[:], c_io8)
    p.dma('sync', rm8[:], c_rm8)
    p.dma('sync', rmask[:], c_rmask)
    p.dma('sync', e127[:], c_e127)
    p.dma('sync', fbsw_row[:], fbsw.rearrange("(a b) -> a b", a=1))
    v = p.dma('gpsimd', identb[:], c_id)
    p.op('gpsimd', lambda e: e.memset(ones128[:], 1.0), 'GE')
    p.op('gpsimd', lambda e: e.memset(onesrow[:], 1.0), 'GE')
    p.op('gpsimd', lambda e: e.memset(carry[:], 0.0), 'GE')
    p.op('gpsimd', lambda e: e.memset(allonesb[:], 1.0), 'GE')

    def load_vec(dst, col0, row, n):
        v = p.dma('sync', scr[2][0:n, 0:128], row.rearrange("(a b) -> a b", b=128))
        p.wait('tensor', 'LD', v)
        te = p.op('tensor', lambda e, n=n: e.transpose(bank[6][:, 0:n], scr[2][0:n, 0:128], ident[0:n, 0:n]), 'TE')
        p.wait('scalar', 'TE', te)
        se = p.op('scalar', lambda e, dst=dst, col0=col0, n=n: e.copy(dst[:, col0:col0 + n],
                                                                      bank[6][:, 0:n]), 'SE')
        p.wait('tensor', 'SE', se)
        p.wait('sync', 'SE', se)

    for i in range(NBLK):
        load_vec(vq, i * 8, abq[i, :], 8)
        load_vec(vk, i * 8, abk[i, :], 8)
        load_vec(vv, i * 8, abv[i, :], 8)
        load_vec(vo, i * 8, abo[i, :], 8)
        load_vec(v1, i * 32, ab1[i, :], 32)
        load_vec(v2, i * 8, ab2[i, :], 8)
        load_vec(vl1s, i * 8, aln1s[i, :], 8)
        load_vec(vl1b, i * 8, aln1b[i, :], 8)
        load_vec(vl2s, i * 8, aln2s[i, :], 8)
        load_vec(vl2b, i * 8, aln2b[i, :], 8)
    load_vec(fvq, 0, fbq, 8)
    load_vec(fvk, 0, fbk, 8)
    v = p.dma('sync', scr[2][0:8, 0:512], fWsw[:, 0:512])
    v = p.dma('sync', scr[1][0:8, 0:512], fWsw[:, 512:1024])
    p.wait('tensor', 'LD', v)
    for dc in range(8):
        src = scr[2] if dc < 4 else scr[1]
        cof = dc * 128 if dc < 4 else (dc - 4) * 128
        te = p.op('tensor', lambda e, dc=dc, src=src, cof=cof: e.transpose(
            bank[6][:, dc * 8:(dc + 1) * 8], src[0:8, cof:cof + 128], ident[0:8, 0:8]), 'TE')
    p.wait('scalar', 'TE', p.cur('TE'))
    se = p.op('scalar', lambda e: e.copy(wswT[:], bank[6][:, 0:64]), 'SE')
    p.wait('tensor', 'SE', se)
    p.wait('vector', 'SE', p.cur('SE'))
    ve = p.op('vector', lambda e: e.tensor_scalar(vnq[:], vq[:], -1.0, None, op0=ALU.mult), 'VE')
    ve = p.op('vector', lambda e: e.tensor_scalar(vnk[:], vk[:], -1.0, None, op0=ALU.mult), 'VE')
    p.barrier()

    def ln_tok(xt, ot, eps, tmbuf, gamma=None, beta=None, resid=None):
        ve = p.op('vector', lambda e: e.reduce_sum(r128a[:], xt[:], axis=AX), 'VE')
        ve = p.op('vector', lambda e: e.tensor_scalar(r128a[:], r128a[:], 1.0 / D, None, op0=ALU.mult), 'VE')
        ve = p.op('vector', lambda e: e.tensor_scalar(tmbuf[:, 0:1024], xt[:], r128a[:], None,
                                                      op0=ALU.subtract), 'VE')
        p.wait('scalar', 'VE', ve)
        se = p.op('scalar', lambda e: e.activation(tmbuf[:, 1024:2048], tmbuf[:, 0:1024], AF.Square,
                                                   accum_out=r128b[:]), 'SE')
        p.wait('vector', 'SE', se)
        ve = p.op('vector', lambda e, eps=eps: e.tensor_scalar(r128b[:], r128b[:], 1.0 / D, eps,
                                                               op0=ALU.mult, op1=ALU.add), 'VE')
        p.wait('scalar', 'VE', ve)
        se = p.op('scalar', lambda e: e.activation(r128c[:], r128b[:], AF.Ln), 'SE')
        se = p.op('scalar', lambda e: e.activation(r128d[:], r128c[:], AF.Exp, scale=-0.5), 'SE')
        p.wait('vector', 'SE', se)
        if gamma is None:
            ve = p.op('vector', lambda e: e.tensor_scalar(ot[:], tmbuf[:, 0:1024], r128d[:], None,
                                                          op0=ALU.mult), 'VE')
        else:
            ve = p.op('vector', lambda e: e.scalar_tensor_tensor(ot[:], tmbuf[:, 0:1024], r128d[:],
                                                                 gamma[:], op0=ALU.mult, op1=ALU.mult), 'VE')
            ve = p.op('vector', lambda e: e.tensor_tensor(ot[:], ot[:], beta[:], op=ALU.add), 'VE')
            if resid is not None:
                ve = p.op('vector', lambda e: e.tensor_tensor(ot[:], ot[:], resid[:], op=ALU.add), 'VE')
        return ve

    def ln_feat(tin, Tc, sv, bv, base, eps, tout, sqb):
        p.wait('tensor', 'VE', p.cur('VE'))
        p.wait('scalar', 'VE', p.cur('VE'))
        for dc in range(8):
            se = p.op('scalar', lambda e, dc=dc: e.activation(sqb[:, dc * Tc:(dc + 1) * Tc],
                                                              tin(dc), AF.Square), 'SE')
        se_sq = se
        for dc in range(8):
            te = p.op('tensor', lambda e, dc=dc: e.matmul(bank[5][0:1, 0:Tc], ones128[:], tin(dc),
                                                          start=(dc == 0), stop=(dc == 7)), 'TE')
        p.wait('tensor', 'SE', se_sq)
        for dc in range(8):
            te = p.op('tensor', lambda e, dc=dc: e.matmul(bank[6][0:1, 0:Tc], ones128[:],
                                                          sqb[:, dc * Tc:(dc + 1) * Tc],
                                                          start=(dc == 0), stop=(dc == 7)), 'TE')
        p.wait('vector', 'TE', te)
        ve = p.op('vector', lambda e, Tc=Tc: e.tensor_scalar(strow[0:1, 0:Tc], bank[5][0:1, 0:Tc],
                                                             1.0 / D, None, op0=ALU.mult), 'VE')
        ve = p.op('vector', lambda e, Tc=Tc: e.tensor_tensor(strow[0:1, 512:512 + Tc], strow[0:1, 0:Tc],
                                                             strow[0:1, 0:Tc], op=ALU.mult), 'VE')
        ve = p.op('vector', lambda e, Tc=Tc: e.scalar_tensor_tensor(strow[0:1, 512:512 + Tc],
                                                                    bank[6][0:1, 0:Tc], 1.0 / D,
                                                                    strow[0:1, 512:512 + Tc],
                                                                    op0=ALU.mult, op1=ALU.subtract), 'VE')
        ve = p.op('vector', lambda e: e.tensor_scalar(strow[0:1, 512:512 + Tc],
                                                      strow[0:1, 512:512 + Tc], eps, None,
                                                      op0=ALU.add), 'VE')
        p.wait('scalar', 'VE', ve)
        se = p.op('scalar', lambda e: e.activation(strow[0:1, 512:512 + Tc],
                                                   strow[0:1, 512:512 + Tc], AF.Ln), 'SE')
        se = p.op('scalar', lambda e, Tc=Tc: e.activation(abrow[0:1, 0:Tc], strow[0:1, 512:512 + Tc],
                                                          AF.Exp, scale=-0.5), 'SE')
        p.wait('vector', 'SE', se)
        ve = p.op('vector', lambda e, Tc=Tc: e.tensor_tensor(abrow[0:1, 512:512 + Tc], strow[0:1, 0:Tc],
                                                             abrow[0:1, 0:Tc], op=ALU.mult), 'VE')
        p.wait('tensor', 'VE', ve)
        te = p.op('tensor', lambda e, Tc=Tc: e.matmul(bank[3][:, 0:Tc], onesrow[0:1, :],
                                                      abrow[0:1, 0:Tc], start=True, stop=True), 'TE')
        te = p.op('tensor', lambda e, Tc=Tc: e.matmul(bank[4][:, 0:Tc], onesrow[0:1, :],
                                                      abrow[0:1, 512:512 + Tc], start=True, stop=True), 'TE')
        p.wait('vector', 'TE', te)
        for dc in range(8):
            ve = p.op('vector', lambda e, dc=dc: e.tensor_tensor(sqb[:, dc * Tc:(dc + 1) * Tc],
                                                                 tin(dc), bank[3][:, 0:Tc], op=ALU.mult), 'VE')
            ve = p.op('vector', lambda e, dc=dc: e.tensor_tensor(sqb[:, dc * Tc:(dc + 1) * Tc],
                                                                 sqb[:, dc * Tc:(dc + 1) * Tc],
                                                                 bank[4][:, 0:Tc], op=ALU.subtract), 'VE')
            p.wait('scalar', 'VE', ve)
            se = p.op('scalar', lambda e, dc=dc: e.activation(tout(dc), sqb[:, dc * Tc:(dc + 1) * Tc],
                                                              AF.Identity,
                                                              bias=bv[:, base + dc:base + dc + 1],
                                                              scale=sv[:, base + dc:base + dc + 1]), 'SE')
        p.wait('tensor', 'SE', se)
        p.wait('vector', 'SE', se)
        return se

    with ExitStack() as ps0:
        sb0 = lambda n, s, d=F32: ps0.enter_context(nc.sbuf_tensor(n, s, d))
        xa = sb0("xa", [128, 1024]); ua = sb0("ua", [128, 1024])
        tm = sb0("tm", [128, 2048]); ustg = sb0("ustg", [128, 4096])
        for tg in range(8):
            for j in range(4):
                tc = tg * 4 + j
                v = p.dma('sync', xa[:], xin[tc * 128:(tc + 1) * 128, :])
                p.wait('vector', 'LD', v)
                ve = ln_tok(xa, ua, LNE_T, tm)
                p.wait('tensor', 'VE', ve)
                for dc in range(8):
                    bk = dc % 2
                    te = p.op('tensor', lambda e, dc=dc, bk=bk: e.transpose(
                        bank[bk][:, 0:128], ua[:, dc * 128:(dc + 1) * 128], ident[:]), 'TE')
                    p.wait('scalar', 'TE', te)
                    se = p.op('scalar', lambda e, dc=dc, j=j, bk=bk: e.copy(
                        ustg[:, dc * 512 + j * 128: dc * 512 + (j + 1) * 128], bank[bk][:, 0:128]), 'SE')
                    p.wait('tensor', 'SE', se)
                p.barrier(('sync', 'vector', 'scalar', 'tensor'))
            for dc in range(8):
                v = p.dma('sync', uA[dc * 128:(dc + 1) * 128, tg * 512:(tg + 1) * 512],
                          ustg[:, dc * 512:(dc + 1) * 512])
            p.wait('scalar', 'LD', v)
            p.barrier(('sync', 'scalar'))
        p.barrier()

    if stage == "p0":
        ov = out.rearrange("t d -> (t d)").rearrange("(d t) -> d t", d=D)
        for dc in range(8):
            p.dma('sync', ov[dc * 128:(dc + 1) * 128, :], uA[dc * 128:(dc + 1) * 128, :])
        p.wait('sync', 'LD', p.cur('LD'))
        return finish(nc, ges, p)

    ucur, unxt = uA, uB
    with ExitStack() as psa:
        for blk in range(NBLK):
            b8 = blk * 8
            b32 = blk * 32
            with ExitStack() as ps1:
                sb1 = lambda n, s, d=F32: ps1.enter_context(nc.sbuf_tensor(n, s, d))
                wA = sb1(f"wA{blk}", [128, 8192], BF16)
                wB = sb1(f"wB{blk}", [128, 8192], BF16)
                ubf = sb1(f"ubf{blk}", [128, 4096], BF16)
                vbf = sb1(f"vbf{blk}", [128, 4096], BF16)
                ufp = sb1(f"ufp{blk}", [128, 4096])
                t1b = sb1(f"t1b{blk}", [128, 4096])
                sqa = sb1(f"sqa{blk}", [128, 4096])
                for c in range(8):
                    p.dma('gpsimd', wA[:, c * 1024:(c + 1) * 1024], aWk[blk, c * 128:(c + 1) * 128, :])
                    gv = p.dma('gpsimd', wB[:, c * 1024:(c + 1) * 1024], aWv[blk, c * 128:(c + 1) * 128, :])
                p.op('gpsimd', lambda e: e.memset(kvacc[:], 0.0), 'GE')
                ge = p.op('gpsimd', lambda e: e.memset(ksacc[:], 0.0), 'GE')
                p.wait('tensor', 'GD', gv)
                p.wait('vector', 'GE', ge)
                for tc in range(8):
                    bi = tc // 4
                    for dc in range(8):
                        gv = p.dma('gpsimd', ubf[:, dc * 512:(dc + 1) * 512],
                                   ucur[dc * 128:(dc + 1) * 128, tc * 512:(tc + 1) * 512])
                    p.wait('tensor', 'GD', gv)
                    for oc in range(8):
                        bk_k = (oc % 2) * 2
                        bk_v = bk_k + 1
                        for dc in range(8):
                            te = p.op('tensor', lambda e, dc=dc, oc=oc, bk_k=bk_k: e.matmul(
                                bank[bk_k][:], wA[:, dc * 1024 + oc * 128: dc * 1024 + (oc + 1) * 128],
                                ubf[:, dc * 512:(dc + 1) * 512], start=(dc == 0), stop=(dc == 7)), 'TE')
                        te_k = te
                        for dc in range(8):
                            te = p.op('tensor', lambda e, dc=dc, oc=oc, bk_v=bk_v: e.matmul(
                                bank[bk_v][:], wB[:, dc * 1024 + oc * 128: dc * 1024 + (oc + 1) * 128],
                                ubf[:, dc * 512:(dc + 1) * 512], start=(dc == 0), stop=(dc == 7)), 'TE')
                        te_v = te
                        p.wait('scalar', 'TE', te_k)
                        se = p.op('scalar', lambda e, oc=oc, bk_k=bk_k: e.activation(
                            scr[0][:], bank[bk_k][:], AF.Relu, bias=vk[:, b8 + oc:b8 + oc + 1]), 'SE')
                        se = p.op('scalar', lambda e, oc=oc, bk_k=bk_k: e.activation(
                            scr[1][:], bank[bk_k][:], AF.Relu, bias=vnk[:, b8 + oc:b8 + oc + 1],
                            scale=-1.0), 'SE')
                        se = p.op('scalar', lambda e: e.activation(scr[1][:], scr[1][:], AF.Exp,
                                                                   scale=-1.0), 'SE')
                        p.wait('scalar', 'TE', te_v)
                        se = p.op('scalar', lambda e, oc=oc, bk_v=bk_v: e.activation(
                            scr[2][:], bank[bk_v][:], AF.Identity, bias=vv[:, b8 + oc:b8 + oc + 1]), 'SE')
                        p.wait('vector', 'SE', se)
                        ve = p.op('vector', lambda e: e.tensor_tensor(scr[0][:], scr[0][:], scr[1][:],
                                                                      op=ALU.add), 'VE')
                        ve = p.op('vector', lambda e: e.reduce_sum(r128b[:], scr[0][:], axis=AX), 'VE')
                        ve = p.op('vector', lambda e, oc=oc, bi=bi: e.tensor_tensor(
                            ksacc[:, bi * 8 + oc:bi * 8 + oc + 1], ksacc[:, bi * 8 + oc:bi * 8 + oc + 1],
                            r128b[:], op=ALU.add), 'VE')
                        ve = p.op('vector', lambda e: e.tensor_tensor(scr[0][:], scr[0][:], scr[2][:],
                                                                      op=ALU.mult), 'VE')
                        ve = p.op('vector', lambda e: e.reduce_sum(r128a[:], scr[0][:], axis=AX), 'VE')
                        ve = p.op('vector', lambda e, oc=oc, bi=bi: e.tensor_tensor(
                            kvacc[:, bi * 8 + oc:bi * 8 + oc + 1], kvacc[:, bi * 8 + oc:bi * 8 + oc + 1],
                            r128a[:], op=ALU.add), 'VE')
                        p.wait('tensor', 'VE', ve)
                        p.wait('scalar', 'VE', ve)
                    p.barrier(('gpsimd', 'tensor', 'scalar', 'vector'))
                for c in range(8):
                    p.dma('gpsimd', wA[:, c * 1024:(c + 1) * 1024], aWq[blk, c * 128:(c + 1) * 128, :])
                    gv = p.dma('gpsimd', wB[:, c * 1024:(c + 1) * 1024], aWo[blk, c * 128:(c + 1) * 128, :])
                p.wait('tensor', 'GD', gv)
                for tc in range(8):
                    bi = tc // 4
                    for dc in range(8):
                        v = p.dma('sync', ufp[:, dc * 512:(dc + 1) * 512],
                                  ucur[dc * 128:(dc + 1) * 128, tc * 512:(tc + 1) * 512])
                        gv = p.dma('gpsimd', ubf[:, dc * 512:(dc + 1) * 512],
                                   ucur[dc * 128:(dc + 1) * 128, tc * 512:(tc + 1) * 512])
                    p.wait('tensor', 'GD', gv)
                    for oc in range(8):
                        bk_q = oc % 2
                        for dc in range(8):
                            te = p.op('tensor', lambda e, dc=dc, oc=oc, bk_q=bk_q: e.matmul(
                                bank[bk_q][:], wA[:, dc * 1024 + oc * 128: dc * 1024 + (oc + 1) * 128],
                                ubf[:, dc * 512:(dc + 1) * 512], start=(dc == 0), stop=(dc == 7)), 'TE')
                        p.wait('scalar', 'TE', te)
                        se = p.op('scalar', lambda e, oc=oc, bk_q=bk_q: e.activation(
                            scr[0][:], bank[bk_q][:], AF.Relu, bias=vq[:, b8 + oc:b8 + oc + 1]), 'SE')
                        se = p.op('scalar', lambda e, oc=oc, bk_q=bk_q: e.activation(
                            scr[1][:], bank[bk_q][:], AF.Relu, bias=vnq[:, b8 + oc:b8 + oc + 1],
                            scale=-1.0), 'SE')
                        se = p.op('scalar', lambda e: e.activation(scr[1][:], scr[1][:], AF.Exp,
                                                                   scale=-1.0), 'SE')
                        p.wait('vector', 'SE', se)
                        ve = p.op('vector', lambda e: e.tensor_tensor(scr[0][:], scr[0][:], scr[1][:],
                                                                      op=ALU.add), 'VE')
                        p.wait('tensor', 'VE', ve)
                        ve = p.op('vector', lambda e, oc=oc, bi=bi: e.tensor_scalar(
                            scr[1][:], scr[0][:], ksacc[:, bi * 8 + oc:bi * 8 + oc + 1], 1e-6,
                            op0=ALU.mult, op1=ALU.add), 'VE')
                        ve = p.op('vector', lambda e: e.reciprocal(scr[2][:], scr[1][:]), 'VE')
                        ve = p.op('vector', lambda e, oc=oc, bi=bi: e.scalar_tensor_tensor(
                            vbf[:, oc * 512:(oc + 1) * 512], scr[0][:],
                            kvacc[:, bi * 8 + oc:bi * 8 + oc + 1], scr[2][:],
                            op0=ALU.mult, op1=ALU.mult), 'VE')
                        p.wait('scalar', 'VE', ve)
                    p.wait('tensor', 'VE', ve)
                    for dc in range(8):
                        bk_a = dc % 2
                        for ac in range(8):
                            te = p.op('tensor', lambda e, ac=ac, dc=dc, bk_a=bk_a: e.matmul(
                                bank[bk_a][:], wB[:, ac * 1024 + dc * 128: ac * 1024 + (dc + 1) * 128],
                                vbf[:, ac * 512:(ac + 1) * 512], start=(ac == 0), stop=(ac == 7)), 'TE')
                        p.wait('scalar', 'TE', te)
                        se = p.op('scalar', lambda e, dc=dc, bk_a=bk_a: e.activation(
                            scr[0][:], bank[bk_a][:], AF.Gelu_apprx_tanh,
                            bias=vo[:, b8 + dc:b8 + dc + 1]), 'SE')
                        se = p.op('scalar', lambda e: e.activation(scr[1][:], scr[0][:],
                                                                   AF.Gelu_apprx_tanh), 'SE')
                        p.wait('vector', 'SE', se)
                        ve = p.op('vector', lambda e, dc=dc: e.tensor_tensor(
                            t1b[:, dc * 512:(dc + 1) * 512], scr[1][:], ufp[:, dc * 512:(dc + 1) * 512],
                            op=ALU.add), 'VE')
                        p.wait('tensor', 'VE', ve)
                        p.wait('scalar', 'VE', ve)
                    ln_feat(lambda dc: t1b[:, dc * 512:(dc + 1) * 512], 512, vl1s, vl1b, b8, LNE_A,
                            lambda dc: ufp[:, dc * 512:(dc + 1) * 512], sqa)
                    p.wait('sync', 'SE', p.cur('SE'))
                    for dc in range(8):
                        v = p.dma('sync', x1T[dc * 128:(dc + 1) * 128, tc * 512:(tc + 1) * 512],
                                  ufp[:, dc * 512:(dc + 1) * 512])
                    p.barrier()
            with ExitStack() as ps2:
                sb2 = lambda n, s, d=F32: ps2.enter_context(nc.sbuf_tensor(n, s, d))
                w1sb = sb2(f"w1sb{blk}", [128, 32768], BF16)
                w2sb = sb2(f"w2sb{blk}", [128, 32768], BF16)
                x1bf = sb2(f"x1bf{blk}", [128, 2048], BF16)
                hbf = sb2(f"hbf{blk}", [128, 8192], BF16)
                ufp = sb2(f"ufp2{blk}", [128, 2048])
                t1b = sb2(f"t1b2{blk}", [128, 2048])
                sqa = sb2(f"sqa2{blk}", [128, 2048])
                for c in range(8):
                    gv = p.dma('gpsimd', w1sb[:, c * 4096:(c + 1) * 4096],
                               aW1[blk, c * 128:(c + 1) * 128, :])
                for c in range(32):
                    gv = p.dma('gpsimd', w2sb[:, c * 1024:(c + 1) * 1024],
                               aW2[blk, c * 128:(c + 1) * 128, :])
                p.wait('tensor', 'GD', gv)
                for tc in range(16):
                    for dc in range(8):
                        v = p.dma('sync', ufp[:, dc * 256:(dc + 1) * 256],
                                  x1T[dc * 128:(dc + 1) * 128, tc * 256:(tc + 1) * 256])
                        gv = p.dma('gpsimd', x1bf[:, dc * 256:(dc + 1) * 256],
                                   x1T[dc * 128:(dc + 1) * 128, tc * 256:(tc + 1) * 256])
                    p.wait('tensor', 'GD', gv)
                    for fc in range(32):
                        bk_h = fc % 2
                        for dc in range(8):
                            te = p.op('tensor', lambda e, dc=dc, fc=fc, bk_h=bk_h: e.matmul(
                                bank[bk_h][:, 0:256],
                                w1sb[:, dc * 4096 + fc * 128: dc * 4096 + (fc + 1) * 128],
                                x1bf[:, dc * 256:(dc + 1) * 256], start=(dc == 0), stop=(dc == 7)), 'TE')
                        p.wait('scalar', 'TE', te)
                        se = p.op('scalar', lambda e, fc=fc, bk_h=bk_h: e.activation(
                            hbf[:, fc * 256:(fc + 1) * 256], bank[bk_h][:, 0:256], AF.Gelu_apprx_tanh,
                            bias=v1[:, b32 + fc:b32 + fc + 1]), 'SE')
                        p.wait('tensor', 'SE', se)
                    for dc in range(8):
                        bk_f = 2 + dc % 2
                        for fc in range(32):
                            te = p.op('tensor', lambda e, fc=fc, dc=dc, bk_f=bk_f: e.matmul(
                                bank[bk_f][:, 0:256],
                                w2sb[:, fc * 1024 + dc * 128: fc * 1024 + (dc + 1) * 128],
                                hbf[:, fc * 256:(fc + 1) * 256], start=(fc == 0), stop=(fc == 31)), 'TE')
                        p.wait('scalar', 'TE', te)
                        se = p.op('scalar', lambda e, dc=dc, bk_f=bk_f: e.activation(
                            scr[0][:, 0:256], bank[bk_f][:, 0:256], AF.Gelu_apprx_tanh,
                            bias=v2[:, b8 + dc:b8 + dc + 1]), 'SE')
                        p.wait('vector', 'SE', se)
                        ve = p.op('vector', lambda e, dc=dc: e.tensor_tensor(
                            t1b[:, dc * 256:(dc + 1) * 256], scr[0][:, 0:256],
                            ufp[:, dc * 256:(dc + 1) * 256], op=ALU.add), 'VE')
                        p.wait('tensor', 'VE', ve)
                        p.wait('scalar', 'VE', ve)
                    ln_feat(lambda dc: t1b[:, dc * 256:(dc + 1) * 256], 256, vl2s, vl2b, b8, LNE_A,
                            lambda dc: ufp[:, dc * 256:(dc + 1) * 256], sqa)
                    p.wait('sync', 'SE', p.cur('SE'))
                    for dc in range(8):
                        v = p.dma('sync', unxt[dc * 128:(dc + 1) * 128, tc * 256:(tc + 1) * 256],
                                  ufp[:, dc * 256:(dc + 1) * 256])
                    p.barrier()
            ucur, unxt = unxt, ucur
            if stage == f"blk{blk}":
                ov = out.rearrange("t d -> (t d)").rearrange("(d t) -> d t", d=D)
                for dc in range(8):
                    p.dma('sync', ov[dc * 128:(dc + 1) * 128, :], ucur[dc * 128:(dc + 1) * 128, :])
                p.wait('sync', 'LD', p.cur('LD'))
                return finish(nc, ges, p)

    with ExitStack() as psm:
        sbm = lambda n, s, d=F32: psm.enter_context(nc.sbuf_tensor(n, s, d))
        xa = sbm("mxa", [128, 1024]); xb = sbm("mxb", [128, 1024])
        ua = sbm("mua", [128, 1024]); tm = sbm("mtm", [128, 2048])
        ustg = sbm("mustg", [128, 2048])
        for tc in range(32):
            v = p.dma('sync', xa[:], xin[tc * 128:(tc + 1) * 128, :])
            for dc in range(8):
                gv = p.dma('gpsimd', ustg[:, dc * 128:(dc + 1) * 128],
                           ucur[dc * 128:(dc + 1) * 128, tc * 128:(tc + 1) * 128])
            p.wait('tensor', 'GD', gv)
            for dc in range(8):
                bk = dc % 2
                te = p.op('tensor', lambda e, dc=dc, bk=bk: e.transpose(
                    bank[bk][:, 0:128], ustg[:, dc * 128:(dc + 1) * 128], ident[:]), 'TE')
                p.wait('scalar', 'TE', te)
                se = p.op('scalar', lambda e, dc=dc, bk=bk: e.copy(
                    xb[:, dc * 128:(dc + 1) * 128], bank[bk][:, 0:128]), 'SE')
                p.wait('tensor', 'SE', se)
            p.wait('vector', 'SE', se)
            p.wait('vector', 'LD', v)
            ve = p.op('vector', lambda e: e.tensor_tensor(xa[:], xa[:], xb[:], op=ALU.add), 'VE')
            p.wait('sync', 'VE', ve)
            v = p.dma('sync', xmid[tc * 128:(tc + 1) * 128, :], xa[:])
            ve = ln_tok(xa, ua, LNE_T, tm)
            p.wait('sync', 'VE', ve)
            v = p.dma('sync', u2tok[tc * 128:(tc + 1) * 128, :], ua[:])
            p.wait('tensor', 'VE', ve)
            for dc in range(8):
                bk = 2 + dc % 2
                te = p.op('tensor', lambda e, dc=dc, bk=bk: e.transpose(
                    bank[bk][:, 0:128], ua[:, dc * 128:(dc + 1) * 128], ident[:]), 'TE')
                p.wait('scalar', 'TE', te)
                se = p.op('scalar', lambda e, dc=dc, bk=bk: e.copy(
                    ustg[:, 1024 + dc * 128: 1024 + (dc + 1) * 128] if False else ustg[:, dc * 128:(dc + 1) * 128],
                    bank[bk][:, 0:128]), 'SE')
                p.wait('tensor', 'SE', se)
            p.wait('sync', 'SE', se)
            for dc in range(8):
                v = p.dma('sync', u2T[dc * 128:(dc + 1) * 128, tc * 128:(tc + 1) * 128],
                          ustg[:, dc * 128:(dc + 1) * 128])
            p.wait('gpsimd', 'LD', v)
            p.barrier(('sync', 'gpsimd', 'tensor', 'scalar', 'vector'))
        p.barrier()

    if stage == "pmid":
        with ExitStack() as pst_:
            xa = pst_.enter_context(nc.sbuf_tensor("dxa", [128, 1024], F32))
            for tc in range(32):
                v = p.dma('sync', xa[:], u2tok[tc * 128:(tc + 1) * 128, :])
                p.wait('sync', 'LD', v)
                v = p.dma('sync', out[tc * 128:(tc + 1) * 128, :], xa[:])
                p.wait('sync', 'LD', v)
        return finish(nc, ges, p)

    with ExitStack() as psf:
        sbf = lambda n, s, d=F32: psf.enter_context(nc.sbuf_tensor(n, s, d))
        qTb = sbf("qTb", [128, 16384], BF16)
        kTb = sbf("kTb", [128, 16384], BF16)
        vtok = sbf("vtok", [128, 16640], BF16)
        l8 = sbf("l8", [128, 8]); e8 = sbf("e8", [128, 8])
        w8a = sbf("w8a", [128, 8]); w8b = sbf("w8b", [128, 8])

        for tt in range(16):
            for h in range(NH):
                ge = p.op('gpsimd', lambda e, tt=tt, h=h: e.memset(
                    vtok[:, tt * 1040 + h * 65 + 64: tt * 1040 + h * 65 + 65], 1.0), 'GE')
        ge_ones = p.cur('GE')

        def transpose_w(dst, src, wstg):
            for rc in range(8):
                gv = p.dma('gpsimd', wstg[:], src[rc * 128:(rc + 1) * 128, :])
                p.wait('tensor', 'GD', gv)
                for cc in range(8):
                    te = p.op('tensor', lambda e, cc=cc: e.transpose(
                        pbf[:, (cc % 4) * 128:(cc % 4 + 1) * 128],
                        wstg[:, cc * 128:(cc + 1) * 128], identb[:]), 'TE')
                    p.wait('scalar', 'TE', te)
                    se = p.op('scalar', lambda e, cc=cc, rc=rc, dst=dst: e.copy(
                        dst[:, cc * 1024 + rc * 128: cc * 1024 + (rc + 1) * 128],
                        pbf[:, (cc % 4) * 128:(cc % 4 + 1) * 128]), 'SE')
                    p.wait('tensor', 'SE', se)
                p.wait('gpsimd', 'SE', se)

        if True:
            for b in range(2):
              with ExitStack() as psf1:
                sbf1 = lambda n, s, d=F32: psf1.enter_context(nc.sbuf_tensor(n, s, d))
                wqT = sbf1(f"wqT{b}", [128, 8192], BF16)
                wkT = sbf1(f"wkT{b}", [128, 8192], BF16)
                wvT = sbf1(f"wvT{b}", [128, 8192], BF16)
                wstg = sbf1(f"wstg{b}", [128, 1024], BF16)
                u2bf = sbf1(f"u2bf{b}", [128, 4096], BF16)
                fbv_row = sbf1(f"fbv_row{b}", [1, 1024])
                p.dma('sync', fbv_row[:], fbv.rearrange("(a b) -> a b", a=1))
                transpose_w(wqT, fWq, wstg)
                transpose_w(wkT, fWk, wstg)
                transpose_w(wvT, fWv, wstg)
                p.barrier()
                for tc in range(4):
                    for dc in range(8):
                        gv = p.dma('gpsimd', u2bf[:, dc * 512:(dc + 1) * 512],
                                   u2T[dc * 128:(dc + 1) * 128, (b * 4 + tc) * 512:(b * 4 + tc + 1) * 512])
                    p.wait('tensor', 'GD', gv)
                    for pi, wT, bias_c, dstT in ((0, wqT, fvq, qTb), (1, wkT, fvk, kTb)):
                        for oc in range(8):
                            bk_x = pi * 2 + oc % 2
                            for dc in range(8):
                                te = p.op('tensor', lambda e, dc=dc, oc=oc, wT=wT, bk_x=bk_x: e.matmul(
                                    bank[bk_x][:], wT[:, dc * 1024 + oc * 128: dc * 1024 + (oc + 1) * 128],
                                    u2bf[:, dc * 512:(dc + 1) * 512], start=(dc == 0), stop=(dc == 7)), 'TE')
                            p.wait('scalar', 'TE', te)
                            se = p.op('scalar', lambda e, oc=oc, tc=tc, bias_c=bias_c, dstT=dstT, bk_x=bk_x:
                                      e.activation(dstT[:, oc * 2048 + tc * 512: oc * 2048 + (tc + 1) * 512],
                                                   bank[bk_x][:], AF.Identity, bias=bias_c[:, oc:oc + 1]), 'SE')
                            p.wait('tensor', 'SE', se)
                    p.wait('scalar', 'GE', ge_ones)
                    for ts_ in range(4):
                        tt = tc * 4 + ts_
                        for half in range(2):
                            bk_x = 4 + half
                            for dc in range(8):
                                te = p.op('tensor', lambda e, dc=dc, ts_=ts_, half=half, bk_x=bk_x: e.matmul(
                                    bank[bk_x][:], u2bf[:, dc * 512 + ts_ * 128: dc * 512 + (ts_ + 1) * 128],
                                    wvT[:, dc * 1024 + half * 512: dc * 1024 + (half + 1) * 512],
                                    start=(dc == 0), stop=False), 'TE')
                            te = p.op('tensor', lambda e, half=half, bk_x=bk_x: e.matmul(
                                bank[bk_x][:], onesrow[0:1, :], fbv_row[0:1, half * 512:(half + 1) * 512],
                                start=False, stop=True), 'TE')
                            p.wait('scalar', 'TE', te)
                            for j in range(8):
                                h = half * 8 + j
                                se = p.op('scalar', lambda e, tt=tt, h=h, j=j, bk_x=bk_x: e.copy(
                                    vtok[:, tt * 1040 + h * 65: tt * 1040 + h * 65 + 64],
                                    bank[bk_x][:, j * 64:(j + 1) * 64]), 'SE')
                            p.wait('tensor', 'SE', se)
                    p.barrier(('gpsimd', 'tensor', 'scalar', 'vector', 'sync'))
              with ExitStack() as psf2:
                sbf2 = lambda n, s, d=F32: psf2.enter_context(nc.sbuf_tensor(n, s, d))
                pbuf = sbf2(f"pbuf{b}", [128, 8192], BF16)
                hstg = sbf2(f"hstg{b}", [128, 4096])
                masks = sbf2(f"masks{b}", [128, 2048], BF16)
                zstg = sbf2(f"zstg{b}", [1, 512]); z16 = sbf2(f"z16{b}", [16, 512])
                r16 = sbf2(f"r16{b}", [16, 512])
                rstage = zstg
                rbc = scr[1]
                v = p.dma('sync', masks[:], c_mask)
                p.wait('vector', 'LD', v)
                for qb in range(4):
                    nreal = 4 * qb + 4
                    for h in range(NH):
                        hb = (64 * h) % 128
                        hc = h // 2
                        for tt in range(nreal):
                            d_off = 512 * qb - 128 * tt
                            bk_s = tt % 2
                            te = p.op('tensor', lambda e, tt=tt, qb=qb, bk_s=bk_s, hb=hb, hc=hc: e.matmul(
                                bank[bk_s][:], kTb[hb:hb + 64, hc * 2048 + tt * 128: hc * 2048 + (tt + 1) * 128],
                                qTb[hb:hb + 64, hc * 2048 + qb * 512: hc * 2048 + (qb + 1) * 512],
                                start=True, stop=True), 'TE')
                            if d_off >= 128:
                                p.wait('scalar', 'TE', te)
                                se = p.op('scalar', lambda e, tt=tt, bk_s=bk_s: e.activation(
                                    pbuf[:, tt * 512:(tt + 1) * 512], bank[bk_s][:], AF.Exp,
                                    scale=1.0 / 32.0), 'SE')
                                p.wait('tensor', 'SE', se)
                            else:
                                c = (-d_off) // 128
                                p.wait('vector', 'TE', te)
                                ve = p.op('vector', lambda e, c=c, bk_s=bk_s: e.tensor_tensor(
                                    scr[0][:], bank[bk_s][:], masks[:, c * 512:(c + 1) * 512],
                                    op=ALU.mult), 'VE')
                                p.wait('scalar', 'VE', ve)
                                se = p.op('scalar', lambda e, tt=tt: e.activation(
                                    pbuf[:, tt * 512:(tt + 1) * 512], scr[0][:], AF.Exp,
                                    scale=1.0 / 32.0), 'SE')
                                p.wait('tensor', 'SE', se)
                                p.wait('vector', 'SE', se)
                        for tt in range(16):
                            if tt < nreal:
                                rhsf = (lambda e, tt=tt: pbuf[:, tt * 512:(tt + 1) * 512])
                            else:
                                rhsf = (lambda e: allonesb[:])
                            te = p.op('tensor', lambda e, tt=tt, h=h, rhsf=rhsf: e.matmul(
                                bank[2][0:65, :], vtok[:, tt * 1040 + h * 65: tt * 1040 + h * 65 + 65],
                                rhsf(e), start=(tt == 0), stop=(tt == 15)), 'TE')
                        p.wait('scalar', 'TE', te)
                        se = p.op('scalar', lambda e: e.copy(zstg[:], bank[2][64:65, :]), 'SE')
                        se2 = p.op('scalar', lambda e, hb=hb, hc=hc: e.copy(
                            hstg[hb:hb + 64, hc * 512:(hc + 1) * 512], bank[2][0:64, :]), 'SE')
                        p.wait('tensor', 'SE', se2)
                        p.wait('gpsimd', 'SE', se)
                        gv = p.dma('gpsimd', z16[h:h + 1, :], zstg[:])
                        p.wait('scalar', 'GD', gv)
                    p.wait('vector', 'GD', p.cur('GD'))
                    ve = p.op('vector', lambda e: e.reciprocal(r16[:], z16[:]), 'VE')
                    p.wait('gpsimd', 'VE', ve)
                    for h in range(NH):
                        hb = (64 * h) % 128
                        hc = h // 2
                        gv = p.dma('gpsimd', rstage[:], r16[h:h + 1, :])
                        p.wait('tensor', 'GD', gv)
                        te = p.op('tensor', lambda e, hb=hb: e.matmul(
                            bank[3][hb:hb + 64, :], onesrow[0:1, 0:64],
                            rstage[0:1, :], start=True, stop=True), 'TE')
                        p.wait('scalar', 'TE', te)
                        se = p.op('scalar', lambda e, hb=hb: e.copy(rbc[hb:hb + 64, :],
                                                                    bank[3][hb:hb + 64, :]), 'SE')
                        p.wait('vector', 'SE', se)
                        ve = p.op('vector', lambda e, hb=hb, hc=hc: e.tensor_tensor(
                            hstg[hb:hb + 64, hc * 512:(hc + 1) * 512],
                            hstg[hb:hb + 64, hc * 512:(hc + 1) * 512], rbc[hb:hb + 64, :],
                            op=ALU.mult), 'VE')
                        p.wait('scalar', 'VE', ve)
                        p.wait('tensor', 'VE', ve)
                        p.wait('gpsimd', 'VE', ve)
                    p.wait('gpsimd', 'VE', p.cur('VE'))
                    for dc in range(8):
                        gv = p.dma('gpsimd', hidT[dc * 128:(dc + 1) * 128,
                                                  (b * 4 + qb) * 512:(b * 4 + qb + 1) * 512],
                                   hstg[:, dc * 512:(dc + 1) * 512])
                    for ts_ in range(4):
                        gtc = b * 16 + qb * 4 + ts_
                        for dc in range(8):
                            te = p.op('tensor', lambda e, dc=dc, ts_=ts_: e.matmul(
                                bank[4][:, 0:8], hstg[:, dc * 512 + ts_ * 128: dc * 512 + (ts_ + 1) * 128],
                                wswT[:, dc * 8:(dc + 1) * 8], start=(dc == 0), stop=False), 'TE')
                        te = p.op('tensor', lambda e: e.matmul(bank[4][:, 0:8], onesrow[0:1, :],
                                                               fbsw_row[0:1, :], start=False, stop=True), 'TE')
                        p.wait('vector', 'TE', te)
                        ve = p.op('vector', lambda e: e.tensor_copy(l8[:], bank[4][:, 0:8]), 'VE')
                        ve = p.op('vector', lambda e: e.reduce_max(r128a[:], l8[:], axis=AX, negate=True), 'VE')
                        p.wait('scalar', 'VE', ve)
                        se = p.op('scalar', lambda e: e.activation(e8[:], l8[:], AF.Exp, bias=r128a[:]), 'SE')
                        p.wait('vector', 'SE', se)
                        ve = p.op('vector', lambda e: e.reduce_sum(r128b[:], e8[:], axis=AX), 'VE')
                        ve = p.op('vector', lambda e: e.reciprocal(r128c[:], r128b[:]), 'VE')
                        ve = p.op('vector', lambda e, gtc=gtc: e.tensor_copy(rp_all[:, gtc:gtc + 1],
                                                                             r128c[:]), 'VE')
                        ve = p.op('vector', lambda e: e.tensor_scalar(r128d[:], r128a[:], -1.0, None,
                                                                      op0=ALU.mult), 'VE')
                        ve = p.op('vector', lambda e: e.tensor_scalar(w8a[:], l8[:], r128d[:], None,
                                                                      op0=ALU.is_ge), 'VE')
                        ve = p.op('vector', lambda e: e.tensor_tensor(w8b[:], w8a[:], rm8[:], op=ALU.mult), 'VE')
                        ve = p.op('vector', lambda e: e.tensor_scalar(w8b[:], w8b[:], -1.0, 8.0,
                                                                      op0=ALU.mult, op1=ALU.add), 'VE')
                        ve = p.op('vector', lambda e: e.tensor_reduce(r128d[:], w8b[:], axis=AX,
                                                                      op=ALU.min), 'VE')
                        ve = p.op('vector', lambda e, gtc=gtc: e.tensor_scalar(
                            oh_all[:, gtc * 8:(gtc + 1) * 8], io8[:], r128d[:], None,
                            op0=ALU.is_equal), 'VE')
                        p.wait('tensor', 'VE', ve)
                        te = p.op('tensor', lambda e, gtc=gtc: e.matmul(
                            bank[5][:, 0:8], sltri[:], oh_all[:, gtc * 8:(gtc + 1) * 8],
                            start=True, stop=False), 'TE')
                        te = p.op('tensor', lambda e: e.matmul(bank[5][:, 0:8], onesrow[0:1, :],
                                                               carry[0:1, :], start=False, stop=True), 'TE')
                        p.wait('vector', 'TE', te)
                        ve = p.op('vector', lambda e, gtc=gtc: e.tensor_copy(
                            rank_all[:, gtc * 8:(gtc + 1) * 8], bank[5][:, 0:8]), 'VE')
                        p.wait('tensor', 'VE', ve)
                        te = p.op('tensor', lambda e, gtc=gtc: e.matmul(
                            bank[4][0:1, 0:8], e127[:], rank_all[:, gtc * 8:(gtc + 1) * 8],
                            start=True, stop=False), 'TE')
                        te = p.op('tensor', lambda e, gtc=gtc: e.matmul(
                            bank[4][0:1, 0:8], e127[:], oh_all[:, gtc * 8:(gtc + 1) * 8],
                            start=False, stop=True), 'TE')
                        p.wait('vector', 'TE', te)
                        ve = p.op('vector', lambda e: e.tensor_copy(carry[0:1, :],
                                                                    bank[4][0:1, 0:8]), 'VE')
                        p.wait('tensor', 'VE', ve)
                    p.barrier(('gpsimd', 'tensor', 'scalar', 'vector', 'sync'))
        p.barrier()

    v = p.dma('sync', agin.ap(), carry[0:1, :])
    p.wait('gpsimd', 'LD', v)
    cc = p.op('gpsimd', lambda e: e.collective_compute(
        "AllGather", ALU.bypass, replica_groups=[list(range(NCORES))],
        ins=[agin.ap().opt()], outs=[agout.ap().opt()]), 'CC')
    p.wait('gpsimd', 'CC', 1)
    gv = p.dma('gpsimd', cnt8[:], agout.ap())
    p.wait('tensor', 'GD', gv)
    te = p.op('tensor', lambda e: e.matmul(bank[5][0:8, 0:1], cnt8[:], rmask[:],
                                           start=True, stop=True), 'TE')
    p.wait('scalar', 'TE', te)
    se = p.op('scalar', lambda e: e.copy(offcol[:], bank[5][0:8, 0:1]), 'SE')
    p.wait('gpsimd', 'SE', se)
    gv = p.dma('gpsimd', offs_row[:], offcol[:])
    p.barrier()

    with ExitStack() as ps3:
        sb3 = lambda n, s, d=F32: ps3.enter_context(nc.sbuf_tensor(n, s, d))
        weT = [sb3(f"weT{i}", [128, 16384], BF16) for i in range(4)]
        wstg3 = sb3("wstg3", [128, 1024], BF16)
        hbf_t = sb3("hbf_t", [128, 1024], BF16)
        htok = sb3("htok", [128, 1024], BF16)
        yacc = sb3("yacc", [128, 1024])
        u2t_t = sb3("u2t_t", [128, 1024]); xm_t = sb3("xm_t", [128, 1024])
        ua3 = sb3("ua3", [128, 1024]); tm3 = sb3("tm3", [128, 2048])
        gam_bc = sb3("gam_bc", [128, 1024]); bet_bc = sb3("bet_bc", [128, 1024])
        fbe_sb = sb3("fbe_sb", [8, 1024])
        ct8 = sb3("ct8", [8, 128]); keptf = sb3("keptf", [128, 8]); ctile = sb3("ctile", [128, 8])

        p.dma('sync', fbe_sb[:], fbe)
        v = p.dma('sync', strow[0:1, 0:1024], gam.rearrange("(a b) -> a b", a=1))
        p.wait('tensor', 'LD', v)
        te = p.op('tensor', lambda e: e.matmul(bank[0][:], onesrow[0:1, :], strow[0:1, 0:512],
                                               start=True, stop=True), 'TE')
        te = p.op('tensor', lambda e: e.matmul(bank[1][:], onesrow[0:1, :], strow[0:1, 512:1024],
                                               start=True, stop=True), 'TE')
        p.wait('scalar', 'TE', te)
        p.op('scalar', lambda e: e.copy(gam_bc[:, 0:512], bank[0][:]), 'SE')
        se = p.op('scalar', lambda e: e.copy(gam_bc[:, 512:1024], bank[1][:]), 'SE')
        p.wait('sync', 'SE', se)
        v = p.dma('sync', strow[0:1, 0:1024], bet.rearrange("(a b) -> a b", a=1))
        p.wait('tensor', 'LD', v)
        p.wait('tensor', 'SE', se)
        te = p.op('tensor', lambda e: e.matmul(bank[0][:], onesrow[0:1, :], strow[0:1, 0:512],
                                               start=True, stop=True), 'TE')
        te = p.op('tensor', lambda e: e.matmul(bank[1][:], onesrow[0:1, :], strow[0:1, 512:1024],
                                               start=True, stop=True), 'TE')
        p.wait('scalar', 'TE', te)
        p.op('scalar', lambda e: e.copy(bet_bc[:, 0:512], bank[0][:]), 'SE')
        se = p.op('scalar', lambda e: e.copy(bet_bc[:, 512:1024], bank[1][:]), 'SE')
        p.wait('tensor', 'SE', se)
        for e_i in range(NE):
            dst = weT[e_i // 2]
            off = (e_i % 2) * 8192
            for rc in range(8):
                gv = p.dma('gpsimd', wstg3[:], fWe[e_i, rc * 128:(rc + 1) * 128, :])
                p.wait('tensor', 'GD', gv)
                for cc_ in range(8):
                    te = p.op('tensor', lambda e, cc_=cc_: e.transpose(
                        pbf[:, (cc_ % 4) * 128:(cc_ % 4 + 1) * 128],
                        wstg3[:, cc_ * 128:(cc_ + 1) * 128], identb[:]), 'TE')
                    p.wait('scalar', 'TE', te)
                    se = p.op('scalar', lambda e, cc_=cc_, rc=rc, dst=dst, off=off: e.copy(
                        dst[:, off + cc_ * 1024 + rc * 128: off + cc_ * 1024 + (rc + 1) * 128],
                        pbf[:, (cc_ % 4) * 128:(cc_ % 4 + 1) * 128]), 'SE')
                    p.wait('tensor', 'SE', se)
                p.wait('gpsimd', 'SE', se)
        p.barrier()

        for tc in range(32):
            for dc in range(8):
                gv = p.dma('gpsimd', hbf_t[:, dc * 128:(dc + 1) * 128],
                           hidT[dc * 128:(dc + 1) * 128, tc * 128:(tc + 1) * 128])
            v = p.dma('sync', u2t_t[:], u2tok[tc * 128:(tc + 1) * 128, :])
            v = p.dma('sync', xm_t[:], xmid[tc * 128:(tc + 1) * 128, :])
            ld_in = v
            p.wait('tensor', 'GD', gv)
            for dc in range(8):
                te = p.op('tensor', lambda e, dc=dc: e.transpose(
                    pbf[:, (dc % 4) * 128:(dc % 4 + 1) * 128],
                    hbf_t[:, dc * 128:(dc + 1) * 128], identb[:]), 'TE')
                p.wait('scalar', 'TE', te)
                se = p.op('scalar', lambda e, dc=dc: e.copy(
                    htok[:, dc * 128:(dc + 1) * 128], pbf[:, (dc % 4) * 128:(dc % 4 + 1) * 128]), 'SE')
                p.wait('tensor', 'SE', se)
            te = p.op('tensor', lambda e: e.matmul(bank[6][:, 0:8], onesrow[0:1, :], offs_row[0:1, :],
                                                   start=True, stop=True), 'TE')
            p.wait('vector', 'TE', te)
            p.wait('vector', 'SE', p.cur('SE'))
            ve = p.op('vector', lambda e, tc=tc: e.tensor_tensor(
                keptf[:], rank_all[:, tc * 8:(tc + 1) * 8], bank[6][:, 0:8], op=ALU.add), 'VE')
            ve = p.op('vector', lambda e: e.tensor_scalar(keptf[:], keptf[:], float(CAP), None,
                                                          op0=ALU.is_lt), 'VE')
            ve = p.op('vector', lambda e, tc=tc: e.tensor_tensor(
                ctile[:], keptf[:], oh_all[:, tc * 8:(tc + 1) * 8], op=ALU.mult), 'VE')
            ve = p.op('vector', lambda e: e.reduce_sum(r128a[:], ctile[:], axis=AX), 'VE')
            ve = p.op('vector', lambda e, tc=tc: e.tensor_scalar(
                ctile[:], ctile[:], rp_all[:, tc:tc + 1], None, op0=ALU.mult), 'VE')
            ve = p.op('vector', lambda e: e.tensor_scalar(r128a[:], r128a[:], -1.0, 1.0,
                                                          op0=ALU.mult, op1=ALU.add), 'VE')
            ve = p.op('vector', lambda e, tc=tc: e.tensor_tensor(
                r128b[:], r128a[:], rp_all[:, tc:tc + 1], op=ALU.mult), 'VE')
            p.wait('tensor', 'VE', ve)
            te = p.op('tensor', lambda e: e.transpose(bank[6][0:8, 0:128], ctile[:], ident[:]), 'TE')
            p.wait('scalar', 'TE', te)
            se = p.op('scalar', lambda e: e.copy(ct8[:], bank[6][0:8, 0:128]), 'SE')
            p.wait('vector', 'SE', se)
            ve = p.op('vector', lambda e: e.tensor_scalar(yacc[:], htok[:], r128b[:], None,
                                                          op0=ALU.mult), 'VE')
            p.wait('tensor', 'VE', ve)
            te = p.op('tensor', lambda e: e.matmul(bank[5][:], ct8[:], fbe_sb[:, 0:512],
                                                   start=True, stop=True), 'TE')
            te = p.op('tensor', lambda e: e.matmul(bank[6][:, 0:512], ct8[:], fbe_sb[:, 512:1024],
                                                   start=True, stop=True), 'TE')
            p.wait('vector', 'TE', te)
            ve = p.op('vector', lambda e: e.tensor_tensor(yacc[:, 0:512], yacc[:, 0:512],
                                                          bank[5][:], op=ALU.add), 'VE')
            ve = p.op('vector', lambda e: e.tensor_tensor(yacc[:, 512:1024], yacc[:, 512:1024],
                                                          bank[6][:, 0:512], op=ALU.add), 'VE')
            p.wait('tensor', 'VE', ve)
            for e_i in range(NE):
                wsrc = weT[e_i // 2]
                off = (e_i % 2) * 8192
                bk0 = (e_i % 2) * 2
                for half in range(2):
                    for dc in range(8):
                        te = p.op('tensor', lambda e, dc=dc, half=half, off=off, wsrc=wsrc, bk0=bk0: e.matmul(
                            bank[bk0 + half][:], hbf_t[:, dc * 128:(dc + 1) * 128],
                            wsrc[:, off + dc * 1024 + half * 512: off + dc * 1024 + (half + 1) * 512],
                            start=(dc == 0), stop=(dc == 7)), 'TE')
                    p.wait('vector', 'TE', te)
                    ve = p.op('vector', lambda e, e_i=e_i, half=half, bk0=bk0: e.scalar_tensor_tensor(
                        yacc[:, half * 512:(half + 1) * 512], bank[bk0 + half][:],
                        ctile[:, e_i:e_i + 1], yacc[:, half * 512:(half + 1) * 512],
                        op0=ALU.mult, op1=ALU.add), 'VE')
                    p.wait('tensor', 'VE', ve)
            p.wait('vector', 'LD', ld_in)
            ve = p.op('vector', lambda e: e.tensor_tensor(yacc[:], yacc[:], u2t_t[:], op=ALU.add), 'VE')
            ve = ln_tok(yacc, ua3, LNE_T, tm3, gamma=gam_bc, beta=bet_bc, resid=xm_t)
            p.wait('sync', 'VE', ve)
            v = p.dma('sync', out[tc * 128:(tc + 1) * 128, :], ua3[:])
            p.barrier(('sync', 'gpsimd', 'tensor', 'scalar', 'vector'))
    p.wait('sync', 'LD', p.cur('LD'))
    return finish(nc, ges, p)


def finish(nc, ges, p):
    def replay(e, which):
        for eng, name, a, k, semh, n in p.ops:
            if eng != which:
                continue
            r = getattr(e, name)(*a, **k)
            if semh is not None:
                r.then_inc(semh, n)

    block = ges.enter_context(nc.Block())

    @block.sync
    def _(e):
        replay(e, 'sync')

    @block.gpsimd
    def _(e):
        replay(e, 'gpsimd')

    @block.scalar
    def _(e):
        replay(e, 'scalar')

    @block.vector
    def _(e):
        replay(e, 'vector')

    @block.tensor
    def _(e):
        replay(e, 'tensor')

    ges.close()
    return nc


def _consts():
    ident = np.eye(128, dtype=np.float32)
    sl = np.triu(np.ones((128, 128), np.float32), 1)
    masks = np.zeros((128, 4, 512), np.float32)
    tp = np.arange(128)[:, None]
    qf = np.arange(512)[None, :]
    for c in range(4):
        masks[:, c, :] = (tp <= qf - 128 * c).astype(np.float32)
    io8 = np.broadcast_to(np.arange(8, dtype=np.float32), (128, 8)).copy()
    e127 = np.zeros((128, 1), np.float32); e127[127, 0] = 1.0
    return (ident, sl, masks.reshape(128, 2048).astype(ml_dtypes.bfloat16), io8,
            (8.0 - io8).astype(np.float32), e127)


_CACHE = {}


def kernel(L0_attn_x, L0_attn_Wq, L0_attn_bq, L0_attn_Wk, L0_attn_bk, L0_attn_Wv,
           L0_attn_bv, L0_attn_Wo, L0_attn_bo, L0_attn_ln1_s, L0_attn_ln1_b,
           L0_attn_W1, L0_attn_b1, L0_attn_W2, L0_attn_b2, L0_attn_ln2_s,
           L0_attn_ln2_b, L0_ffn_Wq, L0_ffn_bq, L0_ffn_Wk, L0_ffn_bk, L0_ffn_Wv,
           L0_ffn_bv, L0_ffn_We, L0_ffn_be, L0_ffn_Wsw, L0_ffn_bsw, L0_ffn_gamma,
           L0_ffn_beta, _stage=None):
    x = np.ascontiguousarray(np.asarray(L0_attn_x, dtype=np.float32))
    ident, sl, masks, io8, rm8, e127c = _consts()
    f32 = lambda a: np.ascontiguousarray(np.asarray(a, dtype=np.float32))
    common = dict(
        aWq=f32(L0_attn_Wq), aWk=f32(L0_attn_Wk), aWv=f32(L0_attn_Wv), aWo=f32(L0_attn_Wo),
        aW1=f32(L0_attn_W1), aW2=f32(L0_attn_W2),
        abq=f32(L0_attn_bq), abk=f32(L0_attn_bk), abv=f32(L0_attn_bv), abo=f32(L0_attn_bo),
        ab1=f32(L0_attn_b1), ab2=f32(L0_attn_b2),
        aln1s=f32(L0_attn_ln1_s), aln1b=f32(L0_attn_ln1_b),
        aln2s=f32(L0_attn_ln2_s), aln2b=f32(L0_attn_ln2_b),
        fWq=f32(L0_ffn_Wq).reshape(1024, D), fWk=f32(L0_ffn_Wk).reshape(1024, D),
        fWv=f32(L0_ffn_Wv).reshape(1024, D),
        fbq=f32(L0_ffn_bq).reshape(-1), fbk=f32(L0_ffn_bk).reshape(-1),
        fbv=f32(L0_ffn_bv).reshape(-1),
        fWe=f32(L0_ffn_We), fbe=f32(L0_ffn_be), fWsw=f32(L0_ffn_Wsw),
        fbsw=f32(L0_ffn_bsw), gam=f32(L0_ffn_gamma), bet=f32(L0_ffn_beta),
        c_id=ident, c_sl=sl, c_mask=masks, c_io8=io8, c_rm8=rm8, c_e127=e127c,
    )
    if _stage not in _CACHE:
        _CACHE[_stage] = build_program(stage=_stage)
    nc = _CACHE[_stage]
    in_maps = []
    for c in range(NCORES):
        m = dict(common)
        m["xin"] = np.ascontiguousarray(x[2 * c:2 * c + 2].reshape(T, D))
        rmk = np.zeros((8, 1), np.float32)
        rmk[:c, 0] = 1.0
        m["c_rmask"] = rmk
        in_maps.append(m)
    res = run_bass_kernel_spmd(nc, in_maps, core_ids=list(range(NCORES)))
    return np.concatenate([np.asarray(r["out"]).reshape(2, S, D) for r in res.results], axis=0)


# version 05
# speedup vs baseline: 1.4382x; 1.4382x over previous
import numpy as np
import ml_dtypes
from contextlib import ExitStack
import concourse.bass as bass
import concourse.mybir as mybir
from concourse.bass_utils import run_bass_kernel_spmd
from concourse.alu_op_type import AluOpType as ALU

F32 = mybir.dt.float32
BF16 = mybir.dt.bfloat16
AF = mybir.ActivationFunctionType
AX = mybir.AxisListType.X

NCORES = 8
T = 4096
D = 1024
FF = 4096
NBLK = 4
S = 2048
NH = 16
NE = 8
CAP = 4915
LNE_A = 1e-6
LNE_T = 1e-5
ENGS = ('sync', 'gpsimd', 'scalar', 'vector', 'tensor')


class _Rec:
    def __getattr__(self, name):
        def call(*a, **k):
            return (name, a, k)
        return call


_REC = _Rec()


class P:
    def __init__(self, nc, es):
        self.nc = nc
        self.es = es
        self.ops = []
        self.ctr = {}
        self.sems = {}
        self.dma_sems = ['LD', 'GD']
        for s in ('LD', 'GD', 'TE', 'VE', 'SE', 'GE', 'BAR', 'CC'):
            self.sems[s] = es.enter_context(nc.semaphore(s))
            self.ctr[s] = 0

    def dmasem(self, name):
        if name not in self.sems:
            self.sems[name] = self.es.enter_context(self.nc.semaphore(name))
            self.ctr[name] = 0
            self.dma_sems.append(name)
        return name

    def emit(self, eng, fn):
        name, a, k = fn(_REC)
        self.ops.append((eng, name, a, k, None, 0))

    def cur(self, sem):
        return self.ctr[sem]

    def wait(self, eng, sem, val):
        h = self.sems[sem]
        self.ops.append((eng, 'wait_ge', (h, val), {}, None, 0))

    def op(self, eng, fn, sem=None, n=1):
        name, a, k = fn(_REC)
        if eng in ('vector', 'scalar', 'gpsimd') and sem in ('VE', 'SE', 'GE'):
            prev = getattr(self, '_last_' + eng, None)
            if prev is not None:
                self.ops.append((eng, 'wait_ge', (self.sems[prev[0]], prev[1]), {}, None, 0))
        if sem is None:
            self.ops.append((eng, name, a, k, None, 0))
            return None
        self.ctr[sem] += n
        self.ops.append((eng, name, a, k, self.sems[sem], n))
        if eng in ('vector', 'scalar', 'gpsimd') and sem in ('VE', 'SE', 'GE'):
            setattr(self, '_last_' + eng, (sem, self.ctr[sem]))
        return self.ctr[sem]

    def dma(self, eng, out, in_, sem=None, **kw):
        if sem is None:
            sem = {'sync': 'LD', 'gpsimd': 'GD', 'scalar': 'LD'}[eng]
        else:
            sem = self.dmasem(sem)
        return self.op(eng, lambda e: e.dma_start(out=out, in_=in_, **kw), sem, 16)

    def use_bank(self, bi):
        w = getattr(self, 'bankuse', {}).get(bi)
        if w is not None:
            self.wait('tensor', w[0], w[1])

    def set_bank(self, bi, sem):
        if not hasattr(self, 'bankuse'):
            self.bankuse = {}
        self.bankuse[bi] = (sem, self.ctr[sem])

    def barrier(self, engines=ENGS):
        if 'tensor' in engines:
            self.wait('sync', 'TE', self.ctr['TE'])
        if 'vector' in engines:
            self.wait('sync', 'VE', self.ctr['VE'])
        if 'scalar' in engines:
            self.wait('sync', 'SE', self.ctr['SE'])
        if 'gpsimd' in engines:
            self.wait('sync', 'GE', self.ctr['GE'])
        for s in self.dma_sems:
            self.wait('sync', s, self.ctr[s])
        self.ctr['BAR'] += 1
        self.ops.append(('sync', 'nop', (), {}, self.sems['BAR'], 1))
        tgt = self.ctr['BAR']
        for eng in engines:
            if eng != 'sync':
                self.wait(eng, 'BAR', tgt)


def build_program(stage=None):
    nc = bass.Bass("TRN2", target_bir_lowering=False)
    ges = ExitStack()

    def din(name, shape, dt=F32):
        return nc.dram_tensor(name, shape, dt, kind="ExternalInput").ap()
    xin = din("xin", [T, D])
    aWq = din("aWq", [NBLK, D, D]); aWk = din("aWk", [NBLK, D, D])
    aWv = din("aWv", [NBLK, D, D]); aWo = din("aWo", [NBLK, D, D])
    aW1 = din("aW1", [NBLK, D, FF]); aW2 = din("aW2", [NBLK, FF, D])
    abq = din("abq", [NBLK, D]); abk = din("abk", [NBLK, D]); abv = din("abv", [NBLK, D])
    abo = din("abo", [NBLK, D]); ab1 = din("ab1", [NBLK, FF]); ab2 = din("ab2", [NBLK, D])
    aln1s = din("aln1s", [NBLK, D]); aln1b = din("aln1b", [NBLK, D])
    aln2s = din("aln2s", [NBLK, D]); aln2b = din("aln2b", [NBLK, D])
    fWq = din("fWq", [1024, D]); fWk = din("fWk", [1024, D]); fWv = din("fWv", [1024, D])
    fbq = din("fbq", [1024]); fbk = din("fbk", [1024]); fbv = din("fbv", [1024])
    fWe = din("fWe", [NE, D, D]); fbe = din("fbe", [NE, D])
    fWsw = din("fWsw", [NE, D]); fbsw = din("fbsw", [NE])
    gam = din("gam", [D]); bet = din("bet", [D])
    c_id = din("c_id", [128, 128]); c_sl = din("c_sl", [128, 128])
    c_mask = din("c_mask", [128, 4 * 512], BF16); c_io8 = din("c_io8", [128, 8])
    c_rm8 = din("c_rm8", [128, 8]); c_rmask = din("c_rmask", [8, 1])
    c_e127 = din("c_e127", [128, 1])
    out = nc.dram_tensor("out", [T, D], F32, kind="ExternalOutput").ap()

    uA = nc.dram_tensor("uA", [D, T], F32).ap()
    uB = nc.dram_tensor("uB", [D, T], F32).ap()
    x1T = nc.dram_tensor("x1T", [D, T], F32).ap()
    xmid = nc.dram_tensor("xmid", [T, D], F32).ap()
    u2tok = nc.dram_tensor("u2tok", [T, D], F32).ap()
    u2T = nc.dram_tensor("u2T", [D, T], F32).ap()
    hidT = nc.dram_tensor("hidT", [D, T], BF16).ap()
    agin = nc.dram_tensor("agin", [1, 8], F32)
    agout = nc.dram_tensor("agout", [8, 8], F32)

    p = P(nc, ges)
    for _s in ('ubA', 'ubB', 'ufA', 'ufB', 'stA', 'stB', 'xbA', 'xbB', 'ufl', 'ufs',
               'p0x', 'p0s', 'pmx', 'pmg', 'pms', 'fw', 'fu', 'fz', 'fr', 'fh',
               'f3h', 'f3u', 'fo'):
        p.dmasem(_s)
    gsb = lambda n, s, d=F32: ges.enter_context(nc.sbuf_tensor(n, s, d))

    ident = gsb("ident", [128, 128])
    identb = gsb("identb", [128, 128], BF16)
    sltri = gsb("sltri", [128, 128])
    allonesb = gsb("allonesb", [128, 512], BF16)
    io8 = gsb("io8", [128, 8]); rm8 = gsb("rm8", [128, 8]); rmask = gsb("rmask", [8, 1])
    ones128 = gsb("ones128", [128, 1]); onesrow = gsb("onesrow", [1, 128])
    e127 = gsb("e127", [128, 1])
    vq = gsb("vq", [128, 32]); vk = gsb("vk", [128, 32]); vv = gsb("vv", [128, 32])
    vo = gsb("vo", [128, 32]); v1 = gsb("v1", [128, 128]); v2 = gsb("v2", [128, 32])
    vnq = gsb("vnq", [128, 32]); vnk = gsb("vnk", [128, 32])
    vl1s = gsb("vl1s", [128, 32]); vl1b = gsb("vl1b", [128, 32])
    vl2s = gsb("vl2s", [128, 32]); vl2b = gsb("vl2b", [128, 32])
    fvq = gsb("fvq", [128, 8]); fvk = gsb("fvk", [128, 8])
    fbsw_row = gsb("fbsw_row", [1, 8]); wswT = gsb("wswT", [128, 64])
    scr = [gsb(f"scr{i}", [128, 512]) for i in range(3)]
    abrow = gsb("abrow", [1, 1024]); strow = gsb("strow", [1, 1024])
    r128a = gsb("r128a", [128, 1]); r128b = gsb("r128b", [128, 1])
    r128c = gsb("r128c", [128, 1]); r128d = gsb("r128d", [128, 1])
    kvacc = gsb("kvacc", [128, 16]); ksacc = gsb("ksacc", [128, 16])
    oh_all = gsb("oh_all", [128, 256]); rank_all = gsb("rank_all", [128, 256])
    rp_all = gsb("rp_all", [128, 32])
    carry = gsb("carry", [1, 8]); offs_row = gsb("offs_row", [1, 8])
    cnt8 = gsb("cnt8", [8, 8]); offcol = gsb("offcol", [8, 1])

    bank = [ges.enter_context(nc.psum_tensor(f"bank{i}", [128, 512], F32)) for i in range(7)]
    pbf = ges.enter_context(nc.psum_tensor("pbf", [128, 1024], BF16))

    p.dma('sync', ident[:], c_id)
    p.dma('sync', sltri[:], c_sl)
    p.dma('sync', io8[:], c_io8)
    p.dma('sync', rm8[:], c_rm8)
    p.dma('sync', rmask[:], c_rmask)
    p.dma('sync', e127[:], c_e127)
    p.dma('sync', fbsw_row[:], fbsw.rearrange("(a b) -> a b", a=1))
    v = p.dma('gpsimd', identb[:], c_id)
    p.op('gpsimd', lambda e: e.memset(ones128[:], 1.0), 'GE')
    p.op('gpsimd', lambda e: e.memset(onesrow[:], 1.0), 'GE')
    p.op('gpsimd', lambda e: e.memset(carry[:], 0.0), 'GE')
    p.op('gpsimd', lambda e: e.memset(allonesb[:], 1.0), 'GE')

    def load_vec(dst, col0, row, n):
        v = p.dma('sync', scr[2][0:n, 0:128], row.rearrange("(a b) -> a b", b=128))
        p.wait('tensor', 'LD', v)
        te = p.op('tensor', lambda e, n=n: e.transpose(bank[6][:, 0:n], scr[2][0:n, 0:128], ident[0:n, 0:n]), 'TE')
        p.wait('scalar', 'TE', te)
        se = p.op('scalar', lambda e, dst=dst, col0=col0, n=n: e.copy(dst[:, col0:col0 + n],
                                                                      bank[6][:, 0:n]), 'SE')
        p.wait('tensor', 'SE', se)
        p.wait('sync', 'SE', se)

    for i in range(NBLK):
        load_vec(vq, i * 8, abq[i, :], 8)
        load_vec(vk, i * 8, abk[i, :], 8)
        load_vec(vv, i * 8, abv[i, :], 8)
        load_vec(vo, i * 8, abo[i, :], 8)
        load_vec(v1, i * 32, ab1[i, :], 32)
        load_vec(v2, i * 8, ab2[i, :], 8)
        load_vec(vl1s, i * 8, aln1s[i, :], 8)
        load_vec(vl1b, i * 8, aln1b[i, :], 8)
        load_vec(vl2s, i * 8, aln2s[i, :], 8)
        load_vec(vl2b, i * 8, aln2b[i, :], 8)
    load_vec(fvq, 0, fbq, 8)
    load_vec(fvk, 0, fbk, 8)
    v = p.dma('sync', scr[2][0:8, 0:512], fWsw[:, 0:512])
    v = p.dma('sync', scr[1][0:8, 0:512], fWsw[:, 512:1024])
    p.wait('tensor', 'LD', v)
    for dc in range(8):
        src = scr[2] if dc < 4 else scr[1]
        cof = dc * 128 if dc < 4 else (dc - 4) * 128
        te = p.op('tensor', lambda e, dc=dc, src=src, cof=cof: e.transpose(
            bank[6][:, dc * 8:(dc + 1) * 8], src[0:8, cof:cof + 128], ident[0:8, 0:8]), 'TE')
    p.wait('scalar', 'TE', p.cur('TE'))
    se = p.op('scalar', lambda e: e.copy(wswT[:], bank[6][:, 0:64]), 'SE')
    p.wait('tensor', 'SE', se)
    p.wait('vector', 'SE', p.cur('SE'))
    ve = p.op('vector', lambda e: e.tensor_scalar(vnq[:], vq[:], -1.0, None, op0=ALU.mult), 'VE')
    ve = p.op('vector', lambda e: e.tensor_scalar(vnk[:], vk[:], -1.0, None, op0=ALU.mult), 'VE')
    p.barrier()

    def ln_tok(xt, ot, eps, tmbuf, gamma=None, beta=None, resid=None):
        ve = p.op('vector', lambda e: e.reduce_sum(r128a[:], xt[:], axis=AX), 'VE')
        ve = p.op('vector', lambda e: e.tensor_scalar(r128a[:], r128a[:], 1.0 / D, None, op0=ALU.mult), 'VE')
        ve = p.op('vector', lambda e: e.tensor_scalar(tmbuf[:, 0:1024], xt[:], r128a[:], None,
                                                      op0=ALU.subtract), 'VE')
        p.wait('scalar', 'VE', ve)
        se = p.op('scalar', lambda e: e.activation(tmbuf[:, 1024:2048], tmbuf[:, 0:1024], AF.Square,
                                                   accum_out=r128b[:]), 'SE')
        p.wait('vector', 'SE', se)
        ve = p.op('vector', lambda e, eps=eps: e.tensor_scalar(r128b[:], r128b[:], 1.0 / D, eps,
                                                               op0=ALU.mult, op1=ALU.add), 'VE')
        p.wait('scalar', 'VE', ve)
        se = p.op('scalar', lambda e: e.activation(r128c[:], r128b[:], AF.Ln), 'SE')
        se = p.op('scalar', lambda e: e.activation(r128d[:], r128c[:], AF.Exp, scale=-0.5), 'SE')
        p.wait('vector', 'SE', se)
        if gamma is None:
            ve = p.op('vector', lambda e: e.tensor_scalar(ot[:], tmbuf[:, 0:1024], r128d[:], None,
                                                          op0=ALU.mult), 'VE')
        else:
            ve = p.op('vector', lambda e: e.scalar_tensor_tensor(ot[:], tmbuf[:, 0:1024], r128d[:],
                                                                 gamma[:], op0=ALU.mult, op1=ALU.mult), 'VE')
            ve = p.op('vector', lambda e: e.tensor_tensor(ot[:], ot[:], beta[:], op=ALU.add), 'VE')
            if resid is not None:
                ve = p.op('vector', lambda e: e.tensor_tensor(ot[:], ot[:], resid[:], op=ALU.add), 'VE')
        return ve

    def ln_feat(tin, Tc, sv, bv, base, eps, tout, sqb):
        p.wait('tensor', 'VE', p.cur('VE'))
        p.wait('scalar', 'VE', p.cur('VE'))
        for dc in range(8):
            se = p.op('scalar', lambda e, dc=dc: e.activation(sqb[:, dc * Tc:(dc + 1) * Tc],
                                                              tin(dc), AF.Square), 'SE')
        se_sq = se
        p.use_bank(5)
        for dc in range(8):
            te = p.op('tensor', lambda e, dc=dc: e.matmul(bank[5][0:1, 0:Tc], ones128[:], tin(dc),
                                                          start=(dc == 0), stop=(dc == 7)), 'TE')
        p.wait('tensor', 'SE', se_sq)
        p.use_bank(6)
        for dc in range(8):
            te = p.op('tensor', lambda e, dc=dc: e.matmul(bank[6][0:1, 0:Tc], ones128[:],
                                                          sqb[:, dc * Tc:(dc + 1) * Tc],
                                                          start=(dc == 0), stop=(dc == 7)), 'TE')
        p.wait('vector', 'TE', te)
        ve = p.op('vector', lambda e, Tc=Tc: e.tensor_scalar(strow[0:1, 0:Tc], bank[5][0:1, 0:Tc],
                                                             1.0 / D, None, op0=ALU.mult), 'VE')
        ve = p.op('vector', lambda e, Tc=Tc: e.tensor_tensor(strow[0:1, 512:512 + Tc], strow[0:1, 0:Tc],
                                                             strow[0:1, 0:Tc], op=ALU.mult), 'VE')
        ve = p.op('vector', lambda e, Tc=Tc: e.scalar_tensor_tensor(strow[0:1, 512:512 + Tc],
                                                                    bank[6][0:1, 0:Tc], 1.0 / D,
                                                                    strow[0:1, 512:512 + Tc],
                                                                    op0=ALU.mult, op1=ALU.subtract), 'VE')
        p.set_bank(5, 'VE')
        p.set_bank(6, 'VE')
        ve = p.op('vector', lambda e: e.tensor_scalar(strow[0:1, 512:512 + Tc],
                                                      strow[0:1, 512:512 + Tc], eps, None,
                                                      op0=ALU.add), 'VE')
        p.wait('scalar', 'VE', ve)
        se = p.op('scalar', lambda e: e.activation(strow[0:1, 512:512 + Tc],
                                                   strow[0:1, 512:512 + Tc], AF.Ln), 'SE')
        se = p.op('scalar', lambda e, Tc=Tc: e.activation(abrow[0:1, 0:Tc], strow[0:1, 512:512 + Tc],
                                                          AF.Exp, scale=-0.5), 'SE')
        p.wait('vector', 'SE', se)
        ve = p.op('vector', lambda e, Tc=Tc: e.tensor_tensor(abrow[0:1, 512:512 + Tc], strow[0:1, 0:Tc],
                                                             abrow[0:1, 0:Tc], op=ALU.mult), 'VE')
        p.wait('tensor', 'VE', ve)
        p.use_bank(3)
        te = p.op('tensor', lambda e, Tc=Tc: e.matmul(bank[3][:, 0:Tc], onesrow[0:1, :],
                                                      abrow[0:1, 0:Tc], start=True, stop=True), 'TE')
        p.use_bank(4)
        te = p.op('tensor', lambda e, Tc=Tc: e.matmul(bank[4][:, 0:Tc], onesrow[0:1, :],
                                                      abrow[0:1, 512:512 + Tc], start=True, stop=True), 'TE')
        p.wait('vector', 'TE', te)
        for dc in range(8):
            ve = p.op('vector', lambda e, dc=dc: e.tensor_tensor(sqb[:, dc * Tc:(dc + 1) * Tc],
                                                                 tin(dc), bank[3][:, 0:Tc], op=ALU.mult), 'VE')
            ve = p.op('vector', lambda e, dc=dc: e.tensor_tensor(sqb[:, dc * Tc:(dc + 1) * Tc],
                                                                 sqb[:, dc * Tc:(dc + 1) * Tc],
                                                                 bank[4][:, 0:Tc], op=ALU.subtract), 'VE')
            p.wait('scalar', 'VE', ve)
            se = p.op('scalar', lambda e, dc=dc: e.activation(tout(dc), sqb[:, dc * Tc:(dc + 1) * Tc],
                                                              AF.Identity,
                                                              bias=bv[:, base + dc:base + dc + 1],
                                                              scale=sv[:, base + dc:base + dc + 1]), 'SE')
        p.set_bank(3, 'VE')
        p.set_bank(4, 'VE')
        p.wait('vector', 'SE', se)
        return se

    with ExitStack() as ps0:
        sb0 = lambda n, s, d=F32: ps0.enter_context(nc.sbuf_tensor(n, s, d))
        xa = sb0("xa", [128, 1024]); ua = sb0("ua", [128, 1024])
        tm = sb0("tm", [128, 2048]); ustg = sb0("ustg", [128, 4096])
        for tg in range(8):
            for j in range(4):
                tc = tg * 4 + j
                v = p.dma('sync', xa[:], xin[tc * 128:(tc + 1) * 128, :], sem='p0x')
                p.wait('vector', 'p0x', v)
                ve = ln_tok(xa, ua, LNE_T, tm)
                p.wait('tensor', 'VE', ve)
                for dc in range(8):
                    bk = dc % 2
                    te = p.op('tensor', lambda e, dc=dc, bk=bk: e.transpose(
                        bank[bk][:, 0:128], ua[:, dc * 128:(dc + 1) * 128], ident[:]), 'TE')
                    p.wait('scalar', 'TE', te)
                    se = p.op('scalar', lambda e, dc=dc, j=j, bk=bk: e.copy(
                        ustg[:, dc * 512 + j * 128: dc * 512 + (j + 1) * 128], bank[bk][:, 0:128]), 'SE')
                    p.wait('tensor', 'SE', se)
                p.barrier(('sync', 'vector', 'scalar', 'tensor'))
            for dc in range(8):
                v = p.dma('sync', uA[dc * 128:(dc + 1) * 128, tg * 512:(tg + 1) * 512],
                          ustg[:, dc * 512:(dc + 1) * 512], sem='p0s')
            p.wait('scalar', 'p0s', v)
            p.barrier(('sync', 'scalar'))
        p.barrier()

    if stage == "p0":
        ov = out.rearrange("t d -> (t d)").rearrange("(d t) -> d t", d=D)
        for dc in range(8):
            p.dma('sync', ov[dc * 128:(dc + 1) * 128, :], uA[dc * 128:(dc + 1) * 128, :])
        p.wait('sync', 'LD', p.cur('LD'))
        return finish(nc, ges, p)

    ucur, unxt = uA, uB
    with ExitStack() as psa:
        for blk in range(NBLK):
            b8 = blk * 8
            b32 = blk * 32
            with ExitStack() as ps1:
                sb1 = lambda n, s, d=F32: ps1.enter_context(nc.sbuf_tensor(n, s, d))
                wA = sb1(f"wA{blk}", [128, 8192], BF16)
                wB = sb1(f"wB{blk}", [128, 8192], BF16)
                ub2 = [sb1(f"ubf{blk}_{i}", [128, 4096], BF16) for i in range(2)]
                vbf = sb1(f"vbf{blk}", [128, 4096], BF16)
                uf2 = [sb1(f"ufp{blk}_{i}", [128, 4096]) for i in range(2)]
                t1b = sb1(f"t1b{blk}", [128, 4096])
                sqa = sb1(f"sqa{blk}", [128, 4096])
                pscr = sb1(f"pscr{blk}", [128, 3072])

                def scrp(par, i):
                    return pscr[:, (par * 3 + i) * 512:(par * 3 + i + 1) * 512]

                for c in range(8):
                    p.dma('gpsimd', wA[:, c * 1024:(c + 1) * 1024], aWk[blk, c * 128:(c + 1) * 128, :])
                    gv = p.dma('gpsimd', wB[:, c * 1024:(c + 1) * 1024], aWv[blk, c * 128:(c + 1) * 128, :])
                p.op('gpsimd', lambda e: e.memset(kvacc[:], 0.0), 'GE')
                ge = p.op('gpsimd', lambda e: e.memset(ksacc[:], 0.0), 'GE')
                p.wait('tensor', 'GD', gv)
                p.wait('vector', 'GE', ge)

                def load_u(dst, tc, eng='gpsimd', sem=None):
                    for dc in range(8):
                        gv = p.dma(eng, dst[:, dc * 512:(dc + 1) * 512],
                                   ucur[dc * 128:(dc + 1) * 128, tc * 512:(tc + 1) * 512],
                                   sem=sem)
                    return gv

                ubsem = ['ubA', 'ubB']
                ub_ld = [None, None]
                ub_te = [0, 0]
                ub_ld[0] = load_u(ub2[0], 0, sem=ubsem[0])
                ve_par = [0, 0]
                for tc in range(8):
                    bi = tc // 4
                    par_u = tc % 2
                    if tc + 1 < 8:
                        p.wait('gpsimd', 'TE', ub_te[1 - par_u])
                        ub_ld[1 - par_u] = load_u(ub2[1 - par_u], tc + 1, sem=ubsem[1 - par_u])
                    ubf = ub2[par_u]
                    p.wait('tensor', ubsem[par_u], ub_ld[par_u])
                    for oc in range(8):
                        par = oc % 2
                        bk_k = par * 2
                        bk_v = bk_k + 1
                        p.use_bank(bk_k)
                        for dc in range(8):
                            te = p.op('tensor', lambda e, dc=dc, oc=oc, bk_k=bk_k, ubf=ubf: e.matmul(
                                bank[bk_k][:], wA[:, dc * 1024 + oc * 128: dc * 1024 + (oc + 1) * 128],
                                ubf[:, dc * 512:(dc + 1) * 512], start=(dc == 0), stop=(dc == 7)), 'TE')
                        te_k = te
                        p.use_bank(bk_v)
                        for dc in range(8):
                            te = p.op('tensor', lambda e, dc=dc, oc=oc, bk_v=bk_v, ubf=ubf: e.matmul(
                                bank[bk_v][:], wB[:, dc * 1024 + oc * 128: dc * 1024 + (oc + 1) * 128],
                                ubf[:, dc * 512:(dc + 1) * 512], start=(dc == 0), stop=(dc == 7)), 'TE')
                        te_v = te
                        ub_te[par_u] = te_v
                        p.wait('scalar', 'VE', ve_par[par])
                        p.wait('scalar', 'TE', te_k)
                        se = p.op('scalar', lambda e, oc=oc, bk_k=bk_k, par=par: e.activation(
                            scrp(par, 0), bank[bk_k][:], AF.Relu, bias=vk[:, b8 + oc:b8 + oc + 1]), 'SE')
                        se = p.op('scalar', lambda e, oc=oc, bk_k=bk_k, par=par: e.activation(
                            scrp(par, 1), bank[bk_k][:], AF.Relu, bias=vnk[:, b8 + oc:b8 + oc + 1],
                            scale=-1.0), 'SE')
                        p.set_bank(bk_k, 'SE')
                        se = p.op('scalar', lambda e, par=par: e.activation(scrp(par, 1), scrp(par, 1),
                                                                            AF.Exp, scale=-1.0), 'SE')
                        p.wait('scalar', 'TE', te_v)
                        se = p.op('scalar', lambda e, oc=oc, bk_v=bk_v, par=par: e.activation(
                            scrp(par, 2), bank[bk_v][:], AF.Identity, bias=vv[:, b8 + oc:b8 + oc + 1]), 'SE')
                        p.set_bank(bk_v, 'SE')
                        p.wait('vector', 'SE', se)
                        ve = p.op('vector', lambda e, par=par: e.tensor_tensor(
                            scrp(par, 0), scrp(par, 0), scrp(par, 1), op=ALU.add), 'VE')
                        ve = p.op('vector', lambda e, par=par: e.reduce_sum(r128b[:], scrp(par, 0),
                                                                            axis=AX), 'VE')
                        ve = p.op('vector', lambda e, oc=oc, bi=bi: e.tensor_tensor(
                            ksacc[:, bi * 8 + oc:bi * 8 + oc + 1], ksacc[:, bi * 8 + oc:bi * 8 + oc + 1],
                            r128b[:], op=ALU.add), 'VE')
                        ve = p.op('vector', lambda e, par=par: e.tensor_tensor(
                            scrp(par, 0), scrp(par, 0), scrp(par, 2), op=ALU.mult), 'VE')
                        ve = p.op('vector', lambda e, par=par: e.reduce_sum(r128a[:], scrp(par, 0),
                                                                            axis=AX), 'VE')
                        ve = p.op('vector', lambda e, oc=oc, bi=bi: e.tensor_tensor(
                            kvacc[:, bi * 8 + oc:bi * 8 + oc + 1], kvacc[:, bi * 8 + oc:bi * 8 + oc + 1],
                            r128a[:], op=ALU.add), 'VE')
                        ve_par[par] = ve
                p.wait('gpsimd', 'TE', p.cur('TE'))
                for c in range(8):
                    p.dma('gpsimd', wA[:, c * 1024:(c + 1) * 1024], aWq[blk, c * 128:(c + 1) * 128, :])
                    gv = p.dma('gpsimd', wB[:, c * 1024:(c + 1) * 1024], aWo[blk, c * 128:(c + 1) * 128, :])
                p.wait('vector', 'VE', p.cur('VE'))
                p.wait('tensor', 'GD', gv)
                ufsem = ['ufA', 'ufB']
                stsem = ['stA', 'stB']
                ub_ld = [None, None]
                uf_ld = [None, None]
                ub_te = [0, 0]
                uf_st = [0, 0]
                uf_ve = [0, 0]
                ub_ld[0] = load_u(ub2[0], 0, sem=ubsem[0])
                uf_ld[0] = load_u(uf2[0], 0, 'sync', sem=ufsem[0])
                ve_par = [0, 0]
                ln_tail_te = 0
                for tc in range(8):
                    bi = tc // 4
                    par_u = tc % 2
                    if tc + 1 < 8:
                        p.wait('gpsimd', 'TE', ub_te[1 - par_u])
                        ub_ld[1 - par_u] = load_u(ub2[1 - par_u], tc + 1, sem=ubsem[1 - par_u])
                        p.wait('sync', stsem[1 - par_u], uf_st[1 - par_u])
                        p.wait('sync', 'VE', uf_ve[1 - par_u])
                        uf_ld[1 - par_u] = load_u(uf2[1 - par_u], tc + 1, 'sync',
                                                  sem=ufsem[1 - par_u])
                    ubf = ub2[par_u]
                    ufp = uf2[par_u]
                    p.wait('tensor', ubsem[par_u], ub_ld[par_u])
                    for oc in range(8):
                        par = oc % 2
                        bk_q = par
                        p.use_bank(bk_q)
                        for dc in range(8):
                            te = p.op('tensor', lambda e, dc=dc, oc=oc, bk_q=bk_q, ubf=ubf: e.matmul(
                                bank[bk_q][:], wA[:, dc * 1024 + oc * 128: dc * 1024 + (oc + 1) * 128],
                                ubf[:, dc * 512:(dc + 1) * 512], start=(dc == 0), stop=(dc == 7)), 'TE')
                        ub_te[par_u] = te
                        p.wait('scalar', 'VE', ve_par[par])
                        p.wait('scalar', 'TE', te)
                        se = p.op('scalar', lambda e, oc=oc, bk_q=bk_q, par=par: e.activation(
                            scrp(par, 0), bank[bk_q][:], AF.Relu, bias=vq[:, b8 + oc:b8 + oc + 1]), 'SE')
                        se = p.op('scalar', lambda e, oc=oc, bk_q=bk_q, par=par: e.activation(
                            scrp(par, 1), bank[bk_q][:], AF.Relu, bias=vnq[:, b8 + oc:b8 + oc + 1],
                            scale=-1.0), 'SE')
                        p.set_bank(bk_q, 'SE')
                        se = p.op('scalar', lambda e, par=par: e.activation(scrp(par, 1), scrp(par, 1),
                                                                            AF.Exp, scale=-1.0), 'SE')
                        p.wait('vector', 'SE', se)
                        ve = p.op('vector', lambda e, par=par: e.tensor_tensor(
                            scrp(par, 0), scrp(par, 0), scrp(par, 1), op=ALU.add), 'VE')
                        ve = p.op('vector', lambda e, oc=oc, bi=bi, par=par: e.tensor_scalar(
                            scrp(par, 1), scrp(par, 0), ksacc[:, bi * 8 + oc:bi * 8 + oc + 1], 1e-6,
                            op0=ALU.mult, op1=ALU.add), 'VE')
                        ve = p.op('vector', lambda e, par=par: e.reciprocal(scrp(par, 2),
                                                                            scrp(par, 1)), 'VE')
                        ve = p.op('vector', lambda e, oc=oc, bi=bi, par=par: e.scalar_tensor_tensor(
                            vbf[:, oc * 512:(oc + 1) * 512], scrp(par, 0),
                            kvacc[:, bi * 8 + oc:bi * 8 + oc + 1], scrp(par, 2),
                            op0=ALU.mult, op1=ALU.mult), 'VE')
                        ve_par[par] = ve
                    p.wait('tensor', 'VE', ve)
                    p.wait('vector', ufsem[par_u], uf_ld[par_u])
                    for dc in range(8):
                        par = dc % 2
                        bk_a = 2 + par
                        p.use_bank(bk_a)
                        for ac in range(8):
                            te = p.op('tensor', lambda e, ac=ac, dc=dc, bk_a=bk_a: e.matmul(
                                bank[bk_a][:], wB[:, ac * 1024 + dc * 128: ac * 1024 + (dc + 1) * 128],
                                vbf[:, ac * 512:(ac + 1) * 512], start=(ac == 0), stop=(ac == 7)), 'TE')
                        p.wait('scalar', 'VE', ve_par[par])
                        p.wait('scalar', 'TE', te)
                        se = p.op('scalar', lambda e, dc=dc, bk_a=bk_a, par=par: e.activation(
                            scrp(par, 0), bank[bk_a][:], AF.Gelu_apprx_tanh,
                            bias=vo[:, b8 + dc:b8 + dc + 1]), 'SE')
                        p.set_bank(bk_a, 'SE')
                        se = p.op('scalar', lambda e, par=par: e.activation(
                            scrp(par, 1), scrp(par, 0), AF.Gelu_apprx_tanh), 'SE')
                        p.wait('vector', 'SE', se)
                        p.wait('vector', 'TE', ln_tail_te)
                        ve = p.op('vector', lambda e, dc=dc, par=par, ufp=ufp: e.tensor_tensor(
                            t1b[:, dc * 512:(dc + 1) * 512], scrp(par, 1),
                            ufp[:, dc * 512:(dc + 1) * 512], op=ALU.add), 'VE')
                        ve_par[par] = ve
                        uf_ve[par_u] = ve
                    ln_feat(lambda dc, t1b=t1b: t1b[:, dc * 512:(dc + 1) * 512], 512, vl1s, vl1b,
                            b8, LNE_A, lambda dc, ufp=ufp: ufp[:, dc * 512:(dc + 1) * 512], sqa)
                    ln_tail_te = p.cur('TE')
                    p.wait('sync', 'SE', p.cur('SE'))
                    for dc in range(8):
                        v = p.dma('sync', x1T[dc * 128:(dc + 1) * 128, tc * 512:(tc + 1) * 512],
                                  ufp[:, dc * 512:(dc + 1) * 512], sem=stsem[par_u])
                    uf_st[par_u] = v
                p.barrier()
            with ExitStack() as ps2:
                sb2 = lambda n, s, d=F32: ps2.enter_context(nc.sbuf_tensor(n, s, d))
                w1sb = sb2(f"w1sb{blk}", [128, 32768], BF16)
                w2sb = sb2(f"w2sb{blk}", [128, 32768], BF16)
                xb2 = [sb2(f"x1bf{blk}_{i}", [128, 2048], BF16) for i in range(2)]
                hbf = sb2(f"hbf{blk}", [128, 8192], BF16)
                ufp = sb2(f"ufp2{blk}", [128, 2048])
                t1b = sb2(f"t1b2{blk}", [128, 2048])
                sqa = sb2(f"sqa2{blk}", [128, 2048])
                p2scr = sb2(f"p2scr{blk}", [128, 512])
                for c in range(8):
                    gv = p.dma('gpsimd', w1sb[:, c * 4096:(c + 1) * 4096],
                               aW1[blk, c * 128:(c + 1) * 128, :])
                for c in range(32):
                    gv = p.dma('gpsimd', w2sb[:, c * 1024:(c + 1) * 1024],
                               aW2[blk, c * 128:(c + 1) * 128, :])
                p.wait('tensor', 'GD', gv)

                def load_x1(dst, tc, eng, sem=None, ncols=256):
                    for dc in range(8):
                        gv = p.dma(eng, dst[:, dc * ncols:(dc + 1) * ncols],
                                   x1T[dc * 128:(dc + 1) * 128, tc * ncols:(tc + 1) * ncols],
                                   sem=sem)
                    return gv

                xbsem = ['xbA', 'xbB']
                xb_ld = [None, None]
                xb_te = [0, 0]
                xb_ld[0] = load_x1(xb2[0], 0, 'gpsimd', sem=xbsem[0])
                uf_ld = load_x1(ufp, 0, 'sync', sem='ufl')
                uf_st = 0
                uf_ve = 0
                ve_par = [0, 0]
                ln_tail_te = 0
                hbf_te = 0
                for tc in range(16):
                    par_x = tc % 2
                    if tc + 1 < 16:
                        p.wait('gpsimd', 'TE', xb_te[1 - par_x])
                        xb_ld[1 - par_x] = load_x1(xb2[1 - par_x], tc + 1, 'gpsimd',
                                                   sem=xbsem[1 - par_x])
                    x1bf = xb2[par_x]
                    p.wait('tensor', xbsem[par_x], xb_ld[par_x])
                    for fc in range(32):
                        bk_h = fc % 2
                        p.use_bank(bk_h)
                        for dc in range(8):
                            te = p.op('tensor', lambda e, dc=dc, fc=fc, bk_h=bk_h, x1bf=x1bf: e.matmul(
                                bank[bk_h][:, 0:256],
                                w1sb[:, dc * 4096 + fc * 128: dc * 4096 + (fc + 1) * 128],
                                x1bf[:, dc * 256:(dc + 1) * 256], start=(dc == 0), stop=(dc == 7)), 'TE')
                        xb_te[par_x] = te
                        p.wait('scalar', 'TE', te)
                        if fc == 0:
                            p.wait('scalar', 'TE', hbf_te)
                        se = p.op('scalar', lambda e, fc=fc, bk_h=bk_h: e.activation(
                            hbf[:, fc * 256:(fc + 1) * 256], bank[bk_h][:, 0:256], AF.Gelu_apprx_tanh,
                            bias=v1[:, b32 + fc:b32 + fc + 1]), 'SE')
                        p.set_bank(bk_h, 'SE')
                    p.wait('tensor', 'SE', p.cur('SE'))
                    p.wait('vector', 'ufl', uf_ld)
                    for dc in range(8):
                        par = dc % 2
                        bk_f = 2 + par
                        p.use_bank(bk_f)
                        for fc in range(32):
                            te = p.op('tensor', lambda e, fc=fc, dc=dc, bk_f=bk_f: e.matmul(
                                bank[bk_f][:, 0:256],
                                w2sb[:, fc * 1024 + dc * 128: fc * 1024 + (dc + 1) * 128],
                                hbf[:, fc * 256:(fc + 1) * 256], start=(fc == 0), stop=(fc == 31)), 'TE')
                        if dc == 7:
                            hbf_te = te
                        p.wait('scalar', 'VE', ve_par[par])
                        p.wait('scalar', 'TE', te)
                        se = p.op('scalar', lambda e, dc=dc, bk_f=bk_f, par=par: e.activation(
                            p2scr[:, par * 256:(par + 1) * 256], bank[bk_f][:, 0:256],
                            AF.Gelu_apprx_tanh, bias=v2[:, b8 + dc:b8 + dc + 1]), 'SE')
                        p.set_bank(bk_f, 'SE')
                        p.wait('vector', 'SE', se)
                        p.wait('vector', 'TE', ln_tail_te)
                        ve = p.op('vector', lambda e, dc=dc, par=par: e.tensor_tensor(
                            t1b[:, dc * 256:(dc + 1) * 256], p2scr[:, par * 256:(par + 1) * 256],
                            ufp[:, dc * 256:(dc + 1) * 256], op=ALU.add), 'VE')
                        ve_par[par] = ve
                        uf_ve = ve
                    ln_feat(lambda dc, t1b=t1b: t1b[:, dc * 256:(dc + 1) * 256], 256, vl2s, vl2b,
                            b8, LNE_A, lambda dc, ufp=ufp: ufp[:, dc * 256:(dc + 1) * 256], sqa)
                    ln_tail_te = p.cur('TE')
                    p.wait('sync', 'SE', p.cur('SE'))
                    for dc in range(8):
                        v = p.dma('sync', unxt[dc * 128:(dc + 1) * 128, tc * 256:(tc + 1) * 256],
                                  ufp[:, dc * 256:(dc + 1) * 256], sem='ufs')
                    uf_st = v
                    if tc + 1 < 16:
                        p.wait('sync', 'ufs', uf_st)
                        uf_ld = load_x1(ufp, tc + 1, 'sync', sem='ufl')
                p.barrier()
            ucur, unxt = unxt, ucur
            if stage == f"blk{blk}":
                ov = out.rearrange("t d -> (t d)").rearrange("(d t) -> d t", d=D)
                for dc in range(8):
                    p.dma('sync', ov[dc * 128:(dc + 1) * 128, :], ucur[dc * 128:(dc + 1) * 128, :])
                p.wait('sync', 'LD', p.cur('LD'))
                return finish(nc, ges, p)

    with ExitStack() as psm:
        sbm = lambda n, s, d=F32: psm.enter_context(nc.sbuf_tensor(n, s, d))
        xa = sbm("mxa", [128, 1024]); xb = sbm("mxb", [128, 1024])
        ua = sbm("mua", [128, 1024]); tm = sbm("mtm", [128, 2048])
        ustg = sbm("mustg", [128, 2048])
        for tc in range(32):
            vxa = p.dma('sync', xa[:], xin[tc * 128:(tc + 1) * 128, :], sem='pmx')
            for dc in range(8):
                gv = p.dma('gpsimd', ustg[:, dc * 128:(dc + 1) * 128],
                           ucur[dc * 128:(dc + 1) * 128, tc * 128:(tc + 1) * 128], sem='pmg')
            p.wait('tensor', 'pmg', gv)
            for dc in range(8):
                bk = dc % 2
                te = p.op('tensor', lambda e, dc=dc, bk=bk: e.transpose(
                    bank[bk][:, 0:128], ustg[:, dc * 128:(dc + 1) * 128], ident[:]), 'TE')
                p.wait('scalar', 'TE', te)
                se = p.op('scalar', lambda e, dc=dc, bk=bk: e.copy(
                    xb[:, dc * 128:(dc + 1) * 128], bank[bk][:, 0:128]), 'SE')
                p.wait('tensor', 'SE', se)
            p.wait('vector', 'SE', se)
            p.wait('vector', 'pmx', vxa)
            ve = p.op('vector', lambda e: e.tensor_tensor(xa[:], xa[:], xb[:], op=ALU.add), 'VE')
            p.wait('sync', 'VE', ve)
            v = p.dma('sync', xmid[tc * 128:(tc + 1) * 128, :], xa[:], sem='pms')
            ve = ln_tok(xa, ua, LNE_T, tm)
            p.wait('sync', 'VE', ve)
            v = p.dma('sync', u2tok[tc * 128:(tc + 1) * 128, :], ua[:], sem='pms')
            p.wait('tensor', 'VE', ve)
            for dc in range(8):
                bk = 2 + dc % 2
                te = p.op('tensor', lambda e, dc=dc, bk=bk: e.transpose(
                    bank[bk][:, 0:128], ua[:, dc * 128:(dc + 1) * 128], ident[:]), 'TE')
                p.wait('scalar', 'TE', te)
                se = p.op('scalar', lambda e, dc=dc, bk=bk: e.copy(
                    ustg[:, 1024 + dc * 128: 1024 + (dc + 1) * 128] if False else ustg[:, dc * 128:(dc + 1) * 128],
                    bank[bk][:, 0:128]), 'SE')
                p.wait('tensor', 'SE', se)
            p.wait('sync', 'SE', se)
            for dc in range(8):
                v = p.dma('sync', u2T[dc * 128:(dc + 1) * 128, tc * 128:(tc + 1) * 128],
                          ustg[:, dc * 128:(dc + 1) * 128], sem='pms')
            p.wait('gpsimd', 'pms', v)
            p.barrier(('sync', 'gpsimd', 'tensor', 'scalar', 'vector'))
        p.barrier()

    if stage == "pmid":
        with ExitStack() as pst_:
            xa = pst_.enter_context(nc.sbuf_tensor("dxa", [128, 1024], F32))
            for tc in range(32):
                v = p.dma('sync', xa[:], u2tok[tc * 128:(tc + 1) * 128, :])
                p.wait('sync', 'LD', v)
                v = p.dma('sync', out[tc * 128:(tc + 1) * 128, :], xa[:])
                p.wait('sync', 'LD', v)
        return finish(nc, ges, p)

    with ExitStack() as psf:
        sbf = lambda n, s, d=F32: psf.enter_context(nc.sbuf_tensor(n, s, d))
        qTb = sbf("qTb", [128, 16384], BF16)
        kTb = sbf("kTb", [128, 16384], BF16)
        vtok = sbf("vtok", [128, 16640], BF16)
        l8 = sbf("l8", [128, 8]); e8 = sbf("e8", [128, 8])
        w8a = sbf("w8a", [128, 8]); w8b = sbf("w8b", [128, 8])

        for tt in range(16):
            for h in range(NH):
                ge = p.op('gpsimd', lambda e, tt=tt, h=h: e.memset(
                    vtok[:, tt * 1040 + h * 65 + 64: tt * 1040 + h * 65 + 65], 1.0), 'GE')
        ge_ones = p.cur('GE')

        def transpose_w(dst, src, wstg):
            for rc in range(8):
                gv = p.dma('gpsimd', wstg[:], src[rc * 128:(rc + 1) * 128, :], sem='fw')
                p.wait('tensor', 'fw', gv)
                for cc in range(8):
                    te = p.op('tensor', lambda e, cc=cc: e.transpose(
                        pbf[:, (cc % 4) * 128:(cc % 4 + 1) * 128],
                        wstg[:, cc * 128:(cc + 1) * 128], identb[:]), 'TE')
                    p.wait('scalar', 'TE', te)
                    se = p.op('scalar', lambda e, cc=cc, rc=rc, dst=dst: e.copy(
                        dst[:, cc * 1024 + rc * 128: cc * 1024 + (rc + 1) * 128],
                        pbf[:, (cc % 4) * 128:(cc % 4 + 1) * 128]), 'SE')
                    p.wait('tensor', 'SE', se)
                p.wait('gpsimd', 'SE', se)

        if True:
            for b in range(2):
              with ExitStack() as psf1:
                sbf1 = lambda n, s, d=F32: psf1.enter_context(nc.sbuf_tensor(n, s, d))
                wqT = sbf1(f"wqT{b}", [128, 8192], BF16)
                wkT = sbf1(f"wkT{b}", [128, 8192], BF16)
                wvT = sbf1(f"wvT{b}", [128, 8192], BF16)
                wstg = sbf1(f"wstg{b}", [128, 1024], BF16)
                u2bf = sbf1(f"u2bf{b}", [128, 4096], BF16)
                fbv_row = sbf1(f"fbv_row{b}", [1, 1024])
                p.dma('sync', fbv_row[:], fbv.rearrange("(a b) -> a b", a=1))
                transpose_w(wqT, fWq, wstg)
                transpose_w(wkT, fWk, wstg)
                transpose_w(wvT, fWv, wstg)
                p.barrier()
                for tc in range(4):
                    for dc in range(8):
                        gv = p.dma('gpsimd', u2bf[:, dc * 512:(dc + 1) * 512],
                                   u2T[dc * 128:(dc + 1) * 128, (b * 4 + tc) * 512:(b * 4 + tc + 1) * 512],
                                   sem='fu')
                    p.wait('tensor', 'fu', gv)
                    for pi, wT, bias_c, dstT in ((0, wqT, fvq, qTb), (1, wkT, fvk, kTb)):
                        for oc in range(8):
                            bk_x = pi * 2 + oc % 2
                            for dc in range(8):
                                te = p.op('tensor', lambda e, dc=dc, oc=oc, wT=wT, bk_x=bk_x: e.matmul(
                                    bank[bk_x][:], wT[:, dc * 1024 + oc * 128: dc * 1024 + (oc + 1) * 128],
                                    u2bf[:, dc * 512:(dc + 1) * 512], start=(dc == 0), stop=(dc == 7)), 'TE')
                            p.wait('scalar', 'TE', te)
                            se = p.op('scalar', lambda e, oc=oc, tc=tc, bias_c=bias_c, dstT=dstT, bk_x=bk_x:
                                      e.activation(dstT[:, oc * 2048 + tc * 512: oc * 2048 + (tc + 1) * 512],
                                                   bank[bk_x][:], AF.Identity, bias=bias_c[:, oc:oc + 1]), 'SE')
                            p.wait('tensor', 'SE', se)
                    p.wait('scalar', 'GE', ge_ones)
                    for ts_ in range(4):
                        tt = tc * 4 + ts_
                        for half in range(2):
                            bk_x = 4 + half
                            for dc in range(8):
                                te = p.op('tensor', lambda e, dc=dc, ts_=ts_, half=half, bk_x=bk_x: e.matmul(
                                    bank[bk_x][:], u2bf[:, dc * 512 + ts_ * 128: dc * 512 + (ts_ + 1) * 128],
                                    wvT[:, dc * 1024 + half * 512: dc * 1024 + (half + 1) * 512],
                                    start=(dc == 0), stop=False), 'TE')
                            te = p.op('tensor', lambda e, half=half, bk_x=bk_x: e.matmul(
                                bank[bk_x][:], onesrow[0:1, :], fbv_row[0:1, half * 512:(half + 1) * 512],
                                start=False, stop=True), 'TE')
                            p.wait('scalar', 'TE', te)
                            for j in range(8):
                                h = half * 8 + j
                                se = p.op('scalar', lambda e, tt=tt, h=h, j=j, bk_x=bk_x: e.copy(
                                    vtok[:, tt * 1040 + h * 65: tt * 1040 + h * 65 + 64],
                                    bank[bk_x][:, j * 64:(j + 1) * 64]), 'SE')
                            p.wait('tensor', 'SE', se)
                    p.barrier(('gpsimd', 'tensor', 'scalar', 'vector', 'sync'))
              with ExitStack() as psf2:
                sbf2 = lambda n, s, d=F32: psf2.enter_context(nc.sbuf_tensor(n, s, d))
                pbuf = sbf2(f"pbuf{b}", [128, 8192], BF16)
                hstg2 = sbf2(f"hstg{b}", [128, 8192])
                masks = sbf2(f"masks{b}", [128, 2048], BF16)
                zstg = sbf2(f"zstg{b}", [1, 1024]); z16 = sbf2(f"z16{b}", [16, 1024])
                r16 = sbf2(f"r16{b}", [16, 1024])
                rstage = sbf2(f"rstage{b}", [1, 1024])
                rbc = scr[1]
                v = p.dma('sync', masks[:], c_mask)
                p.wait('vector', 'LD', v)
                hstg_te = [0, 0]
                hstg_fh = [0, 0]
                z_ve = [0, 0]
                r_te = [0, 0]
                for qb in range(4):
                    nreal = 4 * qb + 4
                    par = qb % 2
                    hoff = par * 4096
                    zoff = par * 512
                    exp_last = 0
                    for h in range(NH):
                        hb = (64 * h) % 128
                        hc = h // 2
                        bk_pv = 2 + h % 2
                        for tt in range(nreal):
                            d_off = 512 * qb - 128 * tt
                            bk_s = tt % 2
                            p.use_bank(bk_s)
                            te = p.op('tensor', lambda e, tt=tt, qb=qb, bk_s=bk_s, hb=hb, hc=hc: e.matmul(
                                bank[bk_s][:], kTb[hb:hb + 64, hc * 2048 + tt * 128: hc * 2048 + (tt + 1) * 128],
                                qTb[hb:hb + 64, hc * 2048 + qb * 512: hc * 2048 + (qb + 1) * 512],
                                start=True, stop=True), 'TE')
                            if d_off >= 128:
                                p.wait('scalar', 'TE', te)
                                se = p.op('scalar', lambda e, tt=tt, bk_s=bk_s: e.activation(
                                    pbuf[:, tt * 512:(tt + 1) * 512], bank[bk_s][:], AF.Exp,
                                    scale=1.0 / 32.0), 'SE')
                                p.set_bank(bk_s, 'SE')
                            else:
                                c = (-d_off) // 128
                                sp = tt % 2
                                p.wait('vector', 'TE', te)
                                p.wait('vector', 'SE', exp_last)
                                ve = p.op('vector', lambda e, c=c, bk_s=bk_s, sp=sp: e.tensor_tensor(
                                    scr[sp * 2][:], bank[bk_s][:], masks[:, c * 512:(c + 1) * 512],
                                    op=ALU.mult), 'VE')
                                p.set_bank(bk_s, 'VE')
                                p.wait('scalar', 'VE', ve)
                                se = p.op('scalar', lambda e, tt=tt, sp=sp: e.activation(
                                    pbuf[:, tt * 512:(tt + 1) * 512], scr[sp * 2][:], AF.Exp,
                                    scale=1.0 / 32.0), 'SE')
                            exp_last = se
                        p.use_bank(bk_pv)
                        order = list(range(nreal, 16)) + list(range(nreal))
                        for i, tt in enumerate(order):
                            if i == (16 - nreal):
                                p.wait('tensor', 'SE', exp_last)
                            if tt < nreal:
                                rhsf = (lambda e, tt=tt: pbuf[:, tt * 512:(tt + 1) * 512])
                            else:
                                rhsf = (lambda e: allonesb[:])
                            te = p.op('tensor', lambda e, tt=tt, h=h, rhsf=rhsf, i=i, bk_pv=bk_pv,
                                      nr=nreal: e.matmul(
                                bank[bk_pv][0:65, :], vtok[:, tt * 1040 + h * 65: tt * 1040 + h * 65 + 65],
                                rhsf(e), start=(i == 0), stop=(i == 15)), 'TE')
                        p.wait('scalar', 'TE', te)
                        if h == 0:
                            p.wait('scalar', 'TE', hstg_te[par])
                            p.wait('scalar', 'fh', hstg_fh[par])
                            p.wait('scalar', 'VE', z_ve[par])
                        se = p.op('scalar', lambda e, zoff=zoff, bk_pv=bk_pv: e.copy(
                            zstg[0:1, zoff:zoff + 512], bank[bk_pv][64:65, :]), 'SE')
                        se2 = p.op('scalar', lambda e, hb=hb, hc=hc, hoff=hoff, bk_pv=bk_pv: e.copy(
                            hstg2[hb:hb + 64, hoff + hc * 512: hoff + (hc + 1) * 512],
                            bank[bk_pv][0:64, :]), 'SE')
                        p.set_bank(bk_pv, 'SE')
                        p.wait('gpsimd', 'SE', se2)
                        gv = p.dma('gpsimd', z16[h:h + 1, zoff:zoff + 512],
                                   zstg[0:1, zoff:zoff + 512], sem='fz')
                        p.wait('scalar', 'fz', gv)
                    p.wait('vector', 'fz', p.cur('fz'))
                    ve = p.op('vector', lambda e, zoff=zoff: e.reciprocal(
                        r16[:, zoff:zoff + 512], z16[:, zoff:zoff + 512]), 'VE')
                    z_ve[par] = ve
                    p.wait('gpsimd', 'VE', ve)
                    for h in range(NH):
                        hb = (64 * h) % 128
                        hc = h // 2
                        p.wait('gpsimd', 'TE', r_te[par])
                        gv = p.dma('gpsimd', rstage[0:1, zoff:zoff + 512],
                                   r16[h:h + 1, zoff:zoff + 512], sem='fr')
                        p.wait('tensor', 'fr', gv)
                        p.use_bank(4)
                        te = p.op('tensor', lambda e, hb=hb, zoff=zoff: e.matmul(
                            bank[4][hb:hb + 64, :], onesrow[0:1, 0:64],
                            rstage[0:1, zoff:zoff + 512], start=True, stop=True), 'TE')
                        r_te[par] = te
                        p.wait('scalar', 'TE', te)
                        se = p.op('scalar', lambda e, hb=hb: e.copy(rbc[hb:hb + 64, :],
                                                                    bank[4][hb:hb + 64, :]), 'SE')
                        p.set_bank(4, 'SE')
                        p.wait('vector', 'SE', se)
                        ve = p.op('vector', lambda e, hb=hb, hc=hc, hoff=hoff: e.tensor_tensor(
                            hstg2[hb:hb + 64, hoff + hc * 512: hoff + (hc + 1) * 512],
                            hstg2[hb:hb + 64, hoff + hc * 512: hoff + (hc + 1) * 512],
                            rbc[hb:hb + 64, :], op=ALU.mult), 'VE')
                        p.wait('scalar', 'VE', ve)
                    nrm_ve = p.cur('VE')
                    p.wait('gpsimd', 'VE', nrm_ve)
                    for dc in range(8):
                        gv = p.dma('gpsimd', hidT[dc * 128:(dc + 1) * 128,
                                                  (b * 4 + qb) * 512:(b * 4 + qb + 1) * 512],
                                   hstg2[:, hoff + dc * 512: hoff + (dc + 1) * 512], sem='fh')
                    hstg_fh[par] = gv
                    for ts_ in range(4):
                        gtc = b * 16 + qb * 4 + ts_
                        p.use_bank(5)
                        if ts_ == 0:
                            p.wait('tensor', 'VE', nrm_ve)
                        for dc in range(8):
                            te = p.op('tensor', lambda e, dc=dc, ts_=ts_, hoff=hoff: e.matmul(
                                bank[5][:, 0:8],
                                hstg2[:, hoff + dc * 512 + ts_ * 128: hoff + dc * 512 + (ts_ + 1) * 128],
                                wswT[:, dc * 8:(dc + 1) * 8], start=(dc == 0), stop=False), 'TE')
                        te = p.op('tensor', lambda e: e.matmul(bank[5][:, 0:8], onesrow[0:1, :],
                                                               fbsw_row[0:1, :], start=False, stop=True), 'TE')
                        hstg_te[par] = te
                        p.wait('vector', 'TE', te)
                        ve = p.op('vector', lambda e: e.tensor_copy(l8[:], bank[5][:, 0:8]), 'VE')
                        p.set_bank(5, 'VE')
                        ve = p.op('vector', lambda e: e.reduce_max(r128a[:], l8[:], axis=AX, negate=True), 'VE')
                        p.wait('scalar', 'VE', ve)
                        se = p.op('scalar', lambda e: e.activation(e8[:], l8[:], AF.Exp, bias=r128a[:]), 'SE')
                        p.wait('vector', 'SE', se)
                        ve = p.op('vector', lambda e: e.reduce_sum(r128b[:], e8[:], axis=AX), 'VE')
                        ve = p.op('vector', lambda e: e.reciprocal(r128c[:], r128b[:]), 'VE')
                        ve = p.op('vector', lambda e, gtc=gtc: e.tensor_copy(rp_all[:, gtc:gtc + 1],
                                                                             r128c[:]), 'VE')
                        ve = p.op('vector', lambda e: e.tensor_scalar(r128d[:], r128a[:], -1.0, None,
                                                                      op0=ALU.mult), 'VE')
                        ve = p.op('vector', lambda e: e.tensor_scalar(w8a[:], l8[:], r128d[:], None,
                                                                      op0=ALU.is_ge), 'VE')
                        ve = p.op('vector', lambda e: e.tensor_tensor(w8b[:], w8a[:], rm8[:], op=ALU.mult), 'VE')
                        ve = p.op('vector', lambda e: e.tensor_scalar(w8b[:], w8b[:], -1.0, 8.0,
                                                                      op0=ALU.mult, op1=ALU.add), 'VE')
                        ve = p.op('vector', lambda e: e.tensor_reduce(r128d[:], w8b[:], axis=AX,
                                                                      op=ALU.min), 'VE')
                        ve = p.op('vector', lambda e, gtc=gtc: e.tensor_scalar(
                            oh_all[:, gtc * 8:(gtc + 1) * 8], io8[:], r128d[:], None,
                            op0=ALU.is_equal), 'VE')
                        p.wait('tensor', 'VE', ve)
                        p.use_bank(6)
                        te = p.op('tensor', lambda e, gtc=gtc: e.matmul(
                            bank[6][:, 0:8], sltri[:], oh_all[:, gtc * 8:(gtc + 1) * 8],
                            start=True, stop=False), 'TE')
                        te = p.op('tensor', lambda e: e.matmul(bank[6][:, 0:8], onesrow[0:1, :],
                                                               carry[0:1, :], start=False, stop=True), 'TE')
                        p.wait('vector', 'TE', te)
                        ve = p.op('vector', lambda e, gtc=gtc: e.tensor_copy(
                            rank_all[:, gtc * 8:(gtc + 1) * 8], bank[6][:, 0:8]), 'VE')
                        p.set_bank(6, 'VE')
                        p.wait('tensor', 'VE', ve)
                        p.use_bank(6)
                        te = p.op('tensor', lambda e, gtc=gtc: e.matmul(
                            bank[6][0:1, 0:8], e127[:], rank_all[:, gtc * 8:(gtc + 1) * 8],
                            start=True, stop=False), 'TE')
                        te = p.op('tensor', lambda e, gtc=gtc: e.matmul(
                            bank[6][0:1, 0:8], e127[:], oh_all[:, gtc * 8:(gtc + 1) * 8],
                            start=False, stop=True), 'TE')
                        p.wait('vector', 'TE', te)
                        ve = p.op('vector', lambda e: e.tensor_copy(carry[0:1, :],
                                                                    bank[6][0:1, 0:8]), 'VE')
                        p.set_bank(6, 'VE')
                p.barrier()
        p.barrier()

    v = p.dma('sync', agin.ap(), carry[0:1, :])
    p.wait('gpsimd', 'LD', v)
    cc = p.op('gpsimd', lambda e: e.collective_compute(
        "AllGather", ALU.bypass, replica_groups=[list(range(NCORES))],
        ins=[agin.ap().opt()], outs=[agout.ap().opt()]), 'CC')
    p.wait('gpsimd', 'CC', 1)
    gv = p.dma('gpsimd', cnt8[:], agout.ap())
    p.wait('tensor', 'GD', gv)
    te = p.op('tensor', lambda e: e.matmul(bank[5][0:8, 0:1], cnt8[:], rmask[:],
                                           start=True, stop=True), 'TE')
    p.wait('scalar', 'TE', te)
    se = p.op('scalar', lambda e: e.copy(offcol[:], bank[5][0:8, 0:1]), 'SE')
    p.wait('gpsimd', 'SE', se)
    gv = p.dma('gpsimd', offs_row[:], offcol[:])
    p.barrier()

    with ExitStack() as ps3:
        sb3 = lambda n, s, d=F32: ps3.enter_context(nc.sbuf_tensor(n, s, d))
        weT = [sb3(f"weT{i}", [128, 16384], BF16) for i in range(4)]
        wstg3 = sb3("wstg3", [128, 1024], BF16)
        hbf_t = sb3("hbf_t", [128, 1024], BF16)
        htok = sb3("htok", [128, 1024], BF16)
        yacc = sb3("yacc", [128, 1024])
        u2t_t = sb3("u2t_t", [128, 1024]); xm_t = sb3("xm_t", [128, 1024])
        ua3 = sb3("ua3", [128, 1024]); tm3 = sb3("tm3", [128, 2048])
        gam_bc = sb3("gam_bc", [128, 1024]); bet_bc = sb3("bet_bc", [128, 1024])
        fbe_sb = sb3("fbe_sb", [8, 1024])
        ct8 = sb3("ct8", [8, 128]); keptf = sb3("keptf", [128, 8]); ctile = sb3("ctile", [128, 8])

        p.dma('sync', fbe_sb[:], fbe)
        v = p.dma('sync', strow[0:1, 0:1024], gam.rearrange("(a b) -> a b", a=1))
        p.wait('tensor', 'LD', v)
        te = p.op('tensor', lambda e: e.matmul(bank[0][:], onesrow[0:1, :], strow[0:1, 0:512],
                                               start=True, stop=True), 'TE')
        te = p.op('tensor', lambda e: e.matmul(bank[1][:], onesrow[0:1, :], strow[0:1, 512:1024],
                                               start=True, stop=True), 'TE')
        p.wait('scalar', 'TE', te)
        p.op('scalar', lambda e: e.copy(gam_bc[:, 0:512], bank[0][:]), 'SE')
        se = p.op('scalar', lambda e: e.copy(gam_bc[:, 512:1024], bank[1][:]), 'SE')
        p.wait('sync', 'SE', se)
        v = p.dma('sync', strow[0:1, 0:1024], bet.rearrange("(a b) -> a b", a=1))
        p.wait('tensor', 'LD', v)
        p.wait('tensor', 'SE', se)
        te = p.op('tensor', lambda e: e.matmul(bank[0][:], onesrow[0:1, :], strow[0:1, 0:512],
                                               start=True, stop=True), 'TE')
        te = p.op('tensor', lambda e: e.matmul(bank[1][:], onesrow[0:1, :], strow[0:1, 512:1024],
                                               start=True, stop=True), 'TE')
        p.wait('scalar', 'TE', te)
        p.op('scalar', lambda e: e.copy(bet_bc[:, 0:512], bank[0][:]), 'SE')
        se = p.op('scalar', lambda e: e.copy(bet_bc[:, 512:1024], bank[1][:]), 'SE')
        p.wait('tensor', 'SE', se)
        for e_i in range(NE):
            dst = weT[e_i // 2]
            off = (e_i % 2) * 8192
            for rc in range(8):
                gv = p.dma('gpsimd', wstg3[:], fWe[e_i, rc * 128:(rc + 1) * 128, :])
                p.wait('tensor', 'GD', gv)
                for cc_ in range(8):
                    te = p.op('tensor', lambda e, cc_=cc_: e.transpose(
                        pbf[:, (cc_ % 4) * 128:(cc_ % 4 + 1) * 128],
                        wstg3[:, cc_ * 128:(cc_ + 1) * 128], identb[:]), 'TE')
                    p.wait('scalar', 'TE', te)
                    se = p.op('scalar', lambda e, cc_=cc_, rc=rc, dst=dst, off=off: e.copy(
                        dst[:, off + cc_ * 1024 + rc * 128: off + cc_ * 1024 + (rc + 1) * 128],
                        pbf[:, (cc_ % 4) * 128:(cc_ % 4 + 1) * 128]), 'SE')
                    p.wait('tensor', 'SE', se)
                p.wait('gpsimd', 'SE', se)
        p.barrier()

        for tc in range(32):
            for dc in range(8):
                gv = p.dma('gpsimd', hbf_t[:, dc * 128:(dc + 1) * 128],
                           hidT[dc * 128:(dc + 1) * 128, tc * 128:(tc + 1) * 128], sem='f3h')
            v = p.dma('sync', u2t_t[:], u2tok[tc * 128:(tc + 1) * 128, :], sem='f3u')
            v = p.dma('sync', xm_t[:], xmid[tc * 128:(tc + 1) * 128, :], sem='f3u')
            ld_in = v
            p.wait('tensor', 'f3h', gv)
            for dc in range(8):
                te = p.op('tensor', lambda e, dc=dc: e.transpose(
                    pbf[:, (dc % 4) * 128:(dc % 4 + 1) * 128],
                    hbf_t[:, dc * 128:(dc + 1) * 128], identb[:]), 'TE')
                p.wait('scalar', 'TE', te)
                se = p.op('scalar', lambda e, dc=dc: e.copy(
                    htok[:, dc * 128:(dc + 1) * 128], pbf[:, (dc % 4) * 128:(dc % 4 + 1) * 128]), 'SE')
                p.wait('tensor', 'SE', se)
            te = p.op('tensor', lambda e: e.matmul(bank[6][:, 0:8], onesrow[0:1, :], offs_row[0:1, :],
                                                   start=True, stop=True), 'TE')
            p.wait('vector', 'TE', te)
            p.wait('vector', 'SE', p.cur('SE'))
            ve = p.op('vector', lambda e, tc=tc: e.tensor_tensor(
                keptf[:], rank_all[:, tc * 8:(tc + 1) * 8], bank[6][:, 0:8], op=ALU.add), 'VE')
            ve = p.op('vector', lambda e: e.tensor_scalar(keptf[:], keptf[:], float(CAP), None,
                                                          op0=ALU.is_lt), 'VE')
            ve = p.op('vector', lambda e, tc=tc: e.tensor_tensor(
                ctile[:], keptf[:], oh_all[:, tc * 8:(tc + 1) * 8], op=ALU.mult), 'VE')
            ve = p.op('vector', lambda e: e.reduce_sum(r128a[:], ctile[:], axis=AX), 'VE')
            ve = p.op('vector', lambda e, tc=tc: e.tensor_scalar(
                ctile[:], ctile[:], rp_all[:, tc:tc + 1], None, op0=ALU.mult), 'VE')
            ve = p.op('vector', lambda e: e.tensor_scalar(r128a[:], r128a[:], -1.0, 1.0,
                                                          op0=ALU.mult, op1=ALU.add), 'VE')
            ve = p.op('vector', lambda e, tc=tc: e.tensor_tensor(
                r128b[:], r128a[:], rp_all[:, tc:tc + 1], op=ALU.mult), 'VE')
            p.wait('tensor', 'VE', ve)
            te = p.op('tensor', lambda e: e.transpose(bank[6][0:8, 0:128], ctile[:], ident[:]), 'TE')
            p.wait('scalar', 'TE', te)
            se = p.op('scalar', lambda e: e.copy(ct8[:], bank[6][0:8, 0:128]), 'SE')
            p.wait('vector', 'SE', se)
            ve = p.op('vector', lambda e: e.tensor_scalar(yacc[:], htok[:], r128b[:], None,
                                                          op0=ALU.mult), 'VE')
            p.wait('tensor', 'VE', ve)
            te = p.op('tensor', lambda e: e.matmul(bank[5][:], ct8[:], fbe_sb[:, 0:512],
                                                   start=True, stop=True), 'TE')
            te = p.op('tensor', lambda e: e.matmul(bank[6][:, 0:512], ct8[:], fbe_sb[:, 512:1024],
                                                   start=True, stop=True), 'TE')
            p.wait('vector', 'TE', te)
            ve = p.op('vector', lambda e: e.tensor_tensor(yacc[:, 0:512], yacc[:, 0:512],
                                                          bank[5][:], op=ALU.add), 'VE')
            ve = p.op('vector', lambda e: e.tensor_tensor(yacc[:, 512:1024], yacc[:, 512:1024],
                                                          bank[6][:, 0:512], op=ALU.add), 'VE')
            p.wait('tensor', 'VE', ve)
            for e_i in range(NE):
                wsrc = weT[e_i // 2]
                off = (e_i % 2) * 8192
                bk0 = (e_i % 2) * 2
                for half in range(2):
                    for dc in range(8):
                        te = p.op('tensor', lambda e, dc=dc, half=half, off=off, wsrc=wsrc, bk0=bk0: e.matmul(
                            bank[bk0 + half][:], hbf_t[:, dc * 128:(dc + 1) * 128],
                            wsrc[:, off + dc * 1024 + half * 512: off + dc * 1024 + (half + 1) * 512],
                            start=(dc == 0), stop=(dc == 7)), 'TE')
                    p.wait('vector', 'TE', te)
                    ve = p.op('vector', lambda e, e_i=e_i, half=half, bk0=bk0: e.scalar_tensor_tensor(
                        yacc[:, half * 512:(half + 1) * 512], bank[bk0 + half][:],
                        ctile[:, e_i:e_i + 1], yacc[:, half * 512:(half + 1) * 512],
                        op0=ALU.mult, op1=ALU.add), 'VE')
                    p.wait('tensor', 'VE', ve)
            p.wait('vector', 'f3u', ld_in)
            ve = p.op('vector', lambda e: e.tensor_tensor(yacc[:], yacc[:], u2t_t[:], op=ALU.add), 'VE')
            ve = ln_tok(yacc, ua3, LNE_T, tm3, gamma=gam_bc, beta=bet_bc, resid=xm_t)
            p.wait('sync', 'VE', ve)
            v = p.dma('sync', out[tc * 128:(tc + 1) * 128, :], ua3[:], sem='fo')
            p.barrier(('sync', 'gpsimd', 'tensor', 'scalar', 'vector'))
    p.wait('sync', 'fo', p.cur('fo'))
    return finish(nc, ges, p)


def finish(nc, ges, p):
    def replay(e, which):
        for eng, name, a, k, semh, n in p.ops:
            if eng != which:
                continue
            r = getattr(e, name)(*a, **k)
            if semh is not None:
                r.then_inc(semh, n)

    block = ges.enter_context(nc.Block())

    @block.sync
    def _(e):
        replay(e, 'sync')

    @block.gpsimd
    def _(e):
        replay(e, 'gpsimd')

    @block.scalar
    def _(e):
        replay(e, 'scalar')

    @block.vector
    def _(e):
        replay(e, 'vector')

    @block.tensor
    def _(e):
        replay(e, 'tensor')

    ges.close()
    return nc


def _consts():
    ident = np.eye(128, dtype=np.float32)
    sl = np.triu(np.ones((128, 128), np.float32), 1)
    masks = np.zeros((128, 4, 512), np.float32)
    tp = np.arange(128)[:, None]
    qf = np.arange(512)[None, :]
    for c in range(4):
        masks[:, c, :] = (tp <= qf - 128 * c).astype(np.float32)
    io8 = np.broadcast_to(np.arange(8, dtype=np.float32), (128, 8)).copy()
    e127 = np.zeros((128, 1), np.float32); e127[127, 0] = 1.0
    return (ident, sl, masks.reshape(128, 2048).astype(ml_dtypes.bfloat16), io8,
            (8.0 - io8).astype(np.float32), e127)


_CACHE = {}


def kernel(L0_attn_x, L0_attn_Wq, L0_attn_bq, L0_attn_Wk, L0_attn_bk, L0_attn_Wv,
           L0_attn_bv, L0_attn_Wo, L0_attn_bo, L0_attn_ln1_s, L0_attn_ln1_b,
           L0_attn_W1, L0_attn_b1, L0_attn_W2, L0_attn_b2, L0_attn_ln2_s,
           L0_attn_ln2_b, L0_ffn_Wq, L0_ffn_bq, L0_ffn_Wk, L0_ffn_bk, L0_ffn_Wv,
           L0_ffn_bv, L0_ffn_We, L0_ffn_be, L0_ffn_Wsw, L0_ffn_bsw, L0_ffn_gamma,
           L0_ffn_beta, _stage=None):
    x = np.ascontiguousarray(np.asarray(L0_attn_x, dtype=np.float32))
    ident, sl, masks, io8, rm8, e127c = _consts()
    f32 = lambda a: np.ascontiguousarray(np.asarray(a, dtype=np.float32))
    common = dict(
        aWq=f32(L0_attn_Wq), aWk=f32(L0_attn_Wk), aWv=f32(L0_attn_Wv), aWo=f32(L0_attn_Wo),
        aW1=f32(L0_attn_W1), aW2=f32(L0_attn_W2),
        abq=f32(L0_attn_bq), abk=f32(L0_attn_bk), abv=f32(L0_attn_bv), abo=f32(L0_attn_bo),
        ab1=f32(L0_attn_b1), ab2=f32(L0_attn_b2),
        aln1s=f32(L0_attn_ln1_s), aln1b=f32(L0_attn_ln1_b),
        aln2s=f32(L0_attn_ln2_s), aln2b=f32(L0_attn_ln2_b),
        fWq=f32(L0_ffn_Wq).reshape(1024, D), fWk=f32(L0_ffn_Wk).reshape(1024, D),
        fWv=f32(L0_ffn_Wv).reshape(1024, D),
        fbq=f32(L0_ffn_bq).reshape(-1), fbk=f32(L0_ffn_bk).reshape(-1),
        fbv=f32(L0_ffn_bv).reshape(-1),
        fWe=f32(L0_ffn_We), fbe=f32(L0_ffn_be), fWsw=f32(L0_ffn_Wsw),
        fbsw=f32(L0_ffn_bsw), gam=f32(L0_ffn_gamma), bet=f32(L0_ffn_beta),
        c_id=ident, c_sl=sl, c_mask=masks, c_io8=io8, c_rm8=rm8, c_e127=e127c,
    )
    if _stage not in _CACHE:
        _CACHE[_stage] = build_program(stage=_stage)
    nc = _CACHE[_stage]
    in_maps = []
    for c in range(NCORES):
        m = dict(common)
        m["xin"] = np.ascontiguousarray(x[2 * c:2 * c + 2].reshape(T, D))
        rmk = np.zeros((8, 1), np.float32)
        rmk[:c, 0] = 1.0
        m["c_rmask"] = rmk
        in_maps.append(m)
    res = run_bass_kernel_spmd(nc, in_maps, core_ids=list(range(NCORES)))
    return np.concatenate([np.asarray(r["out"]).reshape(2, S, D) for r in res.results], axis=0)


# version 07
# speedup vs baseline: 1.5109x; 1.0506x over previous
import numpy as np
import ml_dtypes
from contextlib import ExitStack
import concourse.bass as bass
import concourse.mybir as mybir
from concourse.bass_utils import run_bass_kernel_spmd
from concourse.alu_op_type import AluOpType as ALU

F32 = mybir.dt.float32
BF16 = mybir.dt.bfloat16
AF = mybir.ActivationFunctionType
AX = mybir.AxisListType.X

NCORES = 8
T = 4096
D = 1024
FF = 4096
NBLK = 4
S = 2048
NH = 16
NE = 8
CAP = 4915
LNE_A = 1e-6
LNE_T = 1e-5
ENGS = ('sync', 'gpsimd', 'scalar', 'vector', 'tensor')


class _Rec:
    def __getattr__(self, name):
        def call(*a, **k):
            return (name, a, k)
        return call


_REC = _Rec()


class P:
    def __init__(self, nc, es):
        self.nc = nc
        self.es = es
        self.ops = []
        self.ctr = {}
        self.sems = {}
        self.dma_sems = ['LD', 'GD']
        for s in ('LD', 'GD', 'TE', 'VE', 'SE', 'GE', 'BAR', 'CC'):
            self.sems[s] = es.enter_context(nc.semaphore(s))
            self.ctr[s] = 0

    def dmasem(self, name):
        if name not in self.sems:
            self.sems[name] = self.es.enter_context(self.nc.semaphore(name))
            self.ctr[name] = 0
            self.dma_sems.append(name)
        return name

    def emit(self, eng, fn):
        name, a, k = fn(_REC)
        self.ops.append((eng, name, a, k, None, 0))

    def cur(self, sem):
        return self.ctr[sem]

    def wait(self, eng, sem, val):
        h = self.sems[sem]
        self.ops.append((eng, 'wait_ge', (h, val), {}, None, 0))

    def op(self, eng, fn, sem=None, n=1):
        name, a, k = fn(_REC)
        if eng in ('vector', 'scalar', 'gpsimd') and sem in ('VE', 'SE', 'GE'):
            prev = getattr(self, '_last_' + eng, None)
            if prev is not None:
                self.ops.append((eng, 'wait_ge', (self.sems[prev[0]], prev[1]), {}, None, 0))
        if sem is None:
            self.ops.append((eng, name, a, k, None, 0))
            return None
        self.ctr[sem] += n
        self.ops.append((eng, name, a, k, self.sems[sem], n))
        if eng in ('vector', 'scalar', 'gpsimd') and sem in ('VE', 'SE', 'GE'):
            setattr(self, '_last_' + eng, (sem, self.ctr[sem]))
        return self.ctr[sem]

    def dma(self, eng, out, in_, sem=None, **kw):
        if sem is None:
            sem = {'sync': 'LD', 'gpsimd': 'GD', 'scalar': 'LD'}[eng]
        else:
            sem = self.dmasem(sem)
        return self.op(eng, lambda e: e.dma_start(out=out, in_=in_, **kw), sem, 16)

    def use_bank(self, bi):
        w = getattr(self, 'bankuse', {}).get(bi)
        if w is not None:
            self.wait('tensor', w[0], w[1])

    def set_bank(self, bi, sem):
        if not hasattr(self, 'bankuse'):
            self.bankuse = {}
        self.bankuse[bi] = (sem, self.ctr[sem])

    def barrier(self, engines=ENGS):
        if 'tensor' in engines:
            self.wait('sync', 'TE', self.ctr['TE'])
        if 'vector' in engines:
            self.wait('sync', 'VE', self.ctr['VE'])
        if 'scalar' in engines:
            self.wait('sync', 'SE', self.ctr['SE'])
        if 'gpsimd' in engines:
            self.wait('sync', 'GE', self.ctr['GE'])
        for s in self.dma_sems:
            self.wait('sync', s, self.ctr[s])
        self.ctr['BAR'] += 1
        self.ops.append(('sync', 'nop', (), {}, self.sems['BAR'], 1))
        tgt = self.ctr['BAR']
        for eng in engines:
            if eng != 'sync':
                self.wait(eng, 'BAR', tgt)


def build_program(stage=None):
    nc = bass.Bass("TRN2", target_bir_lowering=False)
    ges = ExitStack()

    def din(name, shape, dt=F32):
        return nc.dram_tensor(name, shape, dt, kind="ExternalInput").ap()
    xin = din("xin", [T, D])
    aWq = din("aWq", [NBLK, D, D]); aWk = din("aWk", [NBLK, D, D])
    aWv = din("aWv", [NBLK, D, D]); aWo = din("aWo", [NBLK, D, D])
    aW1 = din("aW1", [NBLK, D, FF]); aW2 = din("aW2", [NBLK, FF, D])
    abq = din("abq", [NBLK, D]); abk = din("abk", [NBLK, D]); abv = din("abv", [NBLK, D])
    abo = din("abo", [NBLK, D]); ab1 = din("ab1", [NBLK, FF]); ab2 = din("ab2", [NBLK, D])
    aln1s = din("aln1s", [NBLK, D]); aln1b = din("aln1b", [NBLK, D])
    aln2s = din("aln2s", [NBLK, D]); aln2b = din("aln2b", [NBLK, D])
    fWq = din("fWq", [1024, D]); fWk = din("fWk", [1024, D]); fWv = din("fWv", [1024, D])
    fbq = din("fbq", [1024]); fbk = din("fbk", [1024]); fbv = din("fbv", [1024])
    fWe = din("fWe", [NE, D, D]); fbe = din("fbe", [NE, D])
    fWsw = din("fWsw", [NE, D]); fbsw = din("fbsw", [NE])
    gam = din("gam", [D]); bet = din("bet", [D])
    c_id = din("c_id", [128, 128]); c_sl = din("c_sl", [128, 128])
    c_mask = din("c_mask", [128, 4 * 512], BF16); c_io8 = din("c_io8", [128, 8])
    c_rm8 = din("c_rm8", [128, 8]); c_rmask = din("c_rmask", [8, 1])
    c_e127 = din("c_e127", [128, 1])
    out = nc.dram_tensor("out", [T, D], F32, kind="ExternalOutput").ap()

    uA = nc.dram_tensor("uA", [D, T], F32).ap()
    uB = nc.dram_tensor("uB", [D, T], F32).ap()
    x1T = nc.dram_tensor("x1T", [D, T], F32).ap()
    xmid = nc.dram_tensor("xmid", [T, D], F32).ap()
    u2tok = nc.dram_tensor("u2tok", [T, D], F32).ap()
    u2T = nc.dram_tensor("u2T", [D, T], F32).ap()
    hidT = nc.dram_tensor("hidT", [D, T], BF16).ap()
    agin = nc.dram_tensor("agin", [1, 8], F32)
    agout = nc.dram_tensor("agout", [8, 8], F32)

    p = P(nc, ges)
    for _s in ('ubA', 'ubB', 'ufA', 'ufB', 'stA', 'stB', 'xbA', 'xbB', 'ufl', 'ufs',
               'p0x', 'p0s', 'pmx', 'pmg', 'pms', 'fw', 'fu', 'fz', 'fr', 'fh',
               'f3h', 'f3u', 'fo'):
        p.dmasem(_s)
    gsb = lambda n, s, d=F32: ges.enter_context(nc.sbuf_tensor(n, s, d))

    ident = gsb("ident", [128, 128])
    identb = gsb("identb", [128, 128], BF16)
    sltri = gsb("sltri", [128, 128])
    allonesb = gsb("allonesb", [128, 512], BF16)
    io8 = gsb("io8", [128, 8]); rm8 = gsb("rm8", [128, 8]); rmask = gsb("rmask", [8, 1])
    ones128 = gsb("ones128", [128, 1]); onesrow = gsb("onesrow", [1, 128])
    e127 = gsb("e127", [128, 1])
    vq = gsb("vq", [128, 32]); vk = gsb("vk", [128, 32]); vv = gsb("vv", [128, 32])
    vo = gsb("vo", [128, 32]); v1 = gsb("v1", [128, 128]); v2 = gsb("v2", [128, 32])
    vnq = gsb("vnq", [128, 32]); vnk = gsb("vnk", [128, 32])
    vl1s = gsb("vl1s", [128, 32]); vl1b = gsb("vl1b", [128, 32])
    vl2s = gsb("vl2s", [128, 32]); vl2b = gsb("vl2b", [128, 32])
    fvq = gsb("fvq", [128, 8]); fvk = gsb("fvk", [128, 8])
    fbsw_row = gsb("fbsw_row", [1, 8]); wswT = gsb("wswT", [128, 64])
    scr = [gsb(f"scr{i}", [128, 512]) for i in range(3)]
    abrow = gsb("abrow", [1, 1024]); strow = gsb("strow", [1, 1024])
    r128a = gsb("r128a", [128, 1]); r128b = gsb("r128b", [128, 1])
    r128c = gsb("r128c", [128, 1]); r128d = gsb("r128d", [128, 1])
    kvacc = gsb("kvacc", [128, 16]); ksacc = gsb("ksacc", [128, 16])
    oh_all = gsb("oh_all", [128, 256]); rank_all = gsb("rank_all", [128, 256])
    rp_all = gsb("rp_all", [128, 32])
    carry = gsb("carry", [1, 8]); offs_row = gsb("offs_row", [1, 8])
    cnt8 = gsb("cnt8", [8, 8]); offcol = gsb("offcol", [8, 1])

    bank = [ges.enter_context(nc.psum_tensor(f"bank{i}", [128, 512], F32)) for i in range(7)]
    pbf = ges.enter_context(nc.psum_tensor("pbf", [128, 1024], BF16))

    p.dma('sync', ident[:], c_id)
    p.dma('sync', sltri[:], c_sl)
    p.dma('sync', io8[:], c_io8)
    p.dma('sync', rm8[:], c_rm8)
    p.dma('sync', rmask[:], c_rmask)
    p.dma('sync', e127[:], c_e127)
    p.dma('sync', fbsw_row[:], fbsw.rearrange("(a b) -> a b", a=1))
    v = p.dma('gpsimd', identb[:], c_id)
    p.op('gpsimd', lambda e: e.memset(ones128[:], 1.0), 'GE')
    p.op('gpsimd', lambda e: e.memset(onesrow[:], 1.0), 'GE')
    p.op('gpsimd', lambda e: e.memset(carry[:], 0.0), 'GE')
    p.op('gpsimd', lambda e: e.memset(allonesb[:], 1.0), 'GE')

    def load_vec(dst, col0, row, n):
        v = p.dma('sync', scr[2][0:n, 0:128], row.rearrange("(a b) -> a b", b=128))
        p.wait('tensor', 'LD', v)
        te = p.op('tensor', lambda e, n=n: e.transpose(bank[6][:, 0:n], scr[2][0:n, 0:128], ident[0:n, 0:n]), 'TE')
        p.wait('scalar', 'TE', te)
        se = p.op('scalar', lambda e, dst=dst, col0=col0, n=n: e.copy(dst[:, col0:col0 + n],
                                                                      bank[6][:, 0:n]), 'SE')
        p.wait('tensor', 'SE', se)
        p.wait('sync', 'SE', se)

    for i in range(NBLK):
        load_vec(vq, i * 8, abq[i, :], 8)
        load_vec(vk, i * 8, abk[i, :], 8)
        load_vec(vv, i * 8, abv[i, :], 8)
        load_vec(vo, i * 8, abo[i, :], 8)
        load_vec(v1, i * 32, ab1[i, :], 32)
        load_vec(v2, i * 8, ab2[i, :], 8)
        load_vec(vl1s, i * 8, aln1s[i, :], 8)
        load_vec(vl1b, i * 8, aln1b[i, :], 8)
        load_vec(vl2s, i * 8, aln2s[i, :], 8)
        load_vec(vl2b, i * 8, aln2b[i, :], 8)
    load_vec(fvq, 0, fbq, 8)
    load_vec(fvk, 0, fbk, 8)
    v = p.dma('sync', scr[2][0:8, 0:512], fWsw[:, 0:512])
    v = p.dma('sync', scr[1][0:8, 0:512], fWsw[:, 512:1024])
    p.wait('tensor', 'LD', v)
    for dc in range(8):
        src = scr[2] if dc < 4 else scr[1]
        cof = dc * 128 if dc < 4 else (dc - 4) * 128
        te = p.op('tensor', lambda e, dc=dc, src=src, cof=cof: e.transpose(
            bank[6][:, dc * 8:(dc + 1) * 8], src[0:8, cof:cof + 128], ident[0:8, 0:8]), 'TE')
    p.wait('scalar', 'TE', p.cur('TE'))
    se = p.op('scalar', lambda e: e.copy(wswT[:], bank[6][:, 0:64]), 'SE')
    p.wait('tensor', 'SE', se)
    p.wait('vector', 'SE', p.cur('SE'))
    ve = p.op('vector', lambda e: e.tensor_scalar(vnq[:], vq[:], -1.0, None, op0=ALU.mult), 'VE')
    ve = p.op('vector', lambda e: e.tensor_scalar(vnk[:], vk[:], -1.0, None, op0=ALU.mult), 'VE')
    p.barrier()

    def ln_tok(xt, ot, eps, tmbuf, gamma=None, beta=None, resid=None):
        ve = p.op('vector', lambda e: e.reduce_sum(r128a[:], xt[:], axis=AX), 'VE')
        ve = p.op('vector', lambda e: e.tensor_scalar(r128a[:], r128a[:], 1.0 / D, None, op0=ALU.mult), 'VE')
        ve = p.op('vector', lambda e: e.tensor_scalar(tmbuf[:, 0:1024], xt[:], r128a[:], None,
                                                      op0=ALU.subtract), 'VE')
        p.wait('scalar', 'VE', ve)
        se = p.op('scalar', lambda e: e.activation(tmbuf[:, 1024:2048], tmbuf[:, 0:1024], AF.Square,
                                                   accum_out=r128b[:]), 'SE')
        p.wait('vector', 'SE', se)
        ve = p.op('vector', lambda e, eps=eps: e.tensor_scalar(r128b[:], r128b[:], 1.0 / D, eps,
                                                               op0=ALU.mult, op1=ALU.add), 'VE')
        p.wait('scalar', 'VE', ve)
        se = p.op('scalar', lambda e: e.activation(r128c[:], r128b[:], AF.Ln), 'SE')
        se = p.op('scalar', lambda e: e.activation(r128d[:], r128c[:], AF.Exp, scale=-0.5), 'SE')
        p.wait('vector', 'SE', se)
        if gamma is None:
            ve = p.op('vector', lambda e: e.tensor_scalar(ot[:], tmbuf[:, 0:1024], r128d[:], None,
                                                          op0=ALU.mult), 'VE')
        else:
            ve = p.op('vector', lambda e: e.scalar_tensor_tensor(ot[:], tmbuf[:, 0:1024], r128d[:],
                                                                 gamma[:], op0=ALU.mult, op1=ALU.mult), 'VE')
            ve = p.op('vector', lambda e: e.tensor_tensor(ot[:], ot[:], beta[:], op=ALU.add), 'VE')
            if resid is not None:
                ve = p.op('vector', lambda e: e.tensor_tensor(ot[:], ot[:], resid[:], op=ALU.add), 'VE')
        return ve

    def ln_feat(tin, Tc, sv, bv, base, eps, tout, sqb):
        p.wait('tensor', 'VE', p.cur('VE'))
        p.wait('scalar', 'VE', p.cur('VE'))
        for dc in range(8):
            se = p.op('scalar', lambda e, dc=dc: e.activation(sqb[:, dc * Tc:(dc + 1) * Tc],
                                                              tin(dc), AF.Square), 'SE')
        se_sq = se
        p.use_bank(5)
        for dc in range(8):
            te = p.op('tensor', lambda e, dc=dc: e.matmul(bank[5][0:1, 0:Tc], ones128[:], tin(dc),
                                                          start=(dc == 0), stop=(dc == 7)), 'TE')
        p.wait('tensor', 'SE', se_sq)
        p.use_bank(6)
        for dc in range(8):
            te = p.op('tensor', lambda e, dc=dc: e.matmul(bank[6][0:1, 0:Tc], ones128[:],
                                                          sqb[:, dc * Tc:(dc + 1) * Tc],
                                                          start=(dc == 0), stop=(dc == 7)), 'TE')
        p.wait('vector', 'TE', te)
        ve = p.op('vector', lambda e, Tc=Tc: e.tensor_scalar(strow[0:1, 0:Tc], bank[5][0:1, 0:Tc],
                                                             1.0 / D, None, op0=ALU.mult), 'VE')
        ve = p.op('vector', lambda e, Tc=Tc: e.tensor_tensor(strow[0:1, 512:512 + Tc], strow[0:1, 0:Tc],
                                                             strow[0:1, 0:Tc], op=ALU.mult), 'VE')
        ve = p.op('vector', lambda e, Tc=Tc: e.scalar_tensor_tensor(strow[0:1, 512:512 + Tc],
                                                                    bank[6][0:1, 0:Tc], 1.0 / D,
                                                                    strow[0:1, 512:512 + Tc],
                                                                    op0=ALU.mult, op1=ALU.subtract), 'VE')
        p.set_bank(5, 'VE')
        p.set_bank(6, 'VE')
        ve = p.op('vector', lambda e: e.tensor_scalar(strow[0:1, 512:512 + Tc],
                                                      strow[0:1, 512:512 + Tc], eps, None,
                                                      op0=ALU.add), 'VE')
        p.wait('scalar', 'VE', ve)
        se = p.op('scalar', lambda e: e.activation(strow[0:1, 512:512 + Tc],
                                                   strow[0:1, 512:512 + Tc], AF.Ln), 'SE')
        se = p.op('scalar', lambda e, Tc=Tc: e.activation(abrow[0:1, 0:Tc], strow[0:1, 512:512 + Tc],
                                                          AF.Exp, scale=-0.5), 'SE')
        p.wait('vector', 'SE', se)
        ve = p.op('vector', lambda e, Tc=Tc: e.tensor_tensor(abrow[0:1, 512:512 + Tc], strow[0:1, 0:Tc],
                                                             abrow[0:1, 0:Tc], op=ALU.mult), 'VE')
        p.wait('tensor', 'VE', ve)
        p.use_bank(3)
        te = p.op('tensor', lambda e, Tc=Tc: e.matmul(bank[3][:, 0:Tc], onesrow[0:1, :],
                                                      abrow[0:1, 0:Tc], start=True, stop=True), 'TE')
        p.use_bank(4)
        te = p.op('tensor', lambda e, Tc=Tc: e.matmul(bank[4][:, 0:Tc], onesrow[0:1, :],
                                                      abrow[0:1, 512:512 + Tc], start=True, stop=True), 'TE')
        p.wait('vector', 'TE', te)
        for dc in range(8):
            ve = p.op('vector', lambda e, dc=dc: e.tensor_tensor(sqb[:, dc * Tc:(dc + 1) * Tc],
                                                                 tin(dc), bank[3][:, 0:Tc], op=ALU.mult), 'VE')
            ve = p.op('vector', lambda e, dc=dc: e.tensor_tensor(sqb[:, dc * Tc:(dc + 1) * Tc],
                                                                 sqb[:, dc * Tc:(dc + 1) * Tc],
                                                                 bank[4][:, 0:Tc], op=ALU.subtract), 'VE')
            p.wait('scalar', 'VE', ve)
            se = p.op('scalar', lambda e, dc=dc: e.activation(tout(dc), sqb[:, dc * Tc:(dc + 1) * Tc],
                                                              AF.Identity,
                                                              bias=bv[:, base + dc:base + dc + 1],
                                                              scale=sv[:, base + dc:base + dc + 1]), 'SE')
        p.set_bank(3, 'VE')
        p.set_bank(4, 'VE')
        p.wait('vector', 'SE', se)
        return se

    with ExitStack() as ps0:
        sb0 = lambda n, s, d=F32: ps0.enter_context(nc.sbuf_tensor(n, s, d))
        xa = sb0("xa", [128, 1024]); ua = sb0("ua", [128, 1024])
        tm = sb0("tm", [128, 2048]); ustg = sb0("ustg", [128, 4096])
        for tg in range(8):
            for j in range(4):
                tc = tg * 4 + j
                v = p.dma('sync', xa[:], xin[tc * 128:(tc + 1) * 128, :], sem='p0x')
                p.wait('vector', 'p0x', v)
                ve = ln_tok(xa, ua, LNE_T, tm)
                p.wait('tensor', 'VE', ve)
                for dc in range(8):
                    bk = dc % 2
                    te = p.op('tensor', lambda e, dc=dc, bk=bk: e.transpose(
                        bank[bk][:, 0:128], ua[:, dc * 128:(dc + 1) * 128], ident[:]), 'TE')
                    p.wait('scalar', 'TE', te)
                    se = p.op('scalar', lambda e, dc=dc, j=j, bk=bk: e.copy(
                        ustg[:, dc * 512 + j * 128: dc * 512 + (j + 1) * 128], bank[bk][:, 0:128]), 'SE')
                    p.wait('tensor', 'SE', se)
                p.barrier(('sync', 'vector', 'scalar', 'tensor'))
            for dc in range(8):
                v = p.dma('sync', uA[dc * 128:(dc + 1) * 128, tg * 512:(tg + 1) * 512],
                          ustg[:, dc * 512:(dc + 1) * 512], sem='p0s')
            p.wait('scalar', 'p0s', v)
            p.barrier(('sync', 'scalar'))
        p.barrier()

    if stage == "p0":
        ov = out.rearrange("t d -> (t d)").rearrange("(d t) -> d t", d=D)
        for dc in range(8):
            p.dma('sync', ov[dc * 128:(dc + 1) * 128, :], uA[dc * 128:(dc + 1) * 128, :])
        p.wait('sync', 'LD', p.cur('LD'))
        return finish(nc, ges, p)

    ucur, unxt = uA, uB
    with ExitStack() as psa:
        for blk in range(NBLK):
            b8 = blk * 8
            b32 = blk * 32
            with ExitStack() as ps1:
                sb1 = lambda n, s, d=F32: ps1.enter_context(nc.sbuf_tensor(n, s, d))
                wA = sb1(f"wA{blk}", [128, 8192], BF16)
                wB = sb1(f"wB{blk}", [128, 8192], BF16)
                ub2 = [sb1(f"ubf{blk}_{i}", [128, 4096], BF16) for i in range(2)]
                vbf = sb1(f"vbf{blk}", [128, 4096], BF16)
                uf2 = [sb1(f"ufp{blk}_{i}", [128, 4096]) for i in range(2)]
                t1b = sb1(f"t1b{blk}", [128, 4096])
                sqa = sb1(f"sqa{blk}", [128, 4096])
                pscr = sb1(f"pscr{blk}", [128, 3072])

                def scrp(par, i):
                    return pscr[:, (par * 3 + i) * 512:(par * 3 + i + 1) * 512]

                for c in range(8):
                    p.dma('gpsimd', wA[:, c * 1024:(c + 1) * 1024], aWk[blk, c * 128:(c + 1) * 128, :])
                    gv = p.dma('gpsimd', wB[:, c * 1024:(c + 1) * 1024], aWv[blk, c * 128:(c + 1) * 128, :])
                p.op('gpsimd', lambda e: e.memset(kvacc[:], 0.0), 'GE')
                ge = p.op('gpsimd', lambda e: e.memset(ksacc[:], 0.0), 'GE')
                p.wait('tensor', 'GD', gv)
                p.wait('vector', 'GE', ge)

                def load_u(dst, tc, eng='gpsimd', sem=None):
                    for dc in range(8):
                        gv = p.dma(eng, dst[:, dc * 512:(dc + 1) * 512],
                                   ucur[dc * 128:(dc + 1) * 128, tc * 512:(tc + 1) * 512],
                                   sem=sem)
                    return gv

                ubsem = ['ubA', 'ubB']
                ub_ld = [None, None]
                ub_te = [0, 0]
                ub_ld[0] = load_u(ub2[0], 0, sem=ubsem[0])
                ve_par = [0, 0]
                for tc in range(8):
                    bi = tc // 4
                    par_u = tc % 2
                    if tc + 1 < 8:
                        p.wait('gpsimd', 'TE', ub_te[1 - par_u])
                        ub_ld[1 - par_u] = load_u(ub2[1 - par_u], tc + 1, sem=ubsem[1 - par_u])
                    ubf = ub2[par_u]
                    p.wait('tensor', ubsem[par_u], ub_ld[par_u])
                    for oc in range(8):
                        par = oc % 2
                        bk_k = par * 2
                        bk_v = bk_k + 1
                        p.use_bank(bk_k)
                        for dc in range(8):
                            te = p.op('tensor', lambda e, dc=dc, oc=oc, bk_k=bk_k, ubf=ubf: e.matmul(
                                bank[bk_k][:], wA[:, dc * 1024 + oc * 128: dc * 1024 + (oc + 1) * 128],
                                ubf[:, dc * 512:(dc + 1) * 512], start=(dc == 0), stop=(dc == 7)), 'TE')
                        te_k = te
                        p.use_bank(bk_v)
                        for dc in range(8):
                            te = p.op('tensor', lambda e, dc=dc, oc=oc, bk_v=bk_v, ubf=ubf: e.matmul(
                                bank[bk_v][:], wB[:, dc * 1024 + oc * 128: dc * 1024 + (oc + 1) * 128],
                                ubf[:, dc * 512:(dc + 1) * 512], start=(dc == 0), stop=(dc == 7)), 'TE')
                        te_v = te
                        ub_te[par_u] = te_v
                        p.wait('scalar', 'VE', ve_par[par])
                        p.wait('scalar', 'TE', te_k)
                        se = p.op('scalar', lambda e, oc=oc, bk_k=bk_k, par=par: e.activation(
                            scrp(par, 0), bank[bk_k][:], AF.Relu, bias=vk[:, b8 + oc:b8 + oc + 1]), 'SE')
                        se = p.op('scalar', lambda e, oc=oc, bk_k=bk_k, par=par: e.activation(
                            scrp(par, 1), bank[bk_k][:], AF.Relu, bias=vnk[:, b8 + oc:b8 + oc + 1],
                            scale=-1.0), 'SE')
                        p.set_bank(bk_k, 'SE')
                        se = p.op('scalar', lambda e, par=par: e.activation(scrp(par, 1), scrp(par, 1),
                                                                            AF.Exp, scale=-1.0), 'SE')
                        p.wait('scalar', 'TE', te_v)
                        se = p.op('scalar', lambda e, oc=oc, bk_v=bk_v, par=par: e.activation(
                            scrp(par, 2), bank[bk_v][:], AF.Identity, bias=vv[:, b8 + oc:b8 + oc + 1]), 'SE')
                        p.set_bank(bk_v, 'SE')
                        p.wait('vector', 'SE', se)
                        ve = p.op('vector', lambda e, par=par: e.tensor_tensor(
                            scrp(par, 0), scrp(par, 0), scrp(par, 1), op=ALU.add), 'VE')
                        ve = p.op('vector', lambda e, par=par: e.reduce_sum(r128b[:], scrp(par, 0),
                                                                            axis=AX), 'VE')
                        ve = p.op('vector', lambda e, oc=oc, bi=bi: e.tensor_tensor(
                            ksacc[:, bi * 8 + oc:bi * 8 + oc + 1], ksacc[:, bi * 8 + oc:bi * 8 + oc + 1],
                            r128b[:], op=ALU.add), 'VE')
                        ve = p.op('vector', lambda e, par=par: e.tensor_tensor(
                            scrp(par, 0), scrp(par, 0), scrp(par, 2), op=ALU.mult), 'VE')
                        ve = p.op('vector', lambda e, par=par: e.reduce_sum(r128a[:], scrp(par, 0),
                                                                            axis=AX), 'VE')
                        ve = p.op('vector', lambda e, oc=oc, bi=bi: e.tensor_tensor(
                            kvacc[:, bi * 8 + oc:bi * 8 + oc + 1], kvacc[:, bi * 8 + oc:bi * 8 + oc + 1],
                            r128a[:], op=ALU.add), 'VE')
                        ve_par[par] = ve
                p.wait('gpsimd', 'TE', p.cur('TE'))
                for c in range(8):
                    p.dma('gpsimd', wA[:, c * 1024:(c + 1) * 1024], aWq[blk, c * 128:(c + 1) * 128, :])
                    gv = p.dma('gpsimd', wB[:, c * 1024:(c + 1) * 1024], aWo[blk, c * 128:(c + 1) * 128, :])
                p.wait('vector', 'VE', p.cur('VE'))
                p.wait('tensor', 'GD', gv)
                ufsem = ['ufA', 'ufB']
                stsem = ['stA', 'stB']
                ub_ld = [None, None]
                uf_ld = [None, None]
                ub_te = [0, 0]
                uf_st = [0, 0]
                uf_ve = [0, 0]
                ub_ld[0] = load_u(ub2[0], 0, sem=ubsem[0])
                uf_ld[0] = load_u(uf2[0], 0, 'sync', sem=ufsem[0])
                ve_par = [0, 0]
                ln_tail_te = 0
                for tc in range(8):
                    bi = tc // 4
                    par_u = tc % 2
                    if tc + 1 < 8:
                        p.wait('gpsimd', 'TE', ub_te[1 - par_u])
                        ub_ld[1 - par_u] = load_u(ub2[1 - par_u], tc + 1, sem=ubsem[1 - par_u])
                        p.wait('sync', stsem[1 - par_u], uf_st[1 - par_u])
                        p.wait('sync', 'VE', uf_ve[1 - par_u])
                        uf_ld[1 - par_u] = load_u(uf2[1 - par_u], tc + 1, 'sync',
                                                  sem=ufsem[1 - par_u])
                    ubf = ub2[par_u]
                    ufp = uf2[par_u]
                    p.wait('tensor', ubsem[par_u], ub_ld[par_u])
                    for oc in range(8):
                        par = oc % 2
                        bk_q = par
                        p.use_bank(bk_q)
                        for dc in range(8):
                            te = p.op('tensor', lambda e, dc=dc, oc=oc, bk_q=bk_q, ubf=ubf: e.matmul(
                                bank[bk_q][:], wA[:, dc * 1024 + oc * 128: dc * 1024 + (oc + 1) * 128],
                                ubf[:, dc * 512:(dc + 1) * 512], start=(dc == 0), stop=(dc == 7)), 'TE')
                        ub_te[par_u] = te
                        p.wait('scalar', 'VE', ve_par[par])
                        p.wait('scalar', 'TE', te)
                        se = p.op('scalar', lambda e, oc=oc, bk_q=bk_q, par=par: e.activation(
                            scrp(par, 0), bank[bk_q][:], AF.Relu, bias=vq[:, b8 + oc:b8 + oc + 1]), 'SE')
                        se = p.op('scalar', lambda e, oc=oc, bk_q=bk_q, par=par: e.activation(
                            scrp(par, 1), bank[bk_q][:], AF.Relu, bias=vnq[:, b8 + oc:b8 + oc + 1],
                            scale=-1.0), 'SE')
                        p.set_bank(bk_q, 'SE')
                        se = p.op('scalar', lambda e, par=par: e.activation(scrp(par, 1), scrp(par, 1),
                                                                            AF.Exp, scale=-1.0), 'SE')
                        p.wait('vector', 'SE', se)
                        ve = p.op('vector', lambda e, par=par: e.tensor_tensor(
                            scrp(par, 0), scrp(par, 0), scrp(par, 1), op=ALU.add), 'VE')
                        ve = p.op('vector', lambda e, oc=oc, bi=bi, par=par: e.tensor_scalar(
                            scrp(par, 1), scrp(par, 0), ksacc[:, bi * 8 + oc:bi * 8 + oc + 1], 1e-6,
                            op0=ALU.mult, op1=ALU.add), 'VE')
                        ve = p.op('vector', lambda e, par=par: e.reciprocal(scrp(par, 2),
                                                                            scrp(par, 1)), 'VE')
                        ve = p.op('vector', lambda e, oc=oc, bi=bi, par=par: e.scalar_tensor_tensor(
                            vbf[:, oc * 512:(oc + 1) * 512], scrp(par, 0),
                            kvacc[:, bi * 8 + oc:bi * 8 + oc + 1], scrp(par, 2),
                            op0=ALU.mult, op1=ALU.mult), 'VE')
                        ve_par[par] = ve
                    p.wait('tensor', 'VE', ve)
                    p.wait('vector', ufsem[par_u], uf_ld[par_u])
                    for dc in range(8):
                        par = dc % 2
                        bk_a = 2 + par
                        p.use_bank(bk_a)
                        for ac in range(8):
                            te = p.op('tensor', lambda e, ac=ac, dc=dc, bk_a=bk_a: e.matmul(
                                bank[bk_a][:], wB[:, ac * 1024 + dc * 128: ac * 1024 + (dc + 1) * 128],
                                vbf[:, ac * 512:(ac + 1) * 512], start=(ac == 0), stop=(ac == 7)), 'TE')
                        p.wait('scalar', 'VE', ve_par[par])
                        p.wait('scalar', 'TE', te)
                        se = p.op('scalar', lambda e, dc=dc, bk_a=bk_a, par=par: e.activation(
                            scrp(par, 0), bank[bk_a][:], AF.Gelu_apprx_tanh,
                            bias=vo[:, b8 + dc:b8 + dc + 1]), 'SE')
                        p.set_bank(bk_a, 'SE')
                        se = p.op('scalar', lambda e, par=par: e.activation(
                            scrp(par, 1), scrp(par, 0), AF.Gelu_apprx_tanh), 'SE')
                        p.wait('vector', 'SE', se)
                        p.wait('vector', 'TE', ln_tail_te)
                        ve = p.op('vector', lambda e, dc=dc, par=par, ufp=ufp: e.tensor_tensor(
                            t1b[:, dc * 512:(dc + 1) * 512], scrp(par, 1),
                            ufp[:, dc * 512:(dc + 1) * 512], op=ALU.add), 'VE')
                        ve_par[par] = ve
                        uf_ve[par_u] = ve
                    ln_feat(lambda dc, t1b=t1b: t1b[:, dc * 512:(dc + 1) * 512], 512, vl1s, vl1b,
                            b8, LNE_A, lambda dc, ufp=ufp: ufp[:, dc * 512:(dc + 1) * 512], sqa)
                    ln_tail_te = p.cur('TE')
                    p.wait('sync', 'SE', p.cur('SE'))
                    for dc in range(8):
                        v = p.dma('sync', x1T[dc * 128:(dc + 1) * 128, tc * 512:(tc + 1) * 512],
                                  ufp[:, dc * 512:(dc + 1) * 512], sem=stsem[par_u])
                    uf_st[par_u] = v
                p.barrier()
            with ExitStack() as ps2:
                sb2 = lambda n, s, d=F32: ps2.enter_context(nc.sbuf_tensor(n, s, d))
                w1sb = sb2(f"w1sb{blk}", [128, 32768], BF16)
                w2sb = sb2(f"w2sb{blk}", [128, 32768], BF16)
                xb2 = [sb2(f"x1bf{blk}_{i}", [128, 2048], BF16) for i in range(2)]
                hbf = sb2(f"hbf{blk}", [128, 8192], BF16)
                ufp = sb2(f"ufp2{blk}", [128, 2048])
                t1b = sb2(f"t1b2{blk}", [128, 2048])
                sqa = sb2(f"sqa2{blk}", [128, 2048])
                p2scr = sb2(f"p2scr{blk}", [128, 512])
                for c in range(8):
                    gv = p.dma('gpsimd', w1sb[:, c * 4096:(c + 1) * 4096],
                               aW1[blk, c * 128:(c + 1) * 128, :])
                for c in range(32):
                    gv = p.dma('gpsimd', w2sb[:, c * 1024:(c + 1) * 1024],
                               aW2[blk, c * 128:(c + 1) * 128, :])
                p.wait('tensor', 'GD', gv)

                def load_x1(dst, tc, eng, sem=None, ncols=256):
                    for dc in range(8):
                        gv = p.dma(eng, dst[:, dc * ncols:(dc + 1) * ncols],
                                   x1T[dc * 128:(dc + 1) * 128, tc * ncols:(tc + 1) * ncols],
                                   sem=sem)
                    return gv

                xbsem = ['xbA', 'xbB']
                xb_ld = [None, None]
                xb_te = [0, 0]
                xb_ld[0] = load_x1(xb2[0], 0, 'gpsimd', sem=xbsem[0])
                uf_ld = load_x1(ufp, 0, 'sync', sem='ufl')
                uf_st = 0
                uf_ve = 0
                ve_par = [0, 0]
                ln_tail_te = 0
                hbf_te = 0
                for tc in range(16):
                    par_x = tc % 2
                    if tc + 1 < 16:
                        p.wait('gpsimd', 'TE', xb_te[1 - par_x])
                        xb_ld[1 - par_x] = load_x1(xb2[1 - par_x], tc + 1, 'gpsimd',
                                                   sem=xbsem[1 - par_x])
                    x1bf = xb2[par_x]
                    p.wait('tensor', xbsem[par_x], xb_ld[par_x])
                    for fc in range(32):
                        bk_h = fc % 2
                        p.use_bank(bk_h)
                        for dc in range(8):
                            te = p.op('tensor', lambda e, dc=dc, fc=fc, bk_h=bk_h, x1bf=x1bf: e.matmul(
                                bank[bk_h][:, 0:256],
                                w1sb[:, dc * 4096 + fc * 128: dc * 4096 + (fc + 1) * 128],
                                x1bf[:, dc * 256:(dc + 1) * 256], start=(dc == 0), stop=(dc == 7)), 'TE')
                        xb_te[par_x] = te
                        p.wait('scalar', 'TE', te)
                        if fc == 0:
                            p.wait('scalar', 'TE', hbf_te)
                        se = p.op('scalar', lambda e, fc=fc, bk_h=bk_h: e.activation(
                            hbf[:, fc * 256:(fc + 1) * 256], bank[bk_h][:, 0:256], AF.Gelu_apprx_tanh,
                            bias=v1[:, b32 + fc:b32 + fc + 1]), 'SE')
                        p.set_bank(bk_h, 'SE')
                    p.wait('tensor', 'SE', p.cur('SE'))
                    p.wait('vector', 'ufl', uf_ld)
                    for dc in range(8):
                        par = dc % 2
                        bk_f = 2 + par
                        p.use_bank(bk_f)
                        for fc in range(32):
                            te = p.op('tensor', lambda e, fc=fc, dc=dc, bk_f=bk_f: e.matmul(
                                bank[bk_f][:, 0:256],
                                w2sb[:, fc * 1024 + dc * 128: fc * 1024 + (dc + 1) * 128],
                                hbf[:, fc * 256:(fc + 1) * 256], start=(fc == 0), stop=(fc == 31)), 'TE')
                        if dc == 7:
                            hbf_te = te
                        p.wait('scalar', 'VE', ve_par[par])
                        p.wait('scalar', 'TE', te)
                        se = p.op('scalar', lambda e, dc=dc, bk_f=bk_f, par=par: e.activation(
                            p2scr[:, par * 256:(par + 1) * 256], bank[bk_f][:, 0:256],
                            AF.Gelu_apprx_tanh, bias=v2[:, b8 + dc:b8 + dc + 1]), 'SE')
                        p.set_bank(bk_f, 'SE')
                        p.wait('vector', 'SE', se)
                        p.wait('vector', 'TE', ln_tail_te)
                        ve = p.op('vector', lambda e, dc=dc, par=par: e.tensor_tensor(
                            t1b[:, dc * 256:(dc + 1) * 256], p2scr[:, par * 256:(par + 1) * 256],
                            ufp[:, dc * 256:(dc + 1) * 256], op=ALU.add), 'VE')
                        ve_par[par] = ve
                        uf_ve = ve
                    ln_feat(lambda dc, t1b=t1b: t1b[:, dc * 256:(dc + 1) * 256], 256, vl2s, vl2b,
                            b8, LNE_A, lambda dc, ufp=ufp: ufp[:, dc * 256:(dc + 1) * 256], sqa)
                    ln_tail_te = p.cur('TE')
                    p.wait('sync', 'SE', p.cur('SE'))
                    for dc in range(8):
                        v = p.dma('sync', unxt[dc * 128:(dc + 1) * 128, tc * 256:(tc + 1) * 256],
                                  ufp[:, dc * 256:(dc + 1) * 256], sem='ufs')
                    uf_st = v
                    if tc + 1 < 16:
                        p.wait('sync', 'ufs', uf_st)
                        uf_ld = load_x1(ufp, tc + 1, 'sync', sem='ufl')
                p.barrier()
            ucur, unxt = unxt, ucur
            if stage == f"blk{blk}":
                ov = out.rearrange("t d -> (t d)").rearrange("(d t) -> d t", d=D)
                for dc in range(8):
                    p.dma('sync', ov[dc * 128:(dc + 1) * 128, :], ucur[dc * 128:(dc + 1) * 128, :])
                p.wait('sync', 'LD', p.cur('LD'))
                return finish(nc, ges, p)

    with ExitStack() as psm:
        sbm = lambda n, s, d=F32: psm.enter_context(nc.sbuf_tensor(n, s, d))
        xa = sbm("mxa", [128, 1024]); xb = sbm("mxb", [128, 1024])
        ua = sbm("mua", [128, 1024]); tm = sbm("mtm", [128, 2048])
        ustg = sbm("mustg", [128, 2048])
        for tc in range(32):
            vxa = p.dma('sync', xa[:], xin[tc * 128:(tc + 1) * 128, :], sem='pmx')
            for dc in range(8):
                gv = p.dma('gpsimd', ustg[:, dc * 128:(dc + 1) * 128],
                           ucur[dc * 128:(dc + 1) * 128, tc * 128:(tc + 1) * 128], sem='pmg')
            p.wait('tensor', 'pmg', gv)
            for dc in range(8):
                bk = dc % 2
                te = p.op('tensor', lambda e, dc=dc, bk=bk: e.transpose(
                    bank[bk][:, 0:128], ustg[:, dc * 128:(dc + 1) * 128], ident[:]), 'TE')
                p.wait('scalar', 'TE', te)
                se = p.op('scalar', lambda e, dc=dc, bk=bk: e.copy(
                    xb[:, dc * 128:(dc + 1) * 128], bank[bk][:, 0:128]), 'SE')
                p.wait('tensor', 'SE', se)
            p.wait('vector', 'SE', se)
            p.wait('vector', 'pmx', vxa)
            ve = p.op('vector', lambda e: e.tensor_tensor(xa[:], xa[:], xb[:], op=ALU.add), 'VE')
            p.wait('sync', 'VE', ve)
            v = p.dma('sync', xmid[tc * 128:(tc + 1) * 128, :], xa[:], sem='pms')
            ve = ln_tok(xa, ua, LNE_T, tm)
            p.wait('sync', 'VE', ve)
            v = p.dma('sync', u2tok[tc * 128:(tc + 1) * 128, :], ua[:], sem='pms')
            p.wait('tensor', 'VE', ve)
            for dc in range(8):
                bk = 2 + dc % 2
                te = p.op('tensor', lambda e, dc=dc, bk=bk: e.transpose(
                    bank[bk][:, 0:128], ua[:, dc * 128:(dc + 1) * 128], ident[:]), 'TE')
                p.wait('scalar', 'TE', te)
                se = p.op('scalar', lambda e, dc=dc, bk=bk: e.copy(
                    ustg[:, 1024 + dc * 128: 1024 + (dc + 1) * 128] if False else ustg[:, dc * 128:(dc + 1) * 128],
                    bank[bk][:, 0:128]), 'SE')
                p.wait('tensor', 'SE', se)
            p.wait('sync', 'SE', se)
            for dc in range(8):
                v = p.dma('sync', u2T[dc * 128:(dc + 1) * 128, tc * 128:(tc + 1) * 128],
                          ustg[:, dc * 128:(dc + 1) * 128], sem='pms')
            p.wait('gpsimd', 'pms', v)
            p.barrier(('sync', 'gpsimd', 'tensor', 'scalar', 'vector'))
        p.barrier()

    if stage == "pmid":
        with ExitStack() as pst_:
            xa = pst_.enter_context(nc.sbuf_tensor("dxa", [128, 1024], F32))
            for tc in range(32):
                v = p.dma('sync', xa[:], u2tok[tc * 128:(tc + 1) * 128, :])
                p.wait('sync', 'LD', v)
                v = p.dma('sync', out[tc * 128:(tc + 1) * 128, :], xa[:])
                p.wait('sync', 'LD', v)
        return finish(nc, ges, p)

    with ExitStack() as psf:
        sbf = lambda n, s, d=F32: psf.enter_context(nc.sbuf_tensor(n, s, d))
        qTb = sbf("qTb", [128, 16384], BF16)
        kTb = sbf("kTb", [128, 16384], BF16)
        vtok = sbf("vtok", [128, 16640], BF16)
        l8 = sbf("l8", [128, 8]); e8 = sbf("e8", [128, 8])
        w8a = sbf("w8a", [128, 8]); w8b = sbf("w8b", [128, 8])

        for tt in range(16):
            for h in range(NH):
                ge = p.op('gpsimd', lambda e, tt=tt, h=h: e.memset(
                    vtok[:, tt * 1040 + h * 65 + 64: tt * 1040 + h * 65 + 65], 1.0), 'GE')
        ge_ones = p.cur('GE')

        def transpose_w(dst, src, wstg):
            for rc in range(8):
                gv = p.dma('gpsimd', wstg[:], src[rc * 128:(rc + 1) * 128, :], sem='fw')
                p.wait('tensor', 'fw', gv)
                for cc in range(8):
                    te = p.op('tensor', lambda e, cc=cc: e.transpose(
                        pbf[:, (cc % 4) * 128:(cc % 4 + 1) * 128],
                        wstg[:, cc * 128:(cc + 1) * 128], identb[:]), 'TE')
                    p.wait('scalar', 'TE', te)
                    se = p.op('scalar', lambda e, cc=cc, rc=rc, dst=dst: e.copy(
                        dst[:, cc * 1024 + rc * 128: cc * 1024 + (rc + 1) * 128],
                        pbf[:, (cc % 4) * 128:(cc % 4 + 1) * 128]), 'SE')
                    p.wait('tensor', 'SE', se)
                p.wait('gpsimd', 'SE', se)

        if True:
            for b in range(2):
              with ExitStack() as psf1:
                sbf1 = lambda n, s, d=F32: psf1.enter_context(nc.sbuf_tensor(n, s, d))
                wqT = sbf1(f"wqT{b}", [128, 8192], BF16)
                wkT = sbf1(f"wkT{b}", [128, 8192], BF16)
                wvT = sbf1(f"wvT{b}", [128, 8192], BF16)
                wstg = sbf1(f"wstg{b}", [128, 1024], BF16)
                u2bf = sbf1(f"u2bf{b}", [128, 4096], BF16)
                fbv_row = sbf1(f"fbv_row{b}", [1, 1024])
                p.dma('sync', fbv_row[:], fbv.rearrange("(a b) -> a b", a=1))
                transpose_w(wqT, fWq, wstg)
                transpose_w(wkT, fWk, wstg)
                transpose_w(wvT, fWv, wstg)
                p.barrier()
                for tc in range(4):
                    for dc in range(8):
                        gv = p.dma('gpsimd', u2bf[:, dc * 512:(dc + 1) * 512],
                                   u2T[dc * 128:(dc + 1) * 128, (b * 4 + tc) * 512:(b * 4 + tc + 1) * 512],
                                   sem='fu')
                    p.wait('tensor', 'fu', gv)
                    for pi, wT, bias_c, dstT in ((0, wqT, fvq, qTb), (1, wkT, fvk, kTb)):
                        for oc in range(8):
                            bk_x = pi * 2 + oc % 2
                            for dc in range(8):
                                te = p.op('tensor', lambda e, dc=dc, oc=oc, wT=wT, bk_x=bk_x: e.matmul(
                                    bank[bk_x][:], wT[:, dc * 1024 + oc * 128: dc * 1024 + (oc + 1) * 128],
                                    u2bf[:, dc * 512:(dc + 1) * 512], start=(dc == 0), stop=(dc == 7)), 'TE')
                            p.wait('scalar', 'TE', te)
                            se = p.op('scalar', lambda e, oc=oc, tc=tc, bias_c=bias_c, dstT=dstT, bk_x=bk_x:
                                      e.activation(dstT[:, oc * 2048 + tc * 512: oc * 2048 + (tc + 1) * 512],
                                                   bank[bk_x][:], AF.Identity, bias=bias_c[:, oc:oc + 1]), 'SE')
                            p.wait('tensor', 'SE', se)
                    p.wait('scalar', 'GE', ge_ones)
                    for ts_ in range(4):
                        tt = tc * 4 + ts_
                        for half in range(2):
                            bk_x = 4 + half
                            for dc in range(8):
                                te = p.op('tensor', lambda e, dc=dc, ts_=ts_, half=half, bk_x=bk_x: e.matmul(
                                    bank[bk_x][:], u2bf[:, dc * 512 + ts_ * 128: dc * 512 + (ts_ + 1) * 128],
                                    wvT[:, dc * 1024 + half * 512: dc * 1024 + (half + 1) * 512],
                                    start=(dc == 0), stop=False), 'TE')
                            te = p.op('tensor', lambda e, half=half, bk_x=bk_x: e.matmul(
                                bank[bk_x][:], onesrow[0:1, :], fbv_row[0:1, half * 512:(half + 1) * 512],
                                start=False, stop=True), 'TE')
                            p.wait('scalar', 'TE', te)
                            for j in range(8):
                                h = half * 8 + j
                                se = p.op('scalar', lambda e, tt=tt, h=h, j=j, bk_x=bk_x: e.copy(
                                    vtok[:, tt * 1040 + h * 65: tt * 1040 + h * 65 + 64],
                                    bank[bk_x][:, j * 64:(j + 1) * 64]), 'SE')
                            p.wait('tensor', 'SE', se)
                    p.barrier(('gpsimd', 'tensor', 'scalar', 'vector', 'sync'))
              with ExitStack() as psf2:
                sbf2 = lambda n, s, d=F32: psf2.enter_context(nc.sbuf_tensor(n, s, d))
                pbuf = sbf2(f"pbuf{b}", [128, 8192], BF16)
                hstg2 = sbf2(f"hstg{b}", [128, 8192])
                masks = sbf2(f"masks{b}", [128, 2048], BF16)
                zstg = sbf2(f"zstg{b}", [1, 1024]); z16 = sbf2(f"z16{b}", [16, 1024])
                r16 = sbf2(f"r16{b}", [16, 1024])
                rstage = sbf2(f"rstage{b}", [1, 1024])
                rbc = scr[1]
                v = p.dma('sync', masks[:], c_mask)
                p.wait('vector', 'LD', v)
                hstg_te = [0, 0]
                hstg_fh = [0, 0]
                z_ve = [0, 0]
                r_te = [0, 0]
                for qb in range(4):
                    nreal = 4 * qb + 4
                    par = qb % 2
                    hoff = par * 4096
                    zoff = par * 512
                    exp_last = 0
                    for h in range(NH):
                        hb = (64 * h) % 128
                        hc = h // 2
                        bk_pv = 2 + h % 2
                        for tt in range(nreal):
                            d_off = 512 * qb - 128 * tt
                            bk_s = tt % 2
                            p.use_bank(bk_s)
                            te = p.op('tensor', lambda e, tt=tt, qb=qb, bk_s=bk_s, hb=hb, hc=hc: e.matmul(
                                bank[bk_s][:], kTb[hb:hb + 64, hc * 2048 + tt * 128: hc * 2048 + (tt + 1) * 128],
                                qTb[hb:hb + 64, hc * 2048 + qb * 512: hc * 2048 + (qb + 1) * 512],
                                start=True, stop=True), 'TE')
                            if d_off >= 128:
                                p.wait('scalar', 'TE', te)
                                se = p.op('scalar', lambda e, tt=tt, bk_s=bk_s: e.activation(
                                    pbuf[:, tt * 512:(tt + 1) * 512], bank[bk_s][:], AF.Exp,
                                    scale=1.0 / 32.0), 'SE')
                                p.set_bank(bk_s, 'SE')
                            else:
                                c = (-d_off) // 128
                                sp = tt % 2
                                p.wait('vector', 'TE', te)
                                p.wait('vector', 'SE', exp_last)
                                ve = p.op('vector', lambda e, c=c, bk_s=bk_s, sp=sp: e.tensor_tensor(
                                    scr[sp * 2][:], bank[bk_s][:], masks[:, c * 512:(c + 1) * 512],
                                    op=ALU.mult), 'VE')
                                p.set_bank(bk_s, 'VE')
                                p.wait('scalar', 'VE', ve)
                                se = p.op('scalar', lambda e, tt=tt, sp=sp: e.activation(
                                    pbuf[:, tt * 512:(tt + 1) * 512], scr[sp * 2][:], AF.Exp,
                                    scale=1.0 / 32.0), 'SE')
                            exp_last = se
                        p.use_bank(bk_pv)
                        order = list(range(nreal, 16)) + list(range(nreal))
                        for i, tt in enumerate(order):
                            if i == (16 - nreal):
                                p.wait('tensor', 'SE', exp_last)
                            if tt < nreal:
                                rhsf = (lambda e, tt=tt: pbuf[:, tt * 512:(tt + 1) * 512])
                            else:
                                rhsf = (lambda e: allonesb[:])
                            te = p.op('tensor', lambda e, tt=tt, h=h, rhsf=rhsf, i=i, bk_pv=bk_pv,
                                      nr=nreal: e.matmul(
                                bank[bk_pv][0:65, :], vtok[:, tt * 1040 + h * 65: tt * 1040 + h * 65 + 65],
                                rhsf(e), start=(i == 0), stop=(i == 15)), 'TE')
                        p.wait('scalar', 'TE', te)
                        if h == 0:
                            p.wait('scalar', 'TE', hstg_te[par])
                            p.wait('scalar', 'fh', hstg_fh[par])
                            p.wait('scalar', 'VE', z_ve[par])
                        se = p.op('scalar', lambda e, zoff=zoff, bk_pv=bk_pv: e.copy(
                            zstg[0:1, zoff:zoff + 512], bank[bk_pv][64:65, :]), 'SE')
                        se2 = p.op('scalar', lambda e, hb=hb, hc=hc, hoff=hoff, bk_pv=bk_pv: e.copy(
                            hstg2[hb:hb + 64, hoff + hc * 512: hoff + (hc + 1) * 512],
                            bank[bk_pv][0:64, :]), 'SE')
                        p.set_bank(bk_pv, 'SE')
                        p.wait('gpsimd', 'SE', se2)
                        gv = p.dma('gpsimd', z16[h:h + 1, zoff:zoff + 512],
                                   zstg[0:1, zoff:zoff + 512], sem='fz')
                        p.wait('scalar', 'fz', gv)
                    p.wait('vector', 'fz', p.cur('fz'))
                    ve = p.op('vector', lambda e, zoff=zoff: e.reciprocal(
                        r16[:, zoff:zoff + 512], z16[:, zoff:zoff + 512]), 'VE')
                    z_ve[par] = ve
                    p.wait('gpsimd', 'VE', ve)
                    for h in range(NH):
                        hb = (64 * h) % 128
                        hc = h // 2
                        p.wait('gpsimd', 'TE', r_te[par])
                        gv = p.dma('gpsimd', rstage[0:1, zoff:zoff + 512],
                                   r16[h:h + 1, zoff:zoff + 512], sem='fr')
                        p.wait('tensor', 'fr', gv)
                        p.use_bank(4)
                        te = p.op('tensor', lambda e, hb=hb, zoff=zoff: e.matmul(
                            bank[4][hb:hb + 64, :], onesrow[0:1, 0:64],
                            rstage[0:1, zoff:zoff + 512], start=True, stop=True), 'TE')
                        r_te[par] = te
                        p.wait('scalar', 'TE', te)
                        se = p.op('scalar', lambda e, hb=hb: e.copy(rbc[hb:hb + 64, :],
                                                                    bank[4][hb:hb + 64, :]), 'SE')
                        p.set_bank(4, 'SE')
                        p.wait('vector', 'SE', se)
                        ve = p.op('vector', lambda e, hb=hb, hc=hc, hoff=hoff: e.tensor_tensor(
                            hstg2[hb:hb + 64, hoff + hc * 512: hoff + (hc + 1) * 512],
                            hstg2[hb:hb + 64, hoff + hc * 512: hoff + (hc + 1) * 512],
                            rbc[hb:hb + 64, :], op=ALU.mult), 'VE')
                        p.wait('scalar', 'VE', ve)
                    nrm_ve = p.cur('VE')
                    p.wait('gpsimd', 'VE', nrm_ve)
                    for dc in range(8):
                        gv = p.dma('gpsimd', hidT[dc * 128:(dc + 1) * 128,
                                                  (b * 4 + qb) * 512:(b * 4 + qb + 1) * 512],
                                   hstg2[:, hoff + dc * 512: hoff + (dc + 1) * 512], sem='fh')
                    hstg_fh[par] = gv
                    for ts_ in range(4):
                        gtc = b * 16 + qb * 4 + ts_
                        p.use_bank(5)
                        if ts_ == 0:
                            p.wait('tensor', 'VE', nrm_ve)
                        for dc in range(8):
                            te = p.op('tensor', lambda e, dc=dc, ts_=ts_, hoff=hoff: e.matmul(
                                bank[5][:, 0:8],
                                hstg2[:, hoff + dc * 512 + ts_ * 128: hoff + dc * 512 + (ts_ + 1) * 128],
                                wswT[:, dc * 8:(dc + 1) * 8], start=(dc == 0), stop=False), 'TE')
                        te = p.op('tensor', lambda e: e.matmul(bank[5][:, 0:8], onesrow[0:1, :],
                                                               fbsw_row[0:1, :], start=False, stop=True), 'TE')
                        hstg_te[par] = te
                        p.wait('vector', 'TE', te)
                        ve = p.op('vector', lambda e: e.tensor_copy(l8[:], bank[5][:, 0:8]), 'VE')
                        p.set_bank(5, 'VE')
                        ve = p.op('vector', lambda e: e.reduce_max(r128a[:], l8[:], axis=AX, negate=True), 'VE')
                        p.wait('scalar', 'VE', ve)
                        se = p.op('scalar', lambda e: e.activation(e8[:], l8[:], AF.Exp, bias=r128a[:]), 'SE')
                        p.wait('vector', 'SE', se)
                        ve = p.op('vector', lambda e: e.reduce_sum(r128b[:], e8[:], axis=AX), 'VE')
                        ve = p.op('vector', lambda e: e.reciprocal(r128c[:], r128b[:]), 'VE')
                        ve = p.op('vector', lambda e, gtc=gtc: e.tensor_copy(rp_all[:, gtc:gtc + 1],
                                                                             r128c[:]), 'VE')
                        ve = p.op('vector', lambda e: e.tensor_scalar(r128d[:], r128a[:], -1.0, None,
                                                                      op0=ALU.mult), 'VE')
                        ve = p.op('vector', lambda e: e.tensor_scalar(w8a[:], l8[:], r128d[:], None,
                                                                      op0=ALU.is_ge), 'VE')
                        ve = p.op('vector', lambda e: e.tensor_tensor(w8b[:], w8a[:], rm8[:], op=ALU.mult), 'VE')
                        ve = p.op('vector', lambda e: e.tensor_scalar(w8b[:], w8b[:], -1.0, 8.0,
                                                                      op0=ALU.mult, op1=ALU.add), 'VE')
                        ve = p.op('vector', lambda e: e.tensor_reduce(r128d[:], w8b[:], axis=AX,
                                                                      op=ALU.min), 'VE')
                        ve = p.op('vector', lambda e, gtc=gtc: e.tensor_scalar(
                            oh_all[:, gtc * 8:(gtc + 1) * 8], io8[:], r128d[:], None,
                            op0=ALU.is_equal), 'VE')
                        p.wait('tensor', 'VE', ve)
                        p.use_bank(6)
                        te = p.op('tensor', lambda e, gtc=gtc: e.matmul(
                            bank[6][:, 0:8], sltri[:], oh_all[:, gtc * 8:(gtc + 1) * 8],
                            start=True, stop=False), 'TE')
                        te = p.op('tensor', lambda e: e.matmul(bank[6][:, 0:8], onesrow[0:1, :],
                                                               carry[0:1, :], start=False, stop=True), 'TE')
                        p.wait('vector', 'TE', te)
                        ve = p.op('vector', lambda e, gtc=gtc: e.tensor_copy(
                            rank_all[:, gtc * 8:(gtc + 1) * 8], bank[6][:, 0:8]), 'VE')
                        p.set_bank(6, 'VE')
                        p.wait('tensor', 'VE', ve)
                        p.use_bank(6)
                        te = p.op('tensor', lambda e, gtc=gtc: e.matmul(
                            bank[6][0:1, 0:8], e127[:], rank_all[:, gtc * 8:(gtc + 1) * 8],
                            start=True, stop=False), 'TE')
                        te = p.op('tensor', lambda e, gtc=gtc: e.matmul(
                            bank[6][0:1, 0:8], e127[:], oh_all[:, gtc * 8:(gtc + 1) * 8],
                            start=False, stop=True), 'TE')
                        p.wait('vector', 'TE', te)
                        ve = p.op('vector', lambda e: e.tensor_copy(carry[0:1, :],
                                                                    bank[6][0:1, 0:8]), 'VE')
                        p.set_bank(6, 'VE')
                p.barrier()
        p.barrier()

    v = p.dma('sync', agin.ap(), carry[0:1, :])
    p.wait('gpsimd', 'LD', v)
    cc = p.op('gpsimd', lambda e: e.collective_compute(
        "AllGather", ALU.bypass, replica_groups=[list(range(NCORES))],
        ins=[agin.ap().opt()], outs=[agout.ap().opt()]), 'CC')
    p.wait('gpsimd', 'CC', 1)
    gv = p.dma('gpsimd', cnt8[:], agout.ap())
    p.wait('tensor', 'GD', gv)
    te = p.op('tensor', lambda e: e.matmul(bank[5][0:8, 0:1], cnt8[:], rmask[:],
                                           start=True, stop=True), 'TE')
    p.wait('scalar', 'TE', te)
    se = p.op('scalar', lambda e: e.copy(offcol[:], bank[5][0:8, 0:1]), 'SE')
    p.wait('gpsimd', 'SE', se)
    gv = p.dma('gpsimd', offs_row[:], offcol[:])
    p.barrier()

    with ExitStack() as ps3:
        sb3 = lambda n, s, d=F32: ps3.enter_context(nc.sbuf_tensor(n, s, d))
        weT = [sb3(f"weT{i}", [128, 16384], BF16) for i in range(4)]
        wstg3 = sb3("wstg3", [128, 1024], BF16)
        hbf_t2 = [sb3(f"hbf_t{i}", [128, 1024], BF16) for i in range(2)]
        htok = sb3("htok", [128, 1024], BF16)
        yacc2 = [sb3(f"yacc{i}", [128, 1024]) for i in range(2)]
        u2t_t = sb3("u2t_t", [128, 1024])
        xm_t = sb3("xm_t", [128, 1024])
        ua3 = sb3("ua3", [128, 1024])
        tm3 = sb3("tm3", [128, 2048])
        gam_bc = sb3("gam_bc", [128, 1024]); bet_bc = sb3("bet_bc", [128, 1024])
        fbe_sb = sb3("fbe_sb", [8, 1024])
        ct82 = [sb3(f"ct8_{i}", [8, 128]) for i in range(2)]
        keptf = sb3("keptf", [128, 8])
        ctile2 = [sb3(f"ctile{i}", [128, 8]) for i in range(2)]
        offs_bc = sb3("offs_bc", [128, 8])

        p.dma('sync', fbe_sb[:], fbe)
        v = p.dma('sync', strow[0:1, 0:1024], gam.rearrange("(a b) -> a b", a=1))
        p.wait('tensor', 'LD', v)
        te = p.op('tensor', lambda e: e.matmul(bank[0][:], onesrow[0:1, :], strow[0:1, 0:512],
                                               start=True, stop=True), 'TE')
        te = p.op('tensor', lambda e: e.matmul(bank[1][:], onesrow[0:1, :], strow[0:1, 512:1024],
                                               start=True, stop=True), 'TE')
        p.wait('scalar', 'TE', te)
        p.op('scalar', lambda e: e.copy(gam_bc[:, 0:512], bank[0][:]), 'SE')
        se = p.op('scalar', lambda e: e.copy(gam_bc[:, 512:1024], bank[1][:]), 'SE')
        p.wait('sync', 'SE', se)
        v = p.dma('sync', strow[0:1, 0:1024], bet.rearrange("(a b) -> a b", a=1))
        p.wait('tensor', 'LD', v)
        p.wait('tensor', 'SE', se)
        te = p.op('tensor', lambda e: e.matmul(bank[0][:], onesrow[0:1, :], strow[0:1, 0:512],
                                               start=True, stop=True), 'TE')
        te = p.op('tensor', lambda e: e.matmul(bank[1][:], onesrow[0:1, :], strow[0:1, 512:1024],
                                               start=True, stop=True), 'TE')
        p.wait('scalar', 'TE', te)
        p.op('scalar', lambda e: e.copy(bet_bc[:, 0:512], bank[0][:]), 'SE')
        se = p.op('scalar', lambda e: e.copy(bet_bc[:, 512:1024], bank[1][:]), 'SE')
        p.wait('tensor', 'SE', se)
        for e_i in range(NE):
            dst = weT[e_i // 2]
            off = (e_i % 2) * 8192
            for rc in range(8):
                gv = p.dma('gpsimd', wstg3[:], fWe[e_i, rc * 128:(rc + 1) * 128, :])
                p.wait('tensor', 'GD', gv)
                for cc_ in range(8):
                    te = p.op('tensor', lambda e, cc_=cc_: e.transpose(
                        pbf[:, (cc_ % 4) * 128:(cc_ % 4 + 1) * 128],
                        wstg3[:, cc_ * 128:(cc_ + 1) * 128], identb[:]), 'TE')
                    p.wait('scalar', 'TE', te)
                    se = p.op('scalar', lambda e, cc_=cc_, rc=rc, dst=dst, off=off: e.copy(
                        dst[:, off + cc_ * 1024 + rc * 128: off + cc_ * 1024 + (rc + 1) * 128],
                        pbf[:, (cc_ % 4) * 128:(cc_ % 4 + 1) * 128]), 'SE')
                    p.wait('tensor', 'SE', se)
                p.wait('gpsimd', 'SE', se)
        p.barrier()

        te = p.op('tensor', lambda e: e.matmul(bank[6][:, 0:8], onesrow[0:1, :], offs_row[0:1, :],
                                               start=True, stop=True), 'TE')
        p.wait('scalar', 'TE', te)
        se = p.op('scalar', lambda e: e.copy(offs_bc[:], bank[6][:, 0:8]), 'SE')
        p.set_bank(6, 'SE')
        p.wait('vector', 'SE', se)

        def f3_load(tc, par):
            for dc in range(8):
                gv = p.dma('gpsimd', hbf_t2[par][:, dc * 128:(dc + 1) * 128],
                           hidT[dc * 128:(dc + 1) * 128, tc * 128:(tc + 1) * 128],
                           sem='f3h' + 'AB'[par])
            v = p.dma('sync', u2t_t[:], u2tok[tc * 128:(tc + 1) * 128, :], sem='f3u')
            v = p.dma('sync', xm_t[:], xmid[tc * 128:(tc + 1) * 128, :], sem='f3u')
            return gv, v

        def f3_load_u(tc):
            v = p.dma('sync', u2t_t[:], u2tok[tc * 128:(tc + 1) * 128, :], sem='f3u')
            v = p.dma('sync', xm_t[:], xmid[tc * 128:(tc + 1) * 128, :], sem='f3u')
            return v

        ld_h = [None, None]
        te_h = [0, 0]
        st_o = 0
        yinit_ve = 0
        ld_h[0], _u0 = f3_load(0, 0)
        ld_u = _u0
        for tc in range(32):
            par = tc % 2
            if tc + 1 < 32:
                p.wait('gpsimd', 'TE', te_h[1 - par])
                for dc in range(8):
                    gv = p.dma('gpsimd', hbf_t2[1 - par][:, dc * 128:(dc + 1) * 128],
                               hidT[dc * 128:(dc + 1) * 128, (tc + 1) * 128:(tc + 2) * 128],
                               sem='f3h' + 'AB'[1 - par])
                ld_h[1 - par] = gv
            hbf_t = hbf_t2[par]; yacc = yacc2[par]
            ct8 = ct82[par]; ctile = ctile2[par]
            p.wait('tensor', 'f3h' + 'AB'[par], ld_h[par])
            p.wait('scalar', 'VE', yinit_ve)
            for g in range(2):
                p.use_bank('pbf')
                for j in range(4):
                    dc = g * 4 + j
                    te = p.op('tensor', lambda e, dc=dc, j=j, hbf_t=hbf_t: e.transpose(
                        pbf[:, j * 128:(j + 1) * 128],
                        hbf_t[:, dc * 128:(dc + 1) * 128], identb[:]), 'TE')
                p.wait('scalar', 'TE', te)
                for j in range(4):
                    dc = g * 4 + j
                    se = p.op('scalar', lambda e, dc=dc, j=j, htok=htok: e.copy(
                        htok[:, dc * 128:(dc + 1) * 128], pbf[:, j * 128:(j + 1) * 128]), 'SE')
                p.set_bank('pbf', 'SE')
            se_htok = se
            ve = p.op('vector', lambda e, tc=tc, ctile=ctile: e.tensor_tensor(
                keptf[:], rank_all[:, tc * 8:(tc + 1) * 8], offs_bc[:], op=ALU.add), 'VE')
            ve = p.op('vector', lambda e: e.tensor_scalar(keptf[:], keptf[:], float(CAP), None,
                                                          op0=ALU.is_lt), 'VE')
            ve = p.op('vector', lambda e, tc=tc, ctile=ctile: e.tensor_tensor(
                ctile[:], keptf[:], oh_all[:, tc * 8:(tc + 1) * 8], op=ALU.mult), 'VE')
            ve = p.op('vector', lambda e, ctile=ctile: e.reduce_sum(r128a[:], ctile[:], axis=AX), 'VE')
            ve = p.op('vector', lambda e, tc=tc, ctile=ctile: e.tensor_scalar(
                ctile[:], ctile[:], rp_all[:, tc:tc + 1], None, op0=ALU.mult), 'VE')
            ve = p.op('vector', lambda e: e.tensor_scalar(r128a[:], r128a[:], -1.0, 1.0,
                                                          op0=ALU.mult, op1=ALU.add), 'VE')
            ve = p.op('vector', lambda e, tc=tc: e.tensor_tensor(
                r128b[:], r128a[:], rp_all[:, tc:tc + 1], op=ALU.mult), 'VE')
            p.wait('tensor', 'VE', ve)
            p.use_bank(6)
            te = p.op('tensor', lambda e, ctile=ctile: e.transpose(bank[6][0:8, 0:128], ctile[:],
                                                                   ident[:]), 'TE')
            p.wait('scalar', 'TE', te)
            p.wait('scalar', 'VE', ve)
            se = p.op('scalar', lambda e, ct8=ct8: e.copy(ct8[:], bank[6][0:8, 0:128]), 'SE')
            p.set_bank(6, 'SE')
            p.wait('vector', 'SE', se)
            ve = p.op('vector', lambda e, yacc=yacc: e.tensor_scalar(
                yacc[:], htok[:], r128b[:], None, op0=ALU.mult), 'VE')
            yinit_ve = ve
            p.wait('tensor', 'VE', ve)
            p.use_bank(5)
            te = p.op('tensor', lambda e, ct8=ct8: e.matmul(bank[5][:], ct8[:], fbe_sb[:, 0:512],
                                                            start=True, stop=True), 'TE')
            p.use_bank(6)
            te = p.op('tensor', lambda e, ct8=ct8: e.matmul(bank[6][:, 0:512], ct8[:],
                                                            fbe_sb[:, 512:1024],
                                                            start=True, stop=True), 'TE')
            p.wait('vector', 'TE', te)
            ve = p.op('vector', lambda e, yacc=yacc: e.tensor_tensor(yacc[:, 0:512], yacc[:, 0:512],
                                                                     bank[5][:], op=ALU.add), 'VE')
            p.set_bank(5, 'VE')
            ve = p.op('vector', lambda e, yacc=yacc: e.tensor_tensor(yacc[:, 512:1024],
                                                                     yacc[:, 512:1024],
                                                                     bank[6][:, 0:512], op=ALU.add), 'VE')
            p.set_bank(6, 'VE')
            for e_i in range(NE):
                wsrc = weT[e_i // 2]
                off = (e_i % 2) * 8192
                bk0 = (e_i % 2) * 2
                for half in range(2):
                    p.use_bank(bk0 + half)
                    for dc in range(8):
                        te = p.op('tensor', lambda e, dc=dc, half=half, off=off, wsrc=wsrc,
                                  bk0=bk0, hbf_t=hbf_t: e.matmul(
                            bank[bk0 + half][:], hbf_t[:, dc * 128:(dc + 1) * 128],
                            wsrc[:, off + dc * 1024 + half * 512: off + dc * 1024 + (half + 1) * 512],
                            start=(dc == 0), stop=(dc == 7)), 'TE')
                    if e_i == NE - 1 and half == 1:
                        te_h[par] = te
                    p.wait('vector', 'TE', te)
                    ve = p.op('vector', lambda e, e_i=e_i, half=half, bk0=bk0, yacc=yacc,
                              ctile=ctile: e.scalar_tensor_tensor(
                        yacc[:, half * 512:(half + 1) * 512], bank[bk0 + half][:],
                        ctile[:, e_i:e_i + 1], yacc[:, half * 512:(half + 1) * 512],
                        op0=ALU.mult, op1=ALU.add), 'VE')
                    p.set_bank(bk0 + half, 'VE')
            p.wait('vector', 'f3u', ld_u)
            ve = p.op('vector', lambda e, yacc=yacc: e.tensor_tensor(
                yacc[:], yacc[:], u2t_t[:], op=ALU.add), 'VE')
            p.wait('vector', 'fo', st_o)
            ve = ln_tok(yacc, ua3, LNE_T, tm3, gamma=gam_bc, beta=bet_bc, resid=xm_t)
            p.wait('sync', 'VE', ve)
            v = p.dma('sync', out[tc * 128:(tc + 1) * 128, :], ua3[:], sem='fo')
            st_o = v
            if tc + 1 < 32:
                p.wait('sync', 'VE', ve)
                ld_u = f3_load_u(tc + 1)
    p.wait('sync', 'fo', p.cur('fo'))
    return finish(nc, ges, p)


def finish(nc, ges, p):
    def replay(e, which):
        for eng, name, a, k, semh, n in p.ops:
            if eng != which:
                continue
            r = getattr(e, name)(*a, **k)
            if semh is not None:
                r.then_inc(semh, n)

    block = ges.enter_context(nc.Block())

    @block.sync
    def _(e):
        replay(e, 'sync')

    @block.gpsimd
    def _(e):
        replay(e, 'gpsimd')

    @block.scalar
    def _(e):
        replay(e, 'scalar')

    @block.vector
    def _(e):
        replay(e, 'vector')

    @block.tensor
    def _(e):
        replay(e, 'tensor')

    ges.close()
    return nc


def _consts():
    ident = np.eye(128, dtype=np.float32)
    sl = np.triu(np.ones((128, 128), np.float32), 1)
    masks = np.zeros((128, 4, 512), np.float32)
    tp = np.arange(128)[:, None]
    qf = np.arange(512)[None, :]
    for c in range(4):
        masks[:, c, :] = (tp <= qf - 128 * c).astype(np.float32)
    io8 = np.broadcast_to(np.arange(8, dtype=np.float32), (128, 8)).copy()
    e127 = np.zeros((128, 1), np.float32); e127[127, 0] = 1.0
    return (ident, sl, masks.reshape(128, 2048).astype(ml_dtypes.bfloat16), io8,
            (8.0 - io8).astype(np.float32), e127)


_CACHE = {}


def kernel(L0_attn_x, L0_attn_Wq, L0_attn_bq, L0_attn_Wk, L0_attn_bk, L0_attn_Wv,
           L0_attn_bv, L0_attn_Wo, L0_attn_bo, L0_attn_ln1_s, L0_attn_ln1_b,
           L0_attn_W1, L0_attn_b1, L0_attn_W2, L0_attn_b2, L0_attn_ln2_s,
           L0_attn_ln2_b, L0_ffn_Wq, L0_ffn_bq, L0_ffn_Wk, L0_ffn_bk, L0_ffn_Wv,
           L0_ffn_bv, L0_ffn_We, L0_ffn_be, L0_ffn_Wsw, L0_ffn_bsw, L0_ffn_gamma,
           L0_ffn_beta, _stage=None):
    x = np.ascontiguousarray(np.asarray(L0_attn_x, dtype=np.float32))
    ident, sl, masks, io8, rm8, e127c = _consts()
    f32 = lambda a: np.ascontiguousarray(np.asarray(a, dtype=np.float32))
    common = dict(
        aWq=f32(L0_attn_Wq), aWk=f32(L0_attn_Wk), aWv=f32(L0_attn_Wv), aWo=f32(L0_attn_Wo),
        aW1=f32(L0_attn_W1), aW2=f32(L0_attn_W2),
        abq=f32(L0_attn_bq), abk=f32(L0_attn_bk), abv=f32(L0_attn_bv), abo=f32(L0_attn_bo),
        ab1=f32(L0_attn_b1), ab2=f32(L0_attn_b2),
        aln1s=f32(L0_attn_ln1_s), aln1b=f32(L0_attn_ln1_b),
        aln2s=f32(L0_attn_ln2_s), aln2b=f32(L0_attn_ln2_b),
        fWq=f32(L0_ffn_Wq).reshape(1024, D), fWk=f32(L0_ffn_Wk).reshape(1024, D),
        fWv=f32(L0_ffn_Wv).reshape(1024, D),
        fbq=f32(L0_ffn_bq).reshape(-1), fbk=f32(L0_ffn_bk).reshape(-1),
        fbv=f32(L0_ffn_bv).reshape(-1),
        fWe=f32(L0_ffn_We), fbe=f32(L0_ffn_be), fWsw=f32(L0_ffn_Wsw),
        fbsw=f32(L0_ffn_bsw), gam=f32(L0_ffn_gamma), bet=f32(L0_ffn_beta),
        c_id=ident, c_sl=sl, c_mask=masks, c_io8=io8, c_rm8=rm8, c_e127=e127c,
    )
    if _stage not in _CACHE:
        _CACHE[_stage] = build_program(stage=_stage)
    nc = _CACHE[_stage]
    in_maps = []
    for c in range(NCORES):
        m = dict(common)
        m["xin"] = np.ascontiguousarray(x[2 * c:2 * c + 2].reshape(T, D))
        rmk = np.zeros((8, 1), np.float32)
        rmk[:c, 0] = 1.0
        m["c_rmask"] = rmk
        in_maps.append(m)
    res = run_bass_kernel_spmd(nc, in_maps, core_ids=list(range(NCORES)))
    return np.concatenate([np.asarray(r["out"]).reshape(2, S, D) for r in res.results], axis=0)


# version 08
# speedup vs baseline: 1.5123x; 1.0009x over previous
import numpy as np
import ml_dtypes
from contextlib import ExitStack
import concourse.bass as bass
import concourse.mybir as mybir
from concourse.bass_utils import run_bass_kernel_spmd
from concourse.alu_op_type import AluOpType as ALU

F32 = mybir.dt.float32
BF16 = mybir.dt.bfloat16
AF = mybir.ActivationFunctionType
AX = mybir.AxisListType.X

NCORES = 8
T = 4096
D = 1024
FF = 4096
NBLK = 4
S = 2048
NH = 16
NE = 8
CAP = 4915
LNE_A = 1e-6
LNE_T = 1e-5
ENGS = ('sync', 'gpsimd', 'scalar', 'vector', 'tensor')


class _Rec:
    def __getattr__(self, name):
        def call(*a, **k):
            return (name, a, k)
        return call


_REC = _Rec()


class P:
    def __init__(self, nc, es):
        self.nc = nc
        self.es = es
        self.ops = []
        self.ctr = {}
        self.sems = {}
        self.dma_sems = ['LD', 'GD']
        for s in ('LD', 'GD', 'TE', 'VE', 'SE', 'GE', 'BAR', 'CC'):
            self.sems[s] = es.enter_context(nc.semaphore(s))
            self.ctr[s] = 0

    def dmasem(self, name):
        if name not in self.sems:
            self.sems[name] = self.es.enter_context(self.nc.semaphore(name))
            self.ctr[name] = 0
            self.dma_sems.append(name)
        return name

    def emit(self, eng, fn):
        name, a, k = fn(_REC)
        self.ops.append((eng, name, a, k, None, 0))

    def cur(self, sem):
        return self.ctr[sem]

    def wait(self, eng, sem, val):
        h = self.sems[sem]
        self.ops.append((eng, 'wait_ge', (h, val), {}, None, 0))

    def op(self, eng, fn, sem=None, n=1):
        name, a, k = fn(_REC)
        if eng in ('vector', 'scalar', 'gpsimd') and sem in ('VE', 'SE', 'GE'):
            prev = getattr(self, '_last_' + eng, None)
            if prev is not None:
                self.ops.append((eng, 'wait_ge', (self.sems[prev[0]], prev[1]), {}, None, 0))
        if sem is None:
            self.ops.append((eng, name, a, k, None, 0))
            return None
        self.ctr[sem] += n
        self.ops.append((eng, name, a, k, self.sems[sem], n))
        if eng in ('vector', 'scalar', 'gpsimd') and sem in ('VE', 'SE', 'GE'):
            setattr(self, '_last_' + eng, (sem, self.ctr[sem]))
        return self.ctr[sem]

    def dma(self, eng, out, in_, sem=None, **kw):
        if sem is None:
            sem = {'sync': 'LD', 'gpsimd': 'GD', 'scalar': 'LD'}[eng]
        else:
            sem = self.dmasem(sem)
        return self.op(eng, lambda e: e.dma_start(out=out, in_=in_, **kw), sem, 16)

    def use_bank(self, bi):
        w = getattr(self, 'bankuse', {}).get(bi)
        if w is not None:
            self.wait('tensor', w[0], w[1])

    def set_bank(self, bi, sem):
        if not hasattr(self, 'bankuse'):
            self.bankuse = {}
        self.bankuse[bi] = (sem, self.ctr[sem])

    def barrier(self, engines=ENGS):
        if 'tensor' in engines:
            self.wait('sync', 'TE', self.ctr['TE'])
        if 'vector' in engines:
            self.wait('sync', 'VE', self.ctr['VE'])
        if 'scalar' in engines:
            self.wait('sync', 'SE', self.ctr['SE'])
        if 'gpsimd' in engines:
            self.wait('sync', 'GE', self.ctr['GE'])
        for s in self.dma_sems:
            self.wait('sync', s, self.ctr[s])
        self.ctr['BAR'] += 1
        self.ops.append(('sync', 'nop', (), {}, self.sems['BAR'], 1))
        tgt = self.ctr['BAR']
        for eng in engines:
            if eng != 'sync':
                self.wait(eng, 'BAR', tgt)


def build_program(stage=None):
    nc = bass.Bass("TRN2", target_bir_lowering=False)
    ges = ExitStack()

    def din(name, shape, dt=F32):
        return nc.dram_tensor(name, shape, dt, kind="ExternalInput").ap()
    xin = din("xin", [T, D])
    aWq = din("aWq", [NBLK, D, D]); aWk = din("aWk", [NBLK, D, D])
    aWv = din("aWv", [NBLK, D, D]); aWo = din("aWo", [NBLK, D, D])
    aW1 = din("aW1", [NBLK, D, FF]); aW2 = din("aW2", [NBLK, FF, D])
    abq = din("abq", [NBLK, D]); abk = din("abk", [NBLK, D]); abv = din("abv", [NBLK, D])
    abo = din("abo", [NBLK, D]); ab1 = din("ab1", [NBLK, FF]); ab2 = din("ab2", [NBLK, D])
    aln1s = din("aln1s", [NBLK, D]); aln1b = din("aln1b", [NBLK, D])
    aln2s = din("aln2s", [NBLK, D]); aln2b = din("aln2b", [NBLK, D])
    fWq = din("fWq", [1024, D]); fWk = din("fWk", [1024, D]); fWv = din("fWv", [1024, D])
    fbq = din("fbq", [1024]); fbk = din("fbk", [1024]); fbv = din("fbv", [1024])
    fWe = din("fWe", [NE, D, D]); fbe = din("fbe", [NE, D])
    fWsw = din("fWsw", [NE, D]); fbsw = din("fbsw", [NE])
    gam = din("gam", [D]); bet = din("bet", [D])
    c_id = din("c_id", [128, 128]); c_sl = din("c_sl", [128, 128])
    c_mask = din("c_mask", [128, 4 * 512], BF16); c_io8 = din("c_io8", [128, 8])
    c_rm8 = din("c_rm8", [128, 8]); c_rmask = din("c_rmask", [8, 1])
    c_e127 = din("c_e127", [128, 1])
    out = nc.dram_tensor("out", [T, D], F32, kind="ExternalOutput").ap()

    uA = nc.dram_tensor("uA", [D, T], F32).ap()
    uB = nc.dram_tensor("uB", [D, T], F32).ap()
    x1T = nc.dram_tensor("x1T", [D, T], F32).ap()
    xmid = nc.dram_tensor("xmid", [T, D], F32).ap()
    u2tok = nc.dram_tensor("u2tok", [T, D], F32).ap()
    u2T = nc.dram_tensor("u2T", [D, T], F32).ap()
    hidT = nc.dram_tensor("hidT", [D, T], BF16).ap()
    agin = nc.dram_tensor("agin", [1, 8], F32)
    agout = nc.dram_tensor("agout", [8, 8], F32)

    p = P(nc, ges)
    for _s in ('ubA', 'ubB', 'ufA', 'ufB', 'stA', 'stB', 'xbA', 'xbB', 'ufl', 'ufs',
               'p0x', 'p0s', 'pmx', 'pmg', 'pms', 'fw', 'fu', 'fz', 'fr', 'fh',
               'f3h', 'f3u', 'fo'):
        p.dmasem(_s)
    gsb = lambda n, s, d=F32: ges.enter_context(nc.sbuf_tensor(n, s, d))

    ident = gsb("ident", [128, 128])
    identb = gsb("identb", [128, 128], BF16)
    sltri = gsb("sltri", [128, 128])
    allonesb = gsb("allonesb", [128, 512], BF16)
    io8 = gsb("io8", [128, 8]); rm8 = gsb("rm8", [128, 8]); rmask = gsb("rmask", [8, 1])
    ones128 = gsb("ones128", [128, 1]); onesrow = gsb("onesrow", [1, 128])
    e127 = gsb("e127", [128, 1])
    vq = gsb("vq", [128, 32]); vk = gsb("vk", [128, 32]); vv = gsb("vv", [128, 32])
    vo = gsb("vo", [128, 32]); v1 = gsb("v1", [128, 128]); v2 = gsb("v2", [128, 32])
    vnq = gsb("vnq", [128, 32]); vnk = gsb("vnk", [128, 32])
    vl1s = gsb("vl1s", [128, 32]); vl1b = gsb("vl1b", [128, 32])
    vl2s = gsb("vl2s", [128, 32]); vl2b = gsb("vl2b", [128, 32])
    fvq = gsb("fvq", [128, 8]); fvk = gsb("fvk", [128, 8])
    fbsw_row = gsb("fbsw_row", [1, 8]); wswT = gsb("wswT", [128, 64])
    scr = [gsb(f"scr{i}", [128, 512]) for i in range(3)]
    abrow = gsb("abrow", [1, 1024]); strow = gsb("strow", [1, 1024])
    r128a = gsb("r128a", [128, 1]); r128b = gsb("r128b", [128, 1])
    r128c = gsb("r128c", [128, 1]); r128d = gsb("r128d", [128, 1])
    kvacc = gsb("kvacc", [128, 16]); ksacc = gsb("ksacc", [128, 16])
    oh_all = gsb("oh_all", [128, 256]); rank_all = gsb("rank_all", [128, 256])
    rp_all = gsb("rp_all", [128, 32])
    carry = gsb("carry", [1, 8]); offs_row = gsb("offs_row", [1, 8])
    cnt8 = gsb("cnt8", [8, 8]); offcol = gsb("offcol", [8, 1])

    bank = [ges.enter_context(nc.psum_tensor(f"bank{i}", [128, 512], F32)) for i in range(7)]
    pbf = ges.enter_context(nc.psum_tensor("pbf", [128, 1024], BF16))

    p.dma('sync', ident[:], c_id)
    p.dma('sync', sltri[:], c_sl)
    p.dma('sync', io8[:], c_io8)
    p.dma('sync', rm8[:], c_rm8)
    p.dma('sync', rmask[:], c_rmask)
    p.dma('sync', e127[:], c_e127)
    p.dma('sync', fbsw_row[:], fbsw.rearrange("(a b) -> a b", a=1))
    v = p.dma('gpsimd', identb[:], c_id)
    p.op('gpsimd', lambda e: e.memset(ones128[:], 1.0), 'GE')
    p.op('gpsimd', lambda e: e.memset(onesrow[:], 1.0), 'GE')
    p.op('gpsimd', lambda e: e.memset(carry[:], 0.0), 'GE')
    p.op('gpsimd', lambda e: e.memset(allonesb[:], 1.0), 'GE')

    def load_vec(dst, col0, row, n):
        v = p.dma('sync', scr[2][0:n, 0:128], row.rearrange("(a b) -> a b", b=128))
        p.wait('tensor', 'LD', v)
        te = p.op('tensor', lambda e, n=n: e.transpose(bank[6][:, 0:n], scr[2][0:n, 0:128], ident[0:n, 0:n]), 'TE')
        p.wait('scalar', 'TE', te)
        se = p.op('scalar', lambda e, dst=dst, col0=col0, n=n: e.copy(dst[:, col0:col0 + n],
                                                                      bank[6][:, 0:n]), 'SE')
        p.wait('tensor', 'SE', se)
        p.wait('sync', 'SE', se)

    for i in range(NBLK):
        load_vec(vq, i * 8, abq[i, :], 8)
        load_vec(vk, i * 8, abk[i, :], 8)
        load_vec(vv, i * 8, abv[i, :], 8)
        load_vec(vo, i * 8, abo[i, :], 8)
        load_vec(v1, i * 32, ab1[i, :], 32)
        load_vec(v2, i * 8, ab2[i, :], 8)
        load_vec(vl1s, i * 8, aln1s[i, :], 8)
        load_vec(vl1b, i * 8, aln1b[i, :], 8)
        load_vec(vl2s, i * 8, aln2s[i, :], 8)
        load_vec(vl2b, i * 8, aln2b[i, :], 8)
    load_vec(fvq, 0, fbq, 8)
    load_vec(fvk, 0, fbk, 8)
    v = p.dma('sync', scr[2][0:8, 0:512], fWsw[:, 0:512])
    v = p.dma('sync', scr[1][0:8, 0:512], fWsw[:, 512:1024])
    p.wait('tensor', 'LD', v)
    for dc in range(8):
        src = scr[2] if dc < 4 else scr[1]
        cof = dc * 128 if dc < 4 else (dc - 4) * 128
        te = p.op('tensor', lambda e, dc=dc, src=src, cof=cof: e.transpose(
            bank[6][:, dc * 8:(dc + 1) * 8], src[0:8, cof:cof + 128], ident[0:8, 0:8]), 'TE')
    p.wait('scalar', 'TE', p.cur('TE'))
    se = p.op('scalar', lambda e: e.copy(wswT[:], bank[6][:, 0:64]), 'SE')
    p.wait('tensor', 'SE', se)
    p.wait('vector', 'SE', p.cur('SE'))
    ve = p.op('vector', lambda e: e.tensor_scalar(vnq[:], vq[:], -1.0, None, op0=ALU.mult), 'VE')
    ve = p.op('vector', lambda e: e.tensor_scalar(vnk[:], vk[:], -1.0, None, op0=ALU.mult), 'VE')
    p.barrier()

    def ln_tok(xt, ot, eps, tmbuf, gamma=None, beta=None, resid=None):
        ve = p.op('vector', lambda e: e.reduce_sum(r128a[:], xt[:], axis=AX), 'VE')
        ve = p.op('vector', lambda e: e.tensor_scalar(r128a[:], r128a[:], 1.0 / D, None, op0=ALU.mult), 'VE')
        ve = p.op('vector', lambda e: e.tensor_scalar(tmbuf[:, 0:1024], xt[:], r128a[:], None,
                                                      op0=ALU.subtract), 'VE')
        p.wait('scalar', 'VE', ve)
        se = p.op('scalar', lambda e: e.activation(tmbuf[:, 1024:2048], tmbuf[:, 0:1024], AF.Square,
                                                   accum_out=r128b[:]), 'SE')
        p.wait('vector', 'SE', se)
        ve = p.op('vector', lambda e, eps=eps: e.tensor_scalar(r128b[:], r128b[:], 1.0 / D, eps,
                                                               op0=ALU.mult, op1=ALU.add), 'VE')
        p.wait('scalar', 'VE', ve)
        se = p.op('scalar', lambda e: e.activation(r128c[:], r128b[:], AF.Ln), 'SE')
        se = p.op('scalar', lambda e: e.activation(r128d[:], r128c[:], AF.Exp, scale=-0.5), 'SE')
        p.wait('vector', 'SE', se)
        if gamma is None:
            ve = p.op('vector', lambda e: e.tensor_scalar(ot[:], tmbuf[:, 0:1024], r128d[:], None,
                                                          op0=ALU.mult), 'VE')
        else:
            ve = p.op('vector', lambda e: e.scalar_tensor_tensor(ot[:], tmbuf[:, 0:1024], r128d[:],
                                                                 gamma[:], op0=ALU.mult, op1=ALU.mult), 'VE')
            ve = p.op('vector', lambda e: e.tensor_tensor(ot[:], ot[:], beta[:], op=ALU.add), 'VE')
            if resid is not None:
                ve = p.op('vector', lambda e: e.tensor_tensor(ot[:], ot[:], resid[:], op=ALU.add), 'VE')
        return ve

    def ln_feat(tin, Tc, sv, bv, base, eps, tout, sqb):
        p.wait('tensor', 'VE', p.cur('VE'))
        p.wait('scalar', 'VE', p.cur('VE'))
        for dc in range(8):
            se = p.op('scalar', lambda e, dc=dc: e.activation(sqb[:, dc * Tc:(dc + 1) * Tc],
                                                              tin(dc), AF.Square), 'SE')
        se_sq = se
        p.use_bank(5)
        for dc in range(8):
            te = p.op('tensor', lambda e, dc=dc: e.matmul(bank[5][0:1, 0:Tc], ones128[:], tin(dc),
                                                          start=(dc == 0), stop=(dc == 7)), 'TE')
        p.wait('tensor', 'SE', se_sq)
        p.use_bank(6)
        for dc in range(8):
            te = p.op('tensor', lambda e, dc=dc: e.matmul(bank[6][0:1, 0:Tc], ones128[:],
                                                          sqb[:, dc * Tc:(dc + 1) * Tc],
                                                          start=(dc == 0), stop=(dc == 7)), 'TE')
        p.wait('vector', 'TE', te)
        ve = p.op('vector', lambda e, Tc=Tc: e.tensor_scalar(strow[0:1, 0:Tc], bank[5][0:1, 0:Tc],
                                                             1.0 / D, None, op0=ALU.mult), 'VE')
        ve = p.op('vector', lambda e, Tc=Tc: e.tensor_tensor(strow[0:1, 512:512 + Tc], strow[0:1, 0:Tc],
                                                             strow[0:1, 0:Tc], op=ALU.mult), 'VE')
        ve = p.op('vector', lambda e, Tc=Tc: e.scalar_tensor_tensor(strow[0:1, 512:512 + Tc],
                                                                    bank[6][0:1, 0:Tc], 1.0 / D,
                                                                    strow[0:1, 512:512 + Tc],
                                                                    op0=ALU.mult, op1=ALU.subtract), 'VE')
        p.set_bank(5, 'VE')
        p.set_bank(6, 'VE')
        ve = p.op('vector', lambda e: e.tensor_scalar(strow[0:1, 512:512 + Tc],
                                                      strow[0:1, 512:512 + Tc], eps, None,
                                                      op0=ALU.add), 'VE')
        p.wait('scalar', 'VE', ve)
        se = p.op('scalar', lambda e: e.activation(strow[0:1, 512:512 + Tc],
                                                   strow[0:1, 512:512 + Tc], AF.Ln), 'SE')
        se = p.op('scalar', lambda e, Tc=Tc: e.activation(abrow[0:1, 0:Tc], strow[0:1, 512:512 + Tc],
                                                          AF.Exp, scale=-0.5), 'SE')
        p.wait('vector', 'SE', se)
        ve = p.op('vector', lambda e, Tc=Tc: e.tensor_tensor(abrow[0:1, 512:512 + Tc], strow[0:1, 0:Tc],
                                                             abrow[0:1, 0:Tc], op=ALU.mult), 'VE')
        p.wait('tensor', 'VE', ve)
        p.use_bank(3)
        te = p.op('tensor', lambda e, Tc=Tc: e.matmul(bank[3][:, 0:Tc], onesrow[0:1, :],
                                                      abrow[0:1, 0:Tc], start=True, stop=True), 'TE')
        p.use_bank(4)
        te = p.op('tensor', lambda e, Tc=Tc: e.matmul(bank[4][:, 0:Tc], onesrow[0:1, :],
                                                      abrow[0:1, 512:512 + Tc], start=True, stop=True), 'TE')
        p.wait('vector', 'TE', te)
        for dc in range(8):
            ve = p.op('vector', lambda e, dc=dc: e.tensor_tensor(sqb[:, dc * Tc:(dc + 1) * Tc],
                                                                 tin(dc), bank[3][:, 0:Tc], op=ALU.mult), 'VE')
            ve = p.op('vector', lambda e, dc=dc: e.tensor_tensor(sqb[:, dc * Tc:(dc + 1) * Tc],
                                                                 sqb[:, dc * Tc:(dc + 1) * Tc],
                                                                 bank[4][:, 0:Tc], op=ALU.subtract), 'VE')
            p.wait('scalar', 'VE', ve)
            se = p.op('scalar', lambda e, dc=dc: e.activation(tout(dc), sqb[:, dc * Tc:(dc + 1) * Tc],
                                                              AF.Identity,
                                                              bias=bv[:, base + dc:base + dc + 1],
                                                              scale=sv[:, base + dc:base + dc + 1]), 'SE')
        p.set_bank(3, 'VE')
        p.set_bank(4, 'VE')
        p.wait('vector', 'SE', se)
        return se

    with ExitStack() as ps0:
        sb0 = lambda n, s, d=F32: ps0.enter_context(nc.sbuf_tensor(n, s, d))
        xa = sb0("xa", [128, 1024]); ua = sb0("ua", [128, 1024])
        tm = sb0("tm", [128, 2048]); ustg = sb0("ustg", [128, 4096])
        for tg in range(8):
            for j in range(4):
                tc = tg * 4 + j
                v = p.dma('sync', xa[:], xin[tc * 128:(tc + 1) * 128, :], sem='p0x')
                p.wait('vector', 'p0x', v)
                ve = ln_tok(xa, ua, LNE_T, tm)
                p.wait('tensor', 'VE', ve)
                for dc in range(8):
                    bk = dc % 2
                    te = p.op('tensor', lambda e, dc=dc, bk=bk: e.transpose(
                        bank[bk][:, 0:128], ua[:, dc * 128:(dc + 1) * 128], ident[:]), 'TE')
                    p.wait('scalar', 'TE', te)
                    se = p.op('scalar', lambda e, dc=dc, j=j, bk=bk: e.copy(
                        ustg[:, dc * 512 + j * 128: dc * 512 + (j + 1) * 128], bank[bk][:, 0:128]), 'SE')
                    p.wait('tensor', 'SE', se)
                p.barrier(('sync', 'vector', 'scalar', 'tensor'))
            for dc in range(8):
                v = p.dma('sync', uA[dc * 128:(dc + 1) * 128, tg * 512:(tg + 1) * 512],
                          ustg[:, dc * 512:(dc + 1) * 512], sem='p0s')
            p.wait('scalar', 'p0s', v)
            p.barrier(('sync', 'scalar'))
        p.barrier()

    if stage == "p0":
        ov = out.rearrange("t d -> (t d)").rearrange("(d t) -> d t", d=D)
        for dc in range(8):
            p.dma('sync', ov[dc * 128:(dc + 1) * 128, :], uA[dc * 128:(dc + 1) * 128, :])
        p.wait('sync', 'LD', p.cur('LD'))
        return finish(nc, ges, p)

    ucur, unxt = uA, uB
    with ExitStack() as psa:
        for blk in range(NBLK):
            b8 = blk * 8
            b32 = blk * 32
            with ExitStack() as ps1:
                sb1 = lambda n, s, d=F32: ps1.enter_context(nc.sbuf_tensor(n, s, d))
                wA = sb1(f"wA{blk}", [128, 8192], BF16)
                wB = sb1(f"wB{blk}", [128, 8192], BF16)
                ub2 = [sb1(f"ubf{blk}_{i}", [128, 4096], BF16) for i in range(2)]
                vbf = sb1(f"vbf{blk}", [128, 4096], BF16)
                uf2 = [sb1(f"ufp{blk}_{i}", [128, 4096]) for i in range(2)]
                t1b = sb1(f"t1b{blk}", [128, 4096])
                sqa = sb1(f"sqa{blk}", [128, 4096])
                pscr = sb1(f"pscr{blk}", [128, 3072])

                def scrp(par, i):
                    return pscr[:, (par * 3 + i) * 512:(par * 3 + i + 1) * 512]

                for c in range(8):
                    p.dma('gpsimd', wA[:, c * 1024:(c + 1) * 1024], aWk[blk, c * 128:(c + 1) * 128, :])
                    gv = p.dma('gpsimd', wB[:, c * 1024:(c + 1) * 1024], aWv[blk, c * 128:(c + 1) * 128, :])
                p.op('gpsimd', lambda e: e.memset(kvacc[:], 0.0), 'GE')
                ge = p.op('gpsimd', lambda e: e.memset(ksacc[:], 0.0), 'GE')
                p.wait('tensor', 'GD', gv)
                p.wait('vector', 'GE', ge)

                def load_u(dst, tc, eng='gpsimd', sem=None):
                    for dc in range(8):
                        gv = p.dma(eng, dst[:, dc * 512:(dc + 1) * 512],
                                   ucur[dc * 128:(dc + 1) * 128, tc * 512:(tc + 1) * 512],
                                   sem=sem)
                    return gv

                ubsem = ['ubA', 'ubB']
                ub_ld = [None, None]
                ub_te = [0, 0]
                ub_ld[0] = load_u(ub2[0], 0, sem=ubsem[0])
                ve_par = [0, 0]
                for tc in range(8):
                    bi = tc // 4
                    par_u = tc % 2
                    if tc + 1 < 8:
                        p.wait('gpsimd', 'TE', ub_te[1 - par_u])
                        ub_ld[1 - par_u] = load_u(ub2[1 - par_u], tc + 1, sem=ubsem[1 - par_u])
                    ubf = ub2[par_u]
                    p.wait('tensor', ubsem[par_u], ub_ld[par_u])
                    for oc in range(8):
                        par = oc % 2
                        bk_k = par * 2
                        bk_v = bk_k + 1
                        p.use_bank(bk_k)
                        for dc in range(8):
                            te = p.op('tensor', lambda e, dc=dc, oc=oc, bk_k=bk_k, ubf=ubf: e.matmul(
                                bank[bk_k][:], wA[:, dc * 1024 + oc * 128: dc * 1024 + (oc + 1) * 128],
                                ubf[:, dc * 512:(dc + 1) * 512], start=(dc == 0), stop=(dc == 7)), 'TE')
                        te_k = te
                        p.use_bank(bk_v)
                        for dc in range(8):
                            te = p.op('tensor', lambda e, dc=dc, oc=oc, bk_v=bk_v, ubf=ubf: e.matmul(
                                bank[bk_v][:], wB[:, dc * 1024 + oc * 128: dc * 1024 + (oc + 1) * 128],
                                ubf[:, dc * 512:(dc + 1) * 512], start=(dc == 0), stop=(dc == 7)), 'TE')
                        te_v = te
                        ub_te[par_u] = te_v
                        p.wait('scalar', 'VE', ve_par[par])
                        p.wait('scalar', 'TE', te_k)
                        se = p.op('scalar', lambda e, oc=oc, bk_k=bk_k, par=par: e.activation(
                            scrp(par, 0), bank[bk_k][:], AF.Relu, bias=vk[:, b8 + oc:b8 + oc + 1]), 'SE')
                        se = p.op('scalar', lambda e, oc=oc, bk_k=bk_k, par=par: e.activation(
                            scrp(par, 1), bank[bk_k][:], AF.Relu, bias=vnk[:, b8 + oc:b8 + oc + 1],
                            scale=-1.0), 'SE')
                        p.set_bank(bk_k, 'SE')
                        se = p.op('scalar', lambda e, par=par: e.activation(scrp(par, 1), scrp(par, 1),
                                                                            AF.Exp, scale=-1.0), 'SE')
                        p.wait('scalar', 'TE', te_v)
                        se = p.op('scalar', lambda e, oc=oc, bk_v=bk_v, par=par: e.activation(
                            scrp(par, 2), bank[bk_v][:], AF.Identity, bias=vv[:, b8 + oc:b8 + oc + 1]), 'SE')
                        p.set_bank(bk_v, 'SE')
                        p.wait('vector', 'SE', se)
                        ve = p.op('vector', lambda e, par=par: e.tensor_tensor(
                            scrp(par, 0), scrp(par, 0), scrp(par, 1), op=ALU.add), 'VE')
                        ve = p.op('vector', lambda e, par=par: e.reduce_sum(r128b[:], scrp(par, 0),
                                                                            axis=AX), 'VE')
                        ve = p.op('vector', lambda e, oc=oc, bi=bi: e.tensor_tensor(
                            ksacc[:, bi * 8 + oc:bi * 8 + oc + 1], ksacc[:, bi * 8 + oc:bi * 8 + oc + 1],
                            r128b[:], op=ALU.add), 'VE')
                        ve = p.op('vector', lambda e, par=par: e.tensor_tensor(
                            scrp(par, 0), scrp(par, 0), scrp(par, 2), op=ALU.mult), 'VE')
                        ve = p.op('vector', lambda e, par=par: e.reduce_sum(r128a[:], scrp(par, 0),
                                                                            axis=AX), 'VE')
                        ve = p.op('vector', lambda e, oc=oc, bi=bi: e.tensor_tensor(
                            kvacc[:, bi * 8 + oc:bi * 8 + oc + 1], kvacc[:, bi * 8 + oc:bi * 8 + oc + 1],
                            r128a[:], op=ALU.add), 'VE')
                        ve_par[par] = ve
                p.wait('gpsimd', 'TE', p.cur('TE'))
                for c in range(8):
                    p.dma('gpsimd', wA[:, c * 1024:(c + 1) * 1024], aWq[blk, c * 128:(c + 1) * 128, :])
                    gv = p.dma('gpsimd', wB[:, c * 1024:(c + 1) * 1024], aWo[blk, c * 128:(c + 1) * 128, :])
                p.wait('vector', 'VE', p.cur('VE'))
                p.wait('tensor', 'GD', gv)
                ufsem = ['ufA', 'ufB']
                stsem = ['stA', 'stB']
                ub_ld = [None, None]
                uf_ld = [None, None]
                ub_te = [0, 0]
                uf_st = [0, 0]
                uf_ve = [0, 0]
                ub_ld[0] = load_u(ub2[0], 0, sem=ubsem[0])
                uf_ld[0] = load_u(uf2[0], 0, 'sync', sem=ufsem[0])
                ve_par = [p.cur('VE'), p.cur('VE')]
                ln_tail_te = 0
                for tc in range(8):
                    bi = tc // 4
                    par_u = tc % 2
                    if tc + 1 < 8:
                        p.wait('gpsimd', 'TE', ub_te[1 - par_u])
                        ub_ld[1 - par_u] = load_u(ub2[1 - par_u], tc + 1, sem=ubsem[1 - par_u])
                        p.wait('sync', stsem[1 - par_u], uf_st[1 - par_u])
                        p.wait('sync', 'VE', uf_ve[1 - par_u])
                        uf_ld[1 - par_u] = load_u(uf2[1 - par_u], tc + 1, 'sync',
                                                  sem=ufsem[1 - par_u])
                    ubf = ub2[par_u]
                    ufp = uf2[par_u]
                    p.wait('tensor', ubsem[par_u], ub_ld[par_u])
                    for oc in range(8):
                        par = oc % 2
                        bk_q = par
                        p.use_bank(bk_q)
                        for dc in range(8):
                            te = p.op('tensor', lambda e, dc=dc, oc=oc, bk_q=bk_q, ubf=ubf: e.matmul(
                                bank[bk_q][:], wA[:, dc * 1024 + oc * 128: dc * 1024 + (oc + 1) * 128],
                                ubf[:, dc * 512:(dc + 1) * 512], start=(dc == 0), stop=(dc == 7)), 'TE')
                        ub_te[par_u] = te
                        p.wait('scalar', 'VE', ve_par[par])
                        p.wait('scalar', 'TE', te)
                        se = p.op('scalar', lambda e, oc=oc, bk_q=bk_q, par=par: e.activation(
                            scrp(par, 0), bank[bk_q][:], AF.Relu, bias=vq[:, b8 + oc:b8 + oc + 1]), 'SE')
                        se = p.op('scalar', lambda e, oc=oc, bk_q=bk_q, par=par: e.activation(
                            scrp(par, 1), bank[bk_q][:], AF.Relu, bias=vnq[:, b8 + oc:b8 + oc + 1],
                            scale=-1.0), 'SE')
                        p.set_bank(bk_q, 'SE')
                        se = p.op('scalar', lambda e, par=par: e.activation(scrp(par, 1), scrp(par, 1),
                                                                            AF.Exp, scale=-1.0), 'SE')
                        p.wait('vector', 'SE', se)
                        ve = p.op('vector', lambda e, par=par: e.tensor_tensor(
                            scrp(par, 0), scrp(par, 0), scrp(par, 1), op=ALU.add), 'VE')
                        ve = p.op('vector', lambda e, oc=oc, bi=bi, par=par: e.tensor_scalar(
                            scrp(par, 1), scrp(par, 0), ksacc[:, bi * 8 + oc:bi * 8 + oc + 1], 1e-6,
                            op0=ALU.mult, op1=ALU.add), 'VE')
                        ve = p.op('vector', lambda e, par=par: e.reciprocal(scrp(par, 2),
                                                                            scrp(par, 1)), 'VE')
                        ve = p.op('vector', lambda e, oc=oc, bi=bi, par=par: e.scalar_tensor_tensor(
                            vbf[:, oc * 512:(oc + 1) * 512], scrp(par, 0),
                            kvacc[:, bi * 8 + oc:bi * 8 + oc + 1], scrp(par, 2),
                            op0=ALU.mult, op1=ALU.mult), 'VE')
                        ve_par[par] = ve
                    p.wait('tensor', 'VE', ve)
                    p.wait('vector', ufsem[par_u], uf_ld[par_u])
                    for dc in range(8):
                        par = dc % 2
                        bk_a = 2 + par
                        p.use_bank(bk_a)
                        for ac in range(8):
                            te = p.op('tensor', lambda e, ac=ac, dc=dc, bk_a=bk_a: e.matmul(
                                bank[bk_a][:], wB[:, ac * 1024 + dc * 128: ac * 1024 + (dc + 1) * 128],
                                vbf[:, ac * 512:(ac + 1) * 512], start=(ac == 0), stop=(ac == 7)), 'TE')
                        p.wait('scalar', 'VE', ve_par[par])
                        p.wait('scalar', 'TE', te)
                        se = p.op('scalar', lambda e, dc=dc, bk_a=bk_a, par=par: e.activation(
                            scrp(par, 0), bank[bk_a][:], AF.Gelu_apprx_tanh,
                            bias=vo[:, b8 + dc:b8 + dc + 1]), 'SE')
                        p.set_bank(bk_a, 'SE')
                        se = p.op('scalar', lambda e, par=par: e.activation(
                            scrp(par, 1), scrp(par, 0), AF.Gelu_apprx_tanh), 'SE')
                        p.wait('vector', 'SE', se)
                        p.wait('vector', 'TE', ln_tail_te)
                        ve = p.op('vector', lambda e, dc=dc, par=par, ufp=ufp: e.tensor_tensor(
                            t1b[:, dc * 512:(dc + 1) * 512], scrp(par, 1),
                            ufp[:, dc * 512:(dc + 1) * 512], op=ALU.add), 'VE')
                        ve_par[par] = ve
                        uf_ve[par_u] = ve
                    ln_feat(lambda dc, t1b=t1b: t1b[:, dc * 512:(dc + 1) * 512], 512, vl1s, vl1b,
                            b8, LNE_A, lambda dc, ufp=ufp: ufp[:, dc * 512:(dc + 1) * 512], sqa)
                    ln_tail_te = p.cur('TE')
                    p.wait('sync', 'SE', p.cur('SE'))
                    for dc in range(8):
                        v = p.dma('sync', x1T[dc * 128:(dc + 1) * 128, tc * 512:(tc + 1) * 512],
                                  ufp[:, dc * 512:(dc + 1) * 512], sem=stsem[par_u])
                    uf_st[par_u] = v
                p.barrier()
            with ExitStack() as ps2:
                sb2 = lambda n, s, d=F32: ps2.enter_context(nc.sbuf_tensor(n, s, d))
                w1sb = sb2(f"w1sb{blk}", [128, 32768], BF16)
                w2sb = sb2(f"w2sb{blk}", [128, 32768], BF16)
                xb2 = [sb2(f"x1bf{blk}_{i}", [128, 2048], BF16) for i in range(2)]
                hbf = sb2(f"hbf{blk}", [128, 8192], BF16)
                ufp = sb2(f"ufp2{blk}", [128, 2048])
                t1b = sb2(f"t1b2{blk}", [128, 2048])
                sqa = sb2(f"sqa2{blk}", [128, 2048])
                p2scr = sb2(f"p2scr{blk}", [128, 512])
                for c in range(8):
                    gv = p.dma('gpsimd', w1sb[:, c * 4096:(c + 1) * 4096],
                               aW1[blk, c * 128:(c + 1) * 128, :])
                for c in range(32):
                    gv = p.dma('gpsimd', w2sb[:, c * 1024:(c + 1) * 1024],
                               aW2[blk, c * 128:(c + 1) * 128, :])
                p.wait('tensor', 'GD', gv)

                def load_x1(dst, tc, eng, sem=None, ncols=256):
                    for dc in range(8):
                        gv = p.dma(eng, dst[:, dc * ncols:(dc + 1) * ncols],
                                   x1T[dc * 128:(dc + 1) * 128, tc * ncols:(tc + 1) * ncols],
                                   sem=sem)
                    return gv

                xbsem = ['xbA', 'xbB']
                xb_ld = [None, None]
                xb_te = [0, 0]
                xb_ld[0] = load_x1(xb2[0], 0, 'gpsimd', sem=xbsem[0])
                uf_ld = load_x1(ufp, 0, 'sync', sem='ufl')
                uf_st = 0
                uf_ve = 0
                ve_par = [0, 0]
                ln_tail_te = 0
                hbf_te = 0
                for tc in range(16):
                    par_x = tc % 2
                    if tc + 1 < 16:
                        p.wait('gpsimd', 'TE', xb_te[1 - par_x])
                        xb_ld[1 - par_x] = load_x1(xb2[1 - par_x], tc + 1, 'gpsimd',
                                                   sem=xbsem[1 - par_x])
                    x1bf = xb2[par_x]
                    p.wait('tensor', xbsem[par_x], xb_ld[par_x])
                    for fc in range(32):
                        bk_h = fc % 2
                        p.use_bank(bk_h)
                        for dc in range(8):
                            te = p.op('tensor', lambda e, dc=dc, fc=fc, bk_h=bk_h, x1bf=x1bf: e.matmul(
                                bank[bk_h][:, 0:256],
                                w1sb[:, dc * 4096 + fc * 128: dc * 4096 + (fc + 1) * 128],
                                x1bf[:, dc * 256:(dc + 1) * 256], start=(dc == 0), stop=(dc == 7)), 'TE')
                        xb_te[par_x] = te
                        p.wait('scalar', 'TE', te)
                        if fc == 0:
                            p.wait('scalar', 'TE', hbf_te)
                        se = p.op('scalar', lambda e, fc=fc, bk_h=bk_h: e.activation(
                            hbf[:, fc * 256:(fc + 1) * 256], bank[bk_h][:, 0:256], AF.Gelu_apprx_tanh,
                            bias=v1[:, b32 + fc:b32 + fc + 1]), 'SE')
                        p.set_bank(bk_h, 'SE')
                    p.wait('tensor', 'SE', p.cur('SE'))
                    p.wait('vector', 'ufl', uf_ld)
                    for dc in range(8):
                        par = dc % 2
                        bk_f = 2 + par
                        p.use_bank(bk_f)
                        for fc in range(32):
                            te = p.op('tensor', lambda e, fc=fc, dc=dc, bk_f=bk_f: e.matmul(
                                bank[bk_f][:, 0:256],
                                w2sb[:, fc * 1024 + dc * 128: fc * 1024 + (dc + 1) * 128],
                                hbf[:, fc * 256:(fc + 1) * 256], start=(fc == 0), stop=(fc == 31)), 'TE')
                        if dc == 7:
                            hbf_te = te
                        p.wait('scalar', 'VE', ve_par[par])
                        p.wait('scalar', 'TE', te)
                        se = p.op('scalar', lambda e, dc=dc, bk_f=bk_f, par=par: e.activation(
                            p2scr[:, par * 256:(par + 1) * 256], bank[bk_f][:, 0:256],
                            AF.Gelu_apprx_tanh, bias=v2[:, b8 + dc:b8 + dc + 1]), 'SE')
                        p.set_bank(bk_f, 'SE')
                        p.wait('vector', 'SE', se)
                        p.wait('vector', 'TE', ln_tail_te)
                        ve = p.op('vector', lambda e, dc=dc, par=par: e.tensor_tensor(
                            t1b[:, dc * 256:(dc + 1) * 256], p2scr[:, par * 256:(par + 1) * 256],
                            ufp[:, dc * 256:(dc + 1) * 256], op=ALU.add), 'VE')
                        ve_par[par] = ve
                        uf_ve = ve
                    ln_feat(lambda dc, t1b=t1b: t1b[:, dc * 256:(dc + 1) * 256], 256, vl2s, vl2b,
                            b8, LNE_A, lambda dc, ufp=ufp: ufp[:, dc * 256:(dc + 1) * 256], sqa)
                    ln_tail_te = p.cur('TE')
                    p.wait('sync', 'SE', p.cur('SE'))
                    for dc in range(8):
                        v = p.dma('sync', unxt[dc * 128:(dc + 1) * 128, tc * 256:(tc + 1) * 256],
                                  ufp[:, dc * 256:(dc + 1) * 256], sem='ufs')
                    uf_st = v
                    if tc + 1 < 16:
                        p.wait('sync', 'ufs', uf_st)
                        uf_ld = load_x1(ufp, tc + 1, 'sync', sem='ufl')
                p.barrier()
            ucur, unxt = unxt, ucur
            if stage == f"blk{blk}":
                ov = out.rearrange("t d -> (t d)").rearrange("(d t) -> d t", d=D)
                for dc in range(8):
                    p.dma('sync', ov[dc * 128:(dc + 1) * 128, :], ucur[dc * 128:(dc + 1) * 128, :])
                p.wait('sync', 'LD', p.cur('LD'))
                return finish(nc, ges, p)

    with ExitStack() as psm:
        sbm = lambda n, s, d=F32: psm.enter_context(nc.sbuf_tensor(n, s, d))
        xa = sbm("mxa", [128, 1024]); xb = sbm("mxb", [128, 1024])
        ua = sbm("mua", [128, 1024]); tm = sbm("mtm", [128, 2048])
        ustg = sbm("mustg", [128, 2048])
        for tc in range(32):
            vxa = p.dma('sync', xa[:], xin[tc * 128:(tc + 1) * 128, :], sem='pmx')
            for dc in range(8):
                gv = p.dma('gpsimd', ustg[:, dc * 128:(dc + 1) * 128],
                           ucur[dc * 128:(dc + 1) * 128, tc * 128:(tc + 1) * 128], sem='pmg')
            p.wait('tensor', 'pmg', gv)
            for dc in range(8):
                bk = dc % 2
                te = p.op('tensor', lambda e, dc=dc, bk=bk: e.transpose(
                    bank[bk][:, 0:128], ustg[:, dc * 128:(dc + 1) * 128], ident[:]), 'TE')
                p.wait('scalar', 'TE', te)
                se = p.op('scalar', lambda e, dc=dc, bk=bk: e.copy(
                    xb[:, dc * 128:(dc + 1) * 128], bank[bk][:, 0:128]), 'SE')
                p.wait('tensor', 'SE', se)
            p.wait('vector', 'SE', se)
            p.wait('vector', 'pmx', vxa)
            ve = p.op('vector', lambda e: e.tensor_tensor(xa[:], xa[:], xb[:], op=ALU.add), 'VE')
            p.wait('sync', 'VE', ve)
            v = p.dma('sync', xmid[tc * 128:(tc + 1) * 128, :], xa[:], sem='pms')
            ve = ln_tok(xa, ua, LNE_T, tm)
            p.wait('sync', 'VE', ve)
            v = p.dma('sync', u2tok[tc * 128:(tc + 1) * 128, :], ua[:], sem='pms')
            p.wait('tensor', 'VE', ve)
            for dc in range(8):
                bk = 2 + dc % 2
                te = p.op('tensor', lambda e, dc=dc, bk=bk: e.transpose(
                    bank[bk][:, 0:128], ua[:, dc * 128:(dc + 1) * 128], ident[:]), 'TE')
                p.wait('scalar', 'TE', te)
                se = p.op('scalar', lambda e, dc=dc, bk=bk: e.copy(
                    ustg[:, 1024 + dc * 128: 1024 + (dc + 1) * 128] if False else ustg[:, dc * 128:(dc + 1) * 128],
                    bank[bk][:, 0:128]), 'SE')
                p.wait('tensor', 'SE', se)
            p.wait('sync', 'SE', se)
            for dc in range(8):
                v = p.dma('sync', u2T[dc * 128:(dc + 1) * 128, tc * 128:(tc + 1) * 128],
                          ustg[:, dc * 128:(dc + 1) * 128], sem='pms')
            p.wait('gpsimd', 'pms', v)
            p.barrier(('sync', 'gpsimd', 'tensor', 'scalar', 'vector'))
        p.barrier()

    if stage == "pmid":
        with ExitStack() as pst_:
            xa = pst_.enter_context(nc.sbuf_tensor("dxa", [128, 1024], F32))
            for tc in range(32):
                v = p.dma('sync', xa[:], u2tok[tc * 128:(tc + 1) * 128, :])
                p.wait('sync', 'LD', v)
                v = p.dma('sync', out[tc * 128:(tc + 1) * 128, :], xa[:])
                p.wait('sync', 'LD', v)
        return finish(nc, ges, p)

    with ExitStack() as psf:
        sbf = lambda n, s, d=F32: psf.enter_context(nc.sbuf_tensor(n, s, d))
        qTb = sbf("qTb", [128, 16384], BF16)
        kTb = sbf("kTb", [128, 16384], BF16)
        vtok = sbf("vtok", [128, 16640], BF16)
        l8 = sbf("l8", [128, 8]); e8 = sbf("e8", [128, 8])
        w8a = sbf("w8a", [128, 8]); w8b = sbf("w8b", [128, 8])

        for tt in range(16):
            for h in range(NH):
                ge = p.op('gpsimd', lambda e, tt=tt, h=h: e.memset(
                    vtok[:, tt * 1040 + h * 65 + 64: tt * 1040 + h * 65 + 65], 1.0), 'GE')
        ge_ones = p.cur('GE')

        def transpose_w(dst, src, wstg):
            for rc in range(8):
                gv = p.dma('gpsimd', wstg[:], src[rc * 128:(rc + 1) * 128, :], sem='fw')
                p.wait('tensor', 'fw', gv)
                for cc in range(8):
                    te = p.op('tensor', lambda e, cc=cc: e.transpose(
                        pbf[:, (cc % 4) * 128:(cc % 4 + 1) * 128],
                        wstg[:, cc * 128:(cc + 1) * 128], identb[:]), 'TE')
                    p.wait('scalar', 'TE', te)
                    se = p.op('scalar', lambda e, cc=cc, rc=rc, dst=dst: e.copy(
                        dst[:, cc * 1024 + rc * 128: cc * 1024 + (rc + 1) * 128],
                        pbf[:, (cc % 4) * 128:(cc % 4 + 1) * 128]), 'SE')
                    p.wait('tensor', 'SE', se)
                p.wait('gpsimd', 'SE', se)

        if True:
            for b in range(2):
              with ExitStack() as psf1:
                sbf1 = lambda n, s, d=F32: psf1.enter_context(nc.sbuf_tensor(n, s, d))
                wqT = sbf1(f"wqT{b}", [128, 8192], BF16)
                wkT = sbf1(f"wkT{b}", [128, 8192], BF16)
                wvT = sbf1(f"wvT{b}", [128, 8192], BF16)
                wstg = sbf1(f"wstg{b}", [128, 1024], BF16)
                u2bf = sbf1(f"u2bf{b}", [128, 4096], BF16)
                fbv_row = sbf1(f"fbv_row{b}", [1, 1024])
                p.dma('sync', fbv_row[:], fbv.rearrange("(a b) -> a b", a=1))
                transpose_w(wqT, fWq, wstg)
                transpose_w(wkT, fWk, wstg)
                transpose_w(wvT, fWv, wstg)
                p.barrier()
                for tc in range(4):
                    for dc in range(8):
                        gv = p.dma('gpsimd', u2bf[:, dc * 512:(dc + 1) * 512],
                                   u2T[dc * 128:(dc + 1) * 128, (b * 4 + tc) * 512:(b * 4 + tc + 1) * 512],
                                   sem='fu')
                    p.wait('tensor', 'fu', gv)
                    for pi, wT, bias_c, dstT in ((0, wqT, fvq, qTb), (1, wkT, fvk, kTb)):
                        for oc in range(8):
                            bk_x = pi * 2 + oc % 2
                            for dc in range(8):
                                te = p.op('tensor', lambda e, dc=dc, oc=oc, wT=wT, bk_x=bk_x: e.matmul(
                                    bank[bk_x][:], wT[:, dc * 1024 + oc * 128: dc * 1024 + (oc + 1) * 128],
                                    u2bf[:, dc * 512:(dc + 1) * 512], start=(dc == 0), stop=(dc == 7)), 'TE')
                            p.wait('scalar', 'TE', te)
                            se = p.op('scalar', lambda e, oc=oc, tc=tc, bias_c=bias_c, dstT=dstT, bk_x=bk_x:
                                      e.activation(dstT[:, oc * 2048 + tc * 512: oc * 2048 + (tc + 1) * 512],
                                                   bank[bk_x][:], AF.Identity, bias=bias_c[:, oc:oc + 1]), 'SE')
                            p.wait('tensor', 'SE', se)
                    p.wait('scalar', 'GE', ge_ones)
                    for ts_ in range(4):
                        tt = tc * 4 + ts_
                        for half in range(2):
                            bk_x = 4 + half
                            for dc in range(8):
                                te = p.op('tensor', lambda e, dc=dc, ts_=ts_, half=half, bk_x=bk_x: e.matmul(
                                    bank[bk_x][:], u2bf[:, dc * 512 + ts_ * 128: dc * 512 + (ts_ + 1) * 128],
                                    wvT[:, dc * 1024 + half * 512: dc * 1024 + (half + 1) * 512],
                                    start=(dc == 0), stop=False), 'TE')
                            te = p.op('tensor', lambda e, half=half, bk_x=bk_x: e.matmul(
                                bank[bk_x][:], onesrow[0:1, :], fbv_row[0:1, half * 512:(half + 1) * 512],
                                start=False, stop=True), 'TE')
                            p.wait('scalar', 'TE', te)
                            for j in range(8):
                                h = half * 8 + j
                                se = p.op('scalar', lambda e, tt=tt, h=h, j=j, bk_x=bk_x: e.copy(
                                    vtok[:, tt * 1040 + h * 65: tt * 1040 + h * 65 + 64],
                                    bank[bk_x][:, j * 64:(j + 1) * 64]), 'SE')
                            p.wait('tensor', 'SE', se)
                    p.barrier(('gpsimd', 'tensor', 'scalar', 'vector', 'sync'))
              with ExitStack() as psf2:
                sbf2 = lambda n, s, d=F32: psf2.enter_context(nc.sbuf_tensor(n, s, d))
                pbuf = sbf2(f"pbuf{b}", [128, 8192], BF16)
                hstg2 = sbf2(f"hstg{b}", [128, 8192])
                masks = sbf2(f"masks{b}", [128, 2048], BF16)
                zstg = sbf2(f"zstg{b}", [1, 1024]); z16 = sbf2(f"z16{b}", [16, 1024])
                r16 = sbf2(f"r16{b}", [16, 1024])
                rstage = sbf2(f"rstage{b}", [1, 1024])
                rbc = scr[1]
                v = p.dma('sync', masks[:], c_mask)
                p.wait('vector', 'LD', v)
                hstg_te = [0, 0]
                hstg_fh = [0, 0]
                z_ve = [0, 0]
                r_te = [0, 0]
                for qb in range(4):
                    nreal = 4 * qb + 4
                    par = qb % 2
                    hoff = par * 4096
                    zoff = par * 512
                    exp_last = 0
                    for h in range(NH):
                        hb = (64 * h) % 128
                        hc = h // 2
                        bk_pv = 2 + h % 2
                        for tt in range(nreal):
                            d_off = 512 * qb - 128 * tt
                            bk_s = tt % 2
                            p.use_bank(bk_s)
                            te = p.op('tensor', lambda e, tt=tt, qb=qb, bk_s=bk_s, hb=hb, hc=hc: e.matmul(
                                bank[bk_s][:], kTb[hb:hb + 64, hc * 2048 + tt * 128: hc * 2048 + (tt + 1) * 128],
                                qTb[hb:hb + 64, hc * 2048 + qb * 512: hc * 2048 + (qb + 1) * 512],
                                start=True, stop=True), 'TE')
                            if d_off >= 128:
                                p.wait('scalar', 'TE', te)
                                se = p.op('scalar', lambda e, tt=tt, bk_s=bk_s: e.activation(
                                    pbuf[:, tt * 512:(tt + 1) * 512], bank[bk_s][:], AF.Exp,
                                    scale=1.0 / 32.0), 'SE')
                                p.set_bank(bk_s, 'SE')
                            else:
                                c = (-d_off) // 128
                                sp = tt % 2
                                p.wait('vector', 'TE', te)
                                p.wait('vector', 'SE', exp_last)
                                ve = p.op('vector', lambda e, c=c, bk_s=bk_s, sp=sp: e.tensor_tensor(
                                    scr[sp * 2][:], bank[bk_s][:], masks[:, c * 512:(c + 1) * 512],
                                    op=ALU.mult), 'VE')
                                p.set_bank(bk_s, 'VE')
                                p.wait('scalar', 'VE', ve)
                                se = p.op('scalar', lambda e, tt=tt, sp=sp: e.activation(
                                    pbuf[:, tt * 512:(tt + 1) * 512], scr[sp * 2][:], AF.Exp,
                                    scale=1.0 / 32.0), 'SE')
                            exp_last = se
                        p.use_bank(bk_pv)
                        order = list(range(nreal, 16)) + list(range(nreal))
                        for i, tt in enumerate(order):
                            if i == (16 - nreal):
                                p.wait('tensor', 'SE', exp_last)
                            if tt < nreal:
                                rhsf = (lambda e, tt=tt: pbuf[:, tt * 512:(tt + 1) * 512])
                            else:
                                rhsf = (lambda e: allonesb[:])
                            te = p.op('tensor', lambda e, tt=tt, h=h, rhsf=rhsf, i=i, bk_pv=bk_pv,
                                      nr=nreal: e.matmul(
                                bank[bk_pv][0:65, :], vtok[:, tt * 1040 + h * 65: tt * 1040 + h * 65 + 65],
                                rhsf(e), start=(i == 0), stop=(i == 15)), 'TE')
                        p.wait('scalar', 'TE', te)
                        if h == 0:
                            p.wait('scalar', 'TE', hstg_te[par])
                            p.wait('scalar', 'fh', hstg_fh[par])
                            p.wait('scalar', 'VE', z_ve[par])
                        se = p.op('scalar', lambda e, zoff=zoff, bk_pv=bk_pv: e.copy(
                            zstg[0:1, zoff:zoff + 512], bank[bk_pv][64:65, :]), 'SE')
                        se2 = p.op('scalar', lambda e, hb=hb, hc=hc, hoff=hoff, bk_pv=bk_pv: e.copy(
                            hstg2[hb:hb + 64, hoff + hc * 512: hoff + (hc + 1) * 512],
                            bank[bk_pv][0:64, :]), 'SE')
                        p.set_bank(bk_pv, 'SE')
                        p.wait('gpsimd', 'SE', se2)
                        gv = p.dma('gpsimd', z16[h:h + 1, zoff:zoff + 512],
                                   zstg[0:1, zoff:zoff + 512], sem='fz')
                        p.wait('scalar', 'fz', gv)
                    p.wait('vector', 'fz', p.cur('fz'))
                    ve = p.op('vector', lambda e, zoff=zoff: e.reciprocal(
                        r16[:, zoff:zoff + 512], z16[:, zoff:zoff + 512]), 'VE')
                    z_ve[par] = ve
                    p.wait('gpsimd', 'VE', ve)
                    for h in range(NH):
                        hb = (64 * h) % 128
                        hc = h // 2
                        p.wait('gpsimd', 'TE', r_te[par])
                        gv = p.dma('gpsimd', rstage[0:1, zoff:zoff + 512],
                                   r16[h:h + 1, zoff:zoff + 512], sem='fr')
                        p.wait('tensor', 'fr', gv)
                        p.use_bank(4)
                        te = p.op('tensor', lambda e, hb=hb, zoff=zoff: e.matmul(
                            bank[4][hb:hb + 64, :], onesrow[0:1, 0:64],
                            rstage[0:1, zoff:zoff + 512], start=True, stop=True), 'TE')
                        r_te[par] = te
                        p.wait('scalar', 'TE', te)
                        se = p.op('scalar', lambda e, hb=hb: e.copy(rbc[hb:hb + 64, :],
                                                                    bank[4][hb:hb + 64, :]), 'SE')
                        p.set_bank(4, 'SE')
                        p.wait('vector', 'SE', se)
                        ve = p.op('vector', lambda e, hb=hb, hc=hc, hoff=hoff: e.tensor_tensor(
                            hstg2[hb:hb + 64, hoff + hc * 512: hoff + (hc + 1) * 512],
                            hstg2[hb:hb + 64, hoff + hc * 512: hoff + (hc + 1) * 512],
                            rbc[hb:hb + 64, :], op=ALU.mult), 'VE')
                        p.wait('scalar', 'VE', ve)
                    nrm_ve = p.cur('VE')
                    p.wait('gpsimd', 'VE', nrm_ve)
                    for dc in range(8):
                        gv = p.dma('gpsimd', hidT[dc * 128:(dc + 1) * 128,
                                                  (b * 4 + qb) * 512:(b * 4 + qb + 1) * 512],
                                   hstg2[:, hoff + dc * 512: hoff + (dc + 1) * 512], sem='fh')
                    hstg_fh[par] = gv
                    for ts_ in range(4):
                        gtc = b * 16 + qb * 4 + ts_
                        p.use_bank(5)
                        if ts_ == 0:
                            p.wait('tensor', 'VE', nrm_ve)
                        for dc in range(8):
                            te = p.op('tensor', lambda e, dc=dc, ts_=ts_, hoff=hoff: e.matmul(
                                bank[5][:, 0:8],
                                hstg2[:, hoff + dc * 512 + ts_ * 128: hoff + dc * 512 + (ts_ + 1) * 128],
                                wswT[:, dc * 8:(dc + 1) * 8], start=(dc == 0), stop=False), 'TE')
                        te = p.op('tensor', lambda e: e.matmul(bank[5][:, 0:8], onesrow[0:1, :],
                                                               fbsw_row[0:1, :], start=False, stop=True), 'TE')
                        hstg_te[par] = te
                        p.wait('vector', 'TE', te)
                        ve = p.op('vector', lambda e: e.tensor_copy(l8[:], bank[5][:, 0:8]), 'VE')
                        p.set_bank(5, 'VE')
                        ve = p.op('vector', lambda e: e.reduce_max(r128a[:], l8[:], axis=AX, negate=True), 'VE')
                        p.wait('scalar', 'VE', ve)
                        se = p.op('scalar', lambda e: e.activation(e8[:], l8[:], AF.Exp, bias=r128a[:]), 'SE')
                        p.wait('vector', 'SE', se)
                        ve = p.op('vector', lambda e: e.reduce_sum(r128b[:], e8[:], axis=AX), 'VE')
                        ve = p.op('vector', lambda e: e.reciprocal(r128c[:], r128b[:]), 'VE')
                        ve = p.op('vector', lambda e, gtc=gtc: e.tensor_copy(rp_all[:, gtc:gtc + 1],
                                                                             r128c[:]), 'VE')
                        ve = p.op('vector', lambda e: e.tensor_scalar(r128d[:], r128a[:], -1.0, None,
                                                                      op0=ALU.mult), 'VE')
                        ve = p.op('vector', lambda e: e.tensor_scalar(w8a[:], l8[:], r128d[:], None,
                                                                      op0=ALU.is_ge), 'VE')
                        ve = p.op('vector', lambda e: e.tensor_tensor(w8b[:], w8a[:], rm8[:], op=ALU.mult), 'VE')
                        ve = p.op('vector', lambda e: e.tensor_scalar(w8b[:], w8b[:], -1.0, 8.0,
                                                                      op0=ALU.mult, op1=ALU.add), 'VE')
                        ve = p.op('vector', lambda e: e.tensor_reduce(r128d[:], w8b[:], axis=AX,
                                                                      op=ALU.min), 'VE')
                        ve = p.op('vector', lambda e, gtc=gtc: e.tensor_scalar(
                            oh_all[:, gtc * 8:(gtc + 1) * 8], io8[:], r128d[:], None,
                            op0=ALU.is_equal), 'VE')
                        p.wait('tensor', 'VE', ve)
                        p.use_bank(6)
                        te = p.op('tensor', lambda e, gtc=gtc: e.matmul(
                            bank[6][:, 0:8], sltri[:], oh_all[:, gtc * 8:(gtc + 1) * 8],
                            start=True, stop=False), 'TE')
                        te = p.op('tensor', lambda e: e.matmul(bank[6][:, 0:8], onesrow[0:1, :],
                                                               carry[0:1, :], start=False, stop=True), 'TE')
                        p.wait('vector', 'TE', te)
                        ve = p.op('vector', lambda e, gtc=gtc: e.tensor_copy(
                            rank_all[:, gtc * 8:(gtc + 1) * 8], bank[6][:, 0:8]), 'VE')
                        p.set_bank(6, 'VE')
                        p.wait('tensor', 'VE', ve)
                        p.use_bank(6)
                        te = p.op('tensor', lambda e, gtc=gtc: e.matmul(
                            bank[6][0:1, 0:8], e127[:], rank_all[:, gtc * 8:(gtc + 1) * 8],
                            start=True, stop=False), 'TE')
                        te = p.op('tensor', lambda e, gtc=gtc: e.matmul(
                            bank[6][0:1, 0:8], e127[:], oh_all[:, gtc * 8:(gtc + 1) * 8],
                            start=False, stop=True), 'TE')
                        p.wait('vector', 'TE', te)
                        ve = p.op('vector', lambda e: e.tensor_copy(carry[0:1, :],
                                                                    bank[6][0:1, 0:8]), 'VE')
                        p.set_bank(6, 'VE')
                p.barrier()
        p.barrier()

    v = p.dma('sync', agin.ap(), carry[0:1, :])
    p.wait('gpsimd', 'LD', v)
    cc = p.op('gpsimd', lambda e: e.collective_compute(
        "AllGather", ALU.bypass, replica_groups=[list(range(NCORES))],
        ins=[agin.ap().opt()], outs=[agout.ap().opt()]), 'CC')
    p.wait('gpsimd', 'CC', 1)
    gv = p.dma('gpsimd', cnt8[:], agout.ap())
    p.wait('tensor', 'GD', gv)
    te = p.op('tensor', lambda e: e.matmul(bank[5][0:8, 0:1], cnt8[:], rmask[:],
                                           start=True, stop=True), 'TE')
    p.wait('scalar', 'TE', te)
    se = p.op('scalar', lambda e: e.copy(offcol[:], bank[5][0:8, 0:1]), 'SE')
    p.wait('gpsimd', 'SE', se)
    gv = p.dma('gpsimd', offs_row[:], offcol[:])
    p.barrier()

    with ExitStack() as ps3:
        sb3 = lambda n, s, d=F32: ps3.enter_context(nc.sbuf_tensor(n, s, d))
        weT = [sb3(f"weT{i}", [128, 16384], BF16) for i in range(4)]
        wstg3 = sb3("wstg3", [128, 1024], BF16)
        hbf_t2 = [sb3(f"hbf_t{i}", [128, 1024], BF16) for i in range(2)]
        htok = sb3("htok", [128, 1024], BF16)
        yacc2 = [sb3(f"yacc{i}", [128, 1024]) for i in range(2)]
        u2t_t = sb3("u2t_t", [128, 1024])
        xm_t = sb3("xm_t", [128, 1024])
        ua3 = sb3("ua3", [128, 1024])
        tm3 = sb3("tm3", [128, 2048])
        gam_bc = sb3("gam_bc", [128, 1024]); bet_bc = sb3("bet_bc", [128, 1024])
        fbe_sb = sb3("fbe_sb", [8, 1024])
        ct82 = [sb3(f"ct8_{i}", [8, 128]) for i in range(2)]
        keptf = sb3("keptf", [128, 8])
        ctile2 = [sb3(f"ctile{i}", [128, 8]) for i in range(2)]
        offs_bc = sb3("offs_bc", [128, 8])

        p.dma('sync', fbe_sb[:], fbe)
        v = p.dma('sync', strow[0:1, 0:1024], gam.rearrange("(a b) -> a b", a=1))
        p.wait('tensor', 'LD', v)
        te = p.op('tensor', lambda e: e.matmul(bank[0][:], onesrow[0:1, :], strow[0:1, 0:512],
                                               start=True, stop=True), 'TE')
        te = p.op('tensor', lambda e: e.matmul(bank[1][:], onesrow[0:1, :], strow[0:1, 512:1024],
                                               start=True, stop=True), 'TE')
        p.wait('scalar', 'TE', te)
        p.op('scalar', lambda e: e.copy(gam_bc[:, 0:512], bank[0][:]), 'SE')
        se = p.op('scalar', lambda e: e.copy(gam_bc[:, 512:1024], bank[1][:]), 'SE')
        p.wait('sync', 'SE', se)
        v = p.dma('sync', strow[0:1, 0:1024], bet.rearrange("(a b) -> a b", a=1))
        p.wait('tensor', 'LD', v)
        p.wait('tensor', 'SE', se)
        te = p.op('tensor', lambda e: e.matmul(bank[0][:], onesrow[0:1, :], strow[0:1, 0:512],
                                               start=True, stop=True), 'TE')
        te = p.op('tensor', lambda e: e.matmul(bank[1][:], onesrow[0:1, :], strow[0:1, 512:1024],
                                               start=True, stop=True), 'TE')
        p.wait('scalar', 'TE', te)
        p.op('scalar', lambda e: e.copy(bet_bc[:, 0:512], bank[0][:]), 'SE')
        se = p.op('scalar', lambda e: e.copy(bet_bc[:, 512:1024], bank[1][:]), 'SE')
        p.wait('tensor', 'SE', se)
        for e_i in range(NE):
            dst = weT[e_i // 2]
            off = (e_i % 2) * 8192
            for rc in range(8):
                gv = p.dma('gpsimd', wstg3[:], fWe[e_i, rc * 128:(rc + 1) * 128, :])
                p.wait('tensor', 'GD', gv)
                for cc_ in range(8):
                    te = p.op('tensor', lambda e, cc_=cc_: e.transpose(
                        pbf[:, (cc_ % 4) * 128:(cc_ % 4 + 1) * 128],
                        wstg3[:, cc_ * 128:(cc_ + 1) * 128], identb[:]), 'TE')
                    p.wait('scalar', 'TE', te)
                    se = p.op('scalar', lambda e, cc_=cc_, rc=rc, dst=dst, off=off: e.copy(
                        dst[:, off + cc_ * 1024 + rc * 128: off + cc_ * 1024 + (rc + 1) * 128],
                        pbf[:, (cc_ % 4) * 128:(cc_ % 4 + 1) * 128]), 'SE')
                    p.wait('tensor', 'SE', se)
                p.wait('gpsimd', 'SE', se)
        p.barrier()

        te = p.op('tensor', lambda e: e.matmul(bank[6][:, 0:8], onesrow[0:1, :], offs_row[0:1, :],
                                               start=True, stop=True), 'TE')
        p.wait('scalar', 'TE', te)
        se = p.op('scalar', lambda e: e.copy(offs_bc[:], bank[6][:, 0:8]), 'SE')
        p.set_bank(6, 'SE')
        p.wait('vector', 'SE', se)

        def f3_load(tc, par):
            for dc in range(8):
                gv = p.dma('gpsimd', hbf_t2[par][:, dc * 128:(dc + 1) * 128],
                           hidT[dc * 128:(dc + 1) * 128, tc * 128:(tc + 1) * 128],
                           sem='f3h' + 'AB'[par])
            v = p.dma('sync', u2t_t[:], u2tok[tc * 128:(tc + 1) * 128, :], sem='f3u')
            v = p.dma('sync', xm_t[:], xmid[tc * 128:(tc + 1) * 128, :], sem='f3u')
            return gv, v

        def f3_load_u(tc):
            v = p.dma('sync', u2t_t[:], u2tok[tc * 128:(tc + 1) * 128, :], sem='f3u')
            v = p.dma('sync', xm_t[:], xmid[tc * 128:(tc + 1) * 128, :], sem='f3u')
            return v

        ld_h = [None, None]
        te_h = [0, 0]
        st_o = 0
        yinit_ve = 0
        ld_h[0], _u0 = f3_load(0, 0)
        ld_u = _u0
        for tc in range(32):
            par = tc % 2
            if tc + 1 < 32:
                p.wait('gpsimd', 'TE', te_h[1 - par])
                for dc in range(8):
                    gv = p.dma('gpsimd', hbf_t2[1 - par][:, dc * 128:(dc + 1) * 128],
                               hidT[dc * 128:(dc + 1) * 128, (tc + 1) * 128:(tc + 2) * 128],
                               sem='f3h' + 'AB'[1 - par])
                ld_h[1 - par] = gv
            hbf_t = hbf_t2[par]; yacc = yacc2[par]
            ct8 = ct82[par]; ctile = ctile2[par]
            p.wait('tensor', 'f3h' + 'AB'[par], ld_h[par])
            p.wait('scalar', 'VE', yinit_ve)
            for g in range(2):
                p.use_bank('pbf')
                for j in range(4):
                    dc = g * 4 + j
                    te = p.op('tensor', lambda e, dc=dc, j=j, hbf_t=hbf_t: e.transpose(
                        pbf[:, j * 128:(j + 1) * 128],
                        hbf_t[:, dc * 128:(dc + 1) * 128], identb[:]), 'TE')
                p.wait('scalar', 'TE', te)
                for j in range(4):
                    dc = g * 4 + j
                    se = p.op('scalar', lambda e, dc=dc, j=j, htok=htok: e.copy(
                        htok[:, dc * 128:(dc + 1) * 128], pbf[:, j * 128:(j + 1) * 128]), 'SE')
                p.set_bank('pbf', 'SE')
            se_htok = se
            ve = p.op('vector', lambda e, tc=tc, ctile=ctile: e.tensor_tensor(
                keptf[:], rank_all[:, tc * 8:(tc + 1) * 8], offs_bc[:], op=ALU.add), 'VE')
            ve = p.op('vector', lambda e: e.tensor_scalar(keptf[:], keptf[:], float(CAP), None,
                                                          op0=ALU.is_lt), 'VE')
            ve = p.op('vector', lambda e, tc=tc, ctile=ctile: e.tensor_tensor(
                ctile[:], keptf[:], oh_all[:, tc * 8:(tc + 1) * 8], op=ALU.mult), 'VE')
            ve = p.op('vector', lambda e, ctile=ctile: e.reduce_sum(r128a[:], ctile[:], axis=AX), 'VE')
            ve = p.op('vector', lambda e, tc=tc, ctile=ctile: e.tensor_scalar(
                ctile[:], ctile[:], rp_all[:, tc:tc + 1], None, op0=ALU.mult), 'VE')
            ve = p.op('vector', lambda e: e.tensor_scalar(r128a[:], r128a[:], -1.0, 1.0,
                                                          op0=ALU.mult, op1=ALU.add), 'VE')
            ve = p.op('vector', lambda e, tc=tc: e.tensor_tensor(
                r128b[:], r128a[:], rp_all[:, tc:tc + 1], op=ALU.mult), 'VE')
            p.wait('tensor', 'VE', ve)
            p.use_bank(6)
            te = p.op('tensor', lambda e, ctile=ctile: e.transpose(bank[6][0:8, 0:128], ctile[:],
                                                                   ident[:]), 'TE')
            p.wait('scalar', 'TE', te)
            p.wait('scalar', 'VE', ve)
            se = p.op('scalar', lambda e, ct8=ct8: e.copy(ct8[:], bank[6][0:8, 0:128]), 'SE')
            p.set_bank(6, 'SE')
            p.wait('vector', 'SE', se)
            ve = p.op('vector', lambda e, yacc=yacc: e.tensor_scalar(
                yacc[:], htok[:], r128b[:], None, op0=ALU.mult), 'VE')
            yinit_ve = ve
            p.wait('tensor', 'VE', ve)
            p.use_bank(5)
            te = p.op('tensor', lambda e, ct8=ct8: e.matmul(bank[5][:], ct8[:], fbe_sb[:, 0:512],
                                                            start=True, stop=True), 'TE')
            p.use_bank(6)
            te = p.op('tensor', lambda e, ct8=ct8: e.matmul(bank[6][:, 0:512], ct8[:],
                                                            fbe_sb[:, 512:1024],
                                                            start=True, stop=True), 'TE')
            p.wait('vector', 'TE', te)
            ve = p.op('vector', lambda e, yacc=yacc: e.tensor_tensor(yacc[:, 0:512], yacc[:, 0:512],
                                                                     bank[5][:], op=ALU.add), 'VE')
            p.set_bank(5, 'VE')
            ve = p.op('vector', lambda e, yacc=yacc: e.tensor_tensor(yacc[:, 512:1024],
                                                                     yacc[:, 512:1024],
                                                                     bank[6][:, 0:512], op=ALU.add), 'VE')
            p.set_bank(6, 'VE')
            for e_i in range(NE):
                wsrc = weT[e_i // 2]
                off = (e_i % 2) * 8192
                bk0 = (e_i % 2) * 2
                for half in range(2):
                    p.use_bank(bk0 + half)
                    for dc in range(8):
                        te = p.op('tensor', lambda e, dc=dc, half=half, off=off, wsrc=wsrc,
                                  bk0=bk0, hbf_t=hbf_t: e.matmul(
                            bank[bk0 + half][:], hbf_t[:, dc * 128:(dc + 1) * 128],
                            wsrc[:, off + dc * 1024 + half * 512: off + dc * 1024 + (half + 1) * 512],
                            start=(dc == 0), stop=(dc == 7)), 'TE')
                    if e_i == NE - 1 and half == 1:
                        te_h[par] = te
                    p.wait('vector', 'TE', te)
                    ve = p.op('vector', lambda e, e_i=e_i, half=half, bk0=bk0, yacc=yacc,
                              ctile=ctile: e.scalar_tensor_tensor(
                        yacc[:, half * 512:(half + 1) * 512], bank[bk0 + half][:],
                        ctile[:, e_i:e_i + 1], yacc[:, half * 512:(half + 1) * 512],
                        op0=ALU.mult, op1=ALU.add), 'VE')
                    p.set_bank(bk0 + half, 'VE')
            p.wait('vector', 'f3u', ld_u)
            ve = p.op('vector', lambda e, yacc=yacc: e.tensor_tensor(
                yacc[:], yacc[:], u2t_t[:], op=ALU.add), 'VE')
            p.wait('vector', 'fo', st_o)
            ve = ln_tok(yacc, ua3, LNE_T, tm3, gamma=gam_bc, beta=bet_bc, resid=xm_t)
            p.wait('sync', 'VE', ve)
            v = p.dma('sync', out[tc * 128:(tc + 1) * 128, :], ua3[:], sem='fo')
            st_o = v
            if tc + 1 < 32:
                p.wait('sync', 'VE', ve)
                ld_u = f3_load_u(tc + 1)
    p.wait('sync', 'fo', p.cur('fo'))
    return finish(nc, ges, p)


def finish(nc, ges, p):
    def replay(e, which):
        for eng, name, a, k, semh, n in p.ops:
            if eng != which:
                continue
            r = getattr(e, name)(*a, **k)
            if semh is not None:
                r.then_inc(semh, n)

    block = ges.enter_context(nc.Block())

    @block.sync
    def _(e):
        replay(e, 'sync')

    @block.gpsimd
    def _(e):
        replay(e, 'gpsimd')

    @block.scalar
    def _(e):
        replay(e, 'scalar')

    @block.vector
    def _(e):
        replay(e, 'vector')

    @block.tensor
    def _(e):
        replay(e, 'tensor')

    ges.close()
    return nc


def _consts():
    ident = np.eye(128, dtype=np.float32)
    sl = np.triu(np.ones((128, 128), np.float32), 1)
    masks = np.zeros((128, 4, 512), np.float32)
    tp = np.arange(128)[:, None]
    qf = np.arange(512)[None, :]
    for c in range(4):
        masks[:, c, :] = (tp <= qf - 128 * c).astype(np.float32)
    io8 = np.broadcast_to(np.arange(8, dtype=np.float32), (128, 8)).copy()
    e127 = np.zeros((128, 1), np.float32); e127[127, 0] = 1.0
    return (ident, sl, masks.reshape(128, 2048).astype(ml_dtypes.bfloat16), io8,
            (8.0 - io8).astype(np.float32), e127)


_CACHE = {}


def kernel(L0_attn_x, L0_attn_Wq, L0_attn_bq, L0_attn_Wk, L0_attn_bk, L0_attn_Wv,
           L0_attn_bv, L0_attn_Wo, L0_attn_bo, L0_attn_ln1_s, L0_attn_ln1_b,
           L0_attn_W1, L0_attn_b1, L0_attn_W2, L0_attn_b2, L0_attn_ln2_s,
           L0_attn_ln2_b, L0_ffn_Wq, L0_ffn_bq, L0_ffn_Wk, L0_ffn_bk, L0_ffn_Wv,
           L0_ffn_bv, L0_ffn_We, L0_ffn_be, L0_ffn_Wsw, L0_ffn_bsw, L0_ffn_gamma,
           L0_ffn_beta, _stage=None):
    x = np.ascontiguousarray(np.asarray(L0_attn_x, dtype=np.float32))
    ident, sl, masks, io8, rm8, e127c = _consts()
    f32 = lambda a: np.ascontiguousarray(np.asarray(a, dtype=np.float32))
    common = dict(
        aWq=f32(L0_attn_Wq), aWk=f32(L0_attn_Wk), aWv=f32(L0_attn_Wv), aWo=f32(L0_attn_Wo),
        aW1=f32(L0_attn_W1), aW2=f32(L0_attn_W2),
        abq=f32(L0_attn_bq), abk=f32(L0_attn_bk), abv=f32(L0_attn_bv), abo=f32(L0_attn_bo),
        ab1=f32(L0_attn_b1), ab2=f32(L0_attn_b2),
        aln1s=f32(L0_attn_ln1_s), aln1b=f32(L0_attn_ln1_b),
        aln2s=f32(L0_attn_ln2_s), aln2b=f32(L0_attn_ln2_b),
        fWq=f32(L0_ffn_Wq).reshape(1024, D), fWk=f32(L0_ffn_Wk).reshape(1024, D),
        fWv=f32(L0_ffn_Wv).reshape(1024, D),
        fbq=f32(L0_ffn_bq).reshape(-1), fbk=f32(L0_ffn_bk).reshape(-1),
        fbv=f32(L0_ffn_bv).reshape(-1),
        fWe=f32(L0_ffn_We), fbe=f32(L0_ffn_be), fWsw=f32(L0_ffn_Wsw),
        fbsw=f32(L0_ffn_bsw), gam=f32(L0_ffn_gamma), bet=f32(L0_ffn_beta),
        c_id=ident, c_sl=sl, c_mask=masks, c_io8=io8, c_rm8=rm8, c_e127=e127c,
    )
    if _stage not in _CACHE:
        _CACHE[_stage] = build_program(stage=_stage)
    nc = _CACHE[_stage]
    in_maps = []
    for c in range(NCORES):
        m = dict(common)
        m["xin"] = np.ascontiguousarray(x[2 * c:2 * c + 2].reshape(T, D))
        rmk = np.zeros((8, 1), np.float32)
        rmk[:c, 0] = 1.0
        m["c_rmask"] = rmk
        in_maps.append(m)
    res = run_bass_kernel_spmd(nc, in_maps, core_ids=list(range(NCORES)))
    return np.concatenate([np.asarray(r["out"]).reshape(2, S, D) for r in res.results], axis=0)


# version 09 (final)
# speedup vs baseline: 1.5407x; 1.0187x over previous
import numpy as np
import ml_dtypes
from contextlib import ExitStack
import concourse.bass as bass
import concourse.mybir as mybir
from concourse.bass_utils import run_bass_kernel_spmd
from concourse.alu_op_type import AluOpType as ALU

F32 = mybir.dt.float32
BF16 = mybir.dt.bfloat16
AF = mybir.ActivationFunctionType
AX = mybir.AxisListType.X

NCORES = 8
T = 4096
D = 1024
FF = 4096
NBLK = 4
S = 2048
NH = 16
NE = 8
CAP = 4915
LNE_A = 1e-6
LNE_T = 1e-5
ENGS = ('sync', 'gpsimd', 'scalar', 'vector', 'tensor')


class _Rec:
    def __getattr__(self, name):
        def call(*a, **k):
            return (name, a, k)
        return call


_REC = _Rec()


class P:
    def __init__(self, nc, es):
        self.nc = nc
        self.es = es
        self.ops = []
        self.ctr = {}
        self.sems = {}
        self.dma_sems = ['LD', 'GD']
        for s in ('LD', 'GD', 'TE', 'VE', 'SE', 'GE', 'BAR', 'CC'):
            self.sems[s] = es.enter_context(nc.semaphore(s))
            self.ctr[s] = 0

    def dmasem(self, name):
        if name not in self.sems:
            self.sems[name] = self.es.enter_context(self.nc.semaphore(name))
            self.ctr[name] = 0
            self.dma_sems.append(name)
        return name

    def emit(self, eng, fn):
        name, a, k = fn(_REC)
        self.ops.append((eng, name, a, k, None, 0))

    def cur(self, sem):
        return self.ctr[sem]

    def wait(self, eng, sem, val):
        h = self.sems[sem]
        self.ops.append((eng, 'wait_ge', (h, val), {}, None, 0))

    def op(self, eng, fn, sem=None, n=1):
        name, a, k = fn(_REC)
        if eng in ('vector', 'scalar', 'gpsimd') and sem in ('VE', 'SE', 'GE'):
            prev = getattr(self, '_last_' + eng, None)
            if prev is not None:
                self.ops.append((eng, 'wait_ge', (self.sems[prev[0]], prev[1]), {}, None, 0))
        if sem is None:
            self.ops.append((eng, name, a, k, None, 0))
            return None
        self.ctr[sem] += n
        self.ops.append((eng, name, a, k, self.sems[sem], n))
        if eng in ('vector', 'scalar', 'gpsimd') and sem in ('VE', 'SE', 'GE'):
            setattr(self, '_last_' + eng, (sem, self.ctr[sem]))
        return self.ctr[sem]

    def dma(self, eng, out, in_, sem=None, **kw):
        if sem is None:
            sem = {'sync': 'LD', 'gpsimd': 'GD', 'scalar': 'LD'}[eng]
        else:
            sem = self.dmasem(sem)
        return self.op(eng, lambda e: e.dma_start(out=out, in_=in_, **kw), sem, 16)

    def use_bank(self, bi):
        w = getattr(self, 'bankuse', {}).get(bi)
        if w is not None:
            self.wait('tensor', w[0], w[1])

    def set_bank(self, bi, sem):
        if not hasattr(self, 'bankuse'):
            self.bankuse = {}
        self.bankuse[bi] = (sem, self.ctr[sem])

    def barrier(self, engines=ENGS):
        if 'tensor' in engines:
            self.wait('sync', 'TE', self.ctr['TE'])
        if 'vector' in engines:
            self.wait('sync', 'VE', self.ctr['VE'])
        if 'scalar' in engines:
            self.wait('sync', 'SE', self.ctr['SE'])
        if 'gpsimd' in engines:
            self.wait('sync', 'GE', self.ctr['GE'])
        for s in self.dma_sems:
            self.wait('sync', s, self.ctr[s])
        self.ctr['BAR'] += 1
        self.ops.append(('sync', 'nop', (), {}, self.sems['BAR'], 1))
        tgt = self.ctr['BAR']
        for eng in engines:
            if eng != 'sync':
                self.wait(eng, 'BAR', tgt)


def build_program(stage=None):
    nc = bass.Bass("TRN2", target_bir_lowering=False)
    ges = ExitStack()

    def din(name, shape, dt=F32):
        return nc.dram_tensor(name, shape, dt, kind="ExternalInput").ap()
    xin = din("xin", [T, D])
    aWq = din("aWq", [NBLK, D, D]); aWk = din("aWk", [NBLK, D, D])
    aWv = din("aWv", [NBLK, D, D]); aWo = din("aWo", [NBLK, D, D])
    aW1 = din("aW1", [NBLK, D, FF]); aW2 = din("aW2", [NBLK, FF, D])
    abq = din("abq", [NBLK, D]); abk = din("abk", [NBLK, D]); abv = din("abv", [NBLK, D])
    abo = din("abo", [NBLK, D]); ab1 = din("ab1", [NBLK, FF]); ab2 = din("ab2", [NBLK, D])
    aln1s = din("aln1s", [NBLK, D]); aln1b = din("aln1b", [NBLK, D])
    aln2s = din("aln2s", [NBLK, D]); aln2b = din("aln2b", [NBLK, D])
    fWq = din("fWq", [1024, D]); fWk = din("fWk", [1024, D]); fWv = din("fWv", [1024, D])
    fbq = din("fbq", [1024]); fbk = din("fbk", [1024]); fbv = din("fbv", [1024])
    fWe = din("fWe", [NE, D, D]); fbe = din("fbe", [NE, D])
    fWsw = din("fWsw", [NE, D]); fbsw = din("fbsw", [NE])
    gam = din("gam", [D]); bet = din("bet", [D])
    c_id = din("c_id", [128, 128]); c_sl = din("c_sl", [128, 128])
    c_mask = din("c_mask", [128, 4 * 512], BF16); c_io8 = din("c_io8", [128, 8])
    c_rm8 = din("c_rm8", [128, 8]); c_rmask = din("c_rmask", [8, 1])
    c_e127 = din("c_e127", [128, 1])
    out = nc.dram_tensor("out", [T, D], F32, kind="ExternalOutput").ap()

    uA = nc.dram_tensor("uA", [D, T], F32).ap()
    uB = nc.dram_tensor("uB", [D, T], F32).ap()
    x1T = nc.dram_tensor("x1T", [D, T], F32).ap()
    xmid = nc.dram_tensor("xmid", [T, D], F32).ap()
    u2tok = nc.dram_tensor("u2tok", [T, D], F32).ap()
    u2T = nc.dram_tensor("u2T", [D, T], F32).ap()
    hidT = nc.dram_tensor("hidT", [D, T], BF16).ap()
    agin = nc.dram_tensor("agin", [1, 8], F32)
    agout = nc.dram_tensor("agout", [8, 8], F32)

    p = P(nc, ges)
    for _s in ('ubA', 'ubB', 'ufA', 'ufB', 'stA', 'stB', 'xbA', 'xbB', 'ufl', 'ufs',
               'p0x', 'p0s', 'pmx', 'pmg', 'pms', 'fw', 'fu', 'fz', 'fr', 'fh',
               'f3h', 'f3u', 'fo'):
        p.dmasem(_s)
    gsb = lambda n, s, d=F32: ges.enter_context(nc.sbuf_tensor(n, s, d))

    ident = gsb("ident", [128, 128])
    identb = gsb("identb", [128, 128], BF16)
    sltri = gsb("sltri", [128, 128])
    allonesb = gsb("allonesb", [128, 512], BF16)
    io8 = gsb("io8", [128, 8]); rm8 = gsb("rm8", [128, 8]); rmask = gsb("rmask", [8, 1])
    ones128 = gsb("ones128", [128, 1]); onesrow = gsb("onesrow", [1, 128])
    e127 = gsb("e127", [128, 1])
    vq = gsb("vq", [128, 32]); vk = gsb("vk", [128, 32]); vv = gsb("vv", [128, 32])
    vo = gsb("vo", [128, 32]); v1 = gsb("v1", [128, 128]); v2 = gsb("v2", [128, 32])
    vnq = gsb("vnq", [128, 32]); vnk = gsb("vnk", [128, 32])
    vl1s = gsb("vl1s", [128, 32]); vl1b = gsb("vl1b", [128, 32])
    vl2s = gsb("vl2s", [128, 32]); vl2b = gsb("vl2b", [128, 32])
    fvq = gsb("fvq", [128, 8]); fvk = gsb("fvk", [128, 8])
    fbsw_row = gsb("fbsw_row", [1, 8]); wswT = gsb("wswT", [128, 64])
    scr = [gsb(f"scr{i}", [128, 512]) for i in range(3)]
    abrow = gsb("abrow", [1, 1024]); strow = gsb("strow", [1, 1024])
    r128a = gsb("r128a", [128, 1]); r128b = gsb("r128b", [128, 1])
    r128c = gsb("r128c", [128, 1]); r128d = gsb("r128d", [128, 1])
    kvacc = gsb("kvacc", [128, 16]); ksacc = gsb("ksacc", [128, 16])
    oh_all = gsb("oh_all", [128, 256]); rank_all = gsb("rank_all", [128, 256])
    rp_all = gsb("rp_all", [128, 32])
    carry = gsb("carry", [1, 8]); offs_row = gsb("offs_row", [1, 8])
    cnt8 = gsb("cnt8", [8, 8]); offcol = gsb("offcol", [8, 1])

    bank = [ges.enter_context(nc.psum_tensor(f"bank{i}", [128, 512], F32)) for i in range(7)]
    pbf = ges.enter_context(nc.psum_tensor("pbf", [128, 1024], BF16))

    p.dma('sync', ident[:], c_id)
    p.dma('sync', sltri[:], c_sl)
    p.dma('sync', io8[:], c_io8)
    p.dma('sync', rm8[:], c_rm8)
    p.dma('sync', rmask[:], c_rmask)
    p.dma('sync', e127[:], c_e127)
    p.dma('sync', fbsw_row[:], fbsw.rearrange("(a b) -> a b", a=1))
    v = p.dma('gpsimd', identb[:], c_id)
    p.op('gpsimd', lambda e: e.memset(ones128[:], 1.0), 'GE')
    p.op('gpsimd', lambda e: e.memset(onesrow[:], 1.0), 'GE')
    p.op('gpsimd', lambda e: e.memset(carry[:], 0.0), 'GE')
    p.op('gpsimd', lambda e: e.memset(allonesb[:], 1.0), 'GE')

    def load_vec(dst, col0, row, n):
        v = p.dma('sync', scr[2][0:n, 0:128], row.rearrange("(a b) -> a b", b=128))
        p.wait('tensor', 'LD', v)
        te = p.op('tensor', lambda e, n=n: e.transpose(bank[6][:, 0:n], scr[2][0:n, 0:128], ident[0:n, 0:n]), 'TE')
        p.wait('scalar', 'TE', te)
        se = p.op('scalar', lambda e, dst=dst, col0=col0, n=n: e.copy(dst[:, col0:col0 + n],
                                                                      bank[6][:, 0:n]), 'SE')
        p.wait('tensor', 'SE', se)
        p.wait('sync', 'SE', se)

    for i in range(NBLK):
        load_vec(vq, i * 8, abq[i, :], 8)
        load_vec(vk, i * 8, abk[i, :], 8)
        load_vec(vv, i * 8, abv[i, :], 8)
        load_vec(vo, i * 8, abo[i, :], 8)
        load_vec(v1, i * 32, ab1[i, :], 32)
        load_vec(v2, i * 8, ab2[i, :], 8)
        load_vec(vl1s, i * 8, aln1s[i, :], 8)
        load_vec(vl1b, i * 8, aln1b[i, :], 8)
        load_vec(vl2s, i * 8, aln2s[i, :], 8)
        load_vec(vl2b, i * 8, aln2b[i, :], 8)
    load_vec(fvq, 0, fbq, 8)
    load_vec(fvk, 0, fbk, 8)
    v = p.dma('sync', scr[2][0:8, 0:512], fWsw[:, 0:512])
    v = p.dma('sync', scr[1][0:8, 0:512], fWsw[:, 512:1024])
    p.wait('tensor', 'LD', v)
    for dc in range(8):
        src = scr[2] if dc < 4 else scr[1]
        cof = dc * 128 if dc < 4 else (dc - 4) * 128
        te = p.op('tensor', lambda e, dc=dc, src=src, cof=cof: e.transpose(
            bank[6][:, dc * 8:(dc + 1) * 8], src[0:8, cof:cof + 128], ident[0:8, 0:8]), 'TE')
    p.wait('scalar', 'TE', p.cur('TE'))
    se = p.op('scalar', lambda e: e.copy(wswT[:], bank[6][:, 0:64]), 'SE')
    p.wait('tensor', 'SE', se)
    p.wait('vector', 'SE', p.cur('SE'))
    ve = p.op('vector', lambda e: e.tensor_scalar(vnq[:], vq[:], -1.0, None, op0=ALU.mult), 'VE')
    ve = p.op('vector', lambda e: e.tensor_scalar(vnk[:], vk[:], -1.0, None, op0=ALU.mult), 'VE')
    p.barrier()

    def ln_tok(xt, ot, eps, tmbuf, gamma=None, beta=None, resid=None):
        ve = p.op('vector', lambda e: e.reduce_sum(r128a[:], xt[:], axis=AX), 'VE')
        ve = p.op('vector', lambda e: e.tensor_scalar(r128a[:], r128a[:], 1.0 / D, None, op0=ALU.mult), 'VE')
        ve = p.op('vector', lambda e: e.tensor_scalar(tmbuf[:, 0:1024], xt[:], r128a[:], None,
                                                      op0=ALU.subtract), 'VE')
        p.wait('scalar', 'VE', ve)
        se = p.op('scalar', lambda e: e.activation(tmbuf[:, 1024:2048], tmbuf[:, 0:1024], AF.Square,
                                                   accum_out=r128b[:]), 'SE')
        p.wait('vector', 'SE', se)
        ve = p.op('vector', lambda e, eps=eps: e.tensor_scalar(r128b[:], r128b[:], 1.0 / D, eps,
                                                               op0=ALU.mult, op1=ALU.add), 'VE')
        p.wait('scalar', 'VE', ve)
        se = p.op('scalar', lambda e: e.activation(r128c[:], r128b[:], AF.Ln), 'SE')
        se = p.op('scalar', lambda e: e.activation(r128d[:], r128c[:], AF.Exp, scale=-0.5), 'SE')
        p.wait('vector', 'SE', se)
        if gamma is None:
            ve = p.op('vector', lambda e: e.tensor_scalar(ot[:], tmbuf[:, 0:1024], r128d[:], None,
                                                          op0=ALU.mult), 'VE')
        else:
            ve = p.op('vector', lambda e: e.scalar_tensor_tensor(ot[:], tmbuf[:, 0:1024], r128d[:],
                                                                 gamma[:], op0=ALU.mult, op1=ALU.mult), 'VE')
            ve = p.op('vector', lambda e: e.tensor_tensor(ot[:], ot[:], beta[:], op=ALU.add), 'VE')
            if resid is not None:
                ve = p.op('vector', lambda e: e.tensor_tensor(ot[:], ot[:], resid[:], op=ALU.add), 'VE')
        return ve

    def ln_feat(tin, Tc, sv, bv, base, eps, tout, sqb):
        p.wait('tensor', 'VE', p.cur('VE'))
        p.wait('scalar', 'VE', p.cur('VE'))
        for dc in range(8):
            se = p.op('scalar', lambda e, dc=dc: e.activation(sqb[:, dc * Tc:(dc + 1) * Tc],
                                                              tin(dc), AF.Square), 'SE')
        se_sq = se
        p.use_bank(5)
        for dc in range(8):
            te = p.op('tensor', lambda e, dc=dc: e.matmul(bank[5][0:1, 0:Tc], ones128[:], tin(dc),
                                                          start=(dc == 0), stop=(dc == 7)), 'TE')
        p.wait('tensor', 'SE', se_sq)
        p.use_bank(6)
        for dc in range(8):
            te = p.op('tensor', lambda e, dc=dc: e.matmul(bank[6][0:1, 0:Tc], ones128[:],
                                                          sqb[:, dc * Tc:(dc + 1) * Tc],
                                                          start=(dc == 0), stop=(dc == 7)), 'TE')
        p.wait('vector', 'TE', te)
        ve = p.op('vector', lambda e, Tc=Tc: e.tensor_scalar(strow[0:1, 0:Tc], bank[5][0:1, 0:Tc],
                                                             1.0 / D, None, op0=ALU.mult), 'VE')
        ve = p.op('vector', lambda e, Tc=Tc: e.tensor_tensor(strow[0:1, 512:512 + Tc], strow[0:1, 0:Tc],
                                                             strow[0:1, 0:Tc], op=ALU.mult), 'VE')
        ve = p.op('vector', lambda e, Tc=Tc: e.scalar_tensor_tensor(strow[0:1, 512:512 + Tc],
                                                                    bank[6][0:1, 0:Tc], 1.0 / D,
                                                                    strow[0:1, 512:512 + Tc],
                                                                    op0=ALU.mult, op1=ALU.subtract), 'VE')
        p.set_bank(5, 'VE')
        p.set_bank(6, 'VE')
        ve = p.op('vector', lambda e: e.tensor_scalar(strow[0:1, 512:512 + Tc],
                                                      strow[0:1, 512:512 + Tc], eps, None,
                                                      op0=ALU.add), 'VE')
        p.wait('scalar', 'VE', ve)
        se = p.op('scalar', lambda e: e.activation(strow[0:1, 512:512 + Tc],
                                                   strow[0:1, 512:512 + Tc], AF.Ln), 'SE')
        se = p.op('scalar', lambda e, Tc=Tc: e.activation(abrow[0:1, 0:Tc], strow[0:1, 512:512 + Tc],
                                                          AF.Exp, scale=-0.5), 'SE')
        p.wait('vector', 'SE', se)
        ve = p.op('vector', lambda e, Tc=Tc: e.tensor_tensor(abrow[0:1, 512:512 + Tc], strow[0:1, 0:Tc],
                                                             abrow[0:1, 0:Tc], op=ALU.mult), 'VE')
        p.wait('tensor', 'VE', ve)
        p.use_bank(3)
        te = p.op('tensor', lambda e, Tc=Tc: e.matmul(bank[3][:, 0:Tc], onesrow[0:1, :],
                                                      abrow[0:1, 0:Tc], start=True, stop=True), 'TE')
        p.use_bank(4)
        te = p.op('tensor', lambda e, Tc=Tc: e.matmul(bank[4][:, 0:Tc], onesrow[0:1, :],
                                                      abrow[0:1, 512:512 + Tc], start=True, stop=True), 'TE')
        p.wait('vector', 'TE', te)
        for dc in range(8):
            ve = p.op('vector', lambda e, dc=dc: e.tensor_tensor(sqb[:, dc * Tc:(dc + 1) * Tc],
                                                                 tin(dc), bank[3][:, 0:Tc], op=ALU.mult), 'VE')
            ve = p.op('vector', lambda e, dc=dc: e.tensor_tensor(sqb[:, dc * Tc:(dc + 1) * Tc],
                                                                 sqb[:, dc * Tc:(dc + 1) * Tc],
                                                                 bank[4][:, 0:Tc], op=ALU.subtract), 'VE')
            p.wait('scalar', 'VE', ve)
            se = p.op('scalar', lambda e, dc=dc: e.activation(tout(dc), sqb[:, dc * Tc:(dc + 1) * Tc],
                                                              AF.Identity,
                                                              bias=bv[:, base + dc:base + dc + 1],
                                                              scale=sv[:, base + dc:base + dc + 1]), 'SE')
        p.set_bank(3, 'VE')
        p.set_bank(4, 'VE')
        p.wait('vector', 'SE', se)
        return se

    with ExitStack() as ps0:
        sb0 = lambda n, s, d=F32: ps0.enter_context(nc.sbuf_tensor(n, s, d))
        xa = sb0("xa", [128, 1024]); ua = sb0("ua", [128, 1024])
        tm = sb0("tm", [128, 2048]); ustg = sb0("ustg", [128, 4096])
        for tg in range(8):
            for j in range(4):
                tc = tg * 4 + j
                v = p.dma('sync', xa[:], xin[tc * 128:(tc + 1) * 128, :], sem='p0x')
                p.wait('vector', 'p0x', v)
                ve = ln_tok(xa, ua, LNE_T, tm)
                p.wait('tensor', 'VE', ve)
                for g in range(2):
                    p.use_bank(g)
                    for jj in range(4):
                        dc = g * 4 + jj
                        te = p.op('tensor', lambda e, dc=dc, jj=jj, g=g: e.transpose(
                            bank[g][:, jj * 128:(jj + 1) * 128],
                            ua[:, dc * 128:(dc + 1) * 128], ident[:]), 'TE')
                    p.wait('scalar', 'TE', te)
                    for jj in range(4):
                        dc = g * 4 + jj
                        se = p.op('scalar', lambda e, dc=dc, j=j, jj=jj, g=g: e.copy(
                            ustg[:, dc * 512 + j * 128: dc * 512 + (j + 1) * 128],
                            bank[g][:, jj * 128:(jj + 1) * 128]), 'SE')
                    p.set_bank(g, 'SE')
                p.barrier(('sync', 'vector', 'scalar', 'tensor'))
            for dc in range(8):
                v = p.dma('sync', uA[dc * 128:(dc + 1) * 128, tg * 512:(tg + 1) * 512],
                          ustg[:, dc * 512:(dc + 1) * 512], sem='p0s')
            p.wait('scalar', 'p0s', v)
            p.barrier(('sync', 'scalar'))
        p.barrier()

    if stage == "p0":
        ov = out.rearrange("t d -> (t d)").rearrange("(d t) -> d t", d=D)
        for dc in range(8):
            p.dma('sync', ov[dc * 128:(dc + 1) * 128, :], uA[dc * 128:(dc + 1) * 128, :])
        p.wait('sync', 'LD', p.cur('LD'))
        return finish(nc, ges, p)

    ucur, unxt = uA, uB
    with ExitStack() as psa:
        for blk in range(NBLK):
            b8 = blk * 8
            b32 = blk * 32
            with ExitStack() as ps1:
                sb1 = lambda n, s, d=F32: ps1.enter_context(nc.sbuf_tensor(n, s, d))
                wA = sb1(f"wA{blk}", [128, 8192], BF16)
                wB = sb1(f"wB{blk}", [128, 8192], BF16)
                ub2 = [sb1(f"ubf{blk}_{i}", [128, 4096], BF16) for i in range(2)]
                vbf = sb1(f"vbf{blk}", [128, 4096], BF16)
                uf2 = [sb1(f"ufp{blk}_{i}", [128, 4096]) for i in range(2)]
                t1b = sb1(f"t1b{blk}", [128, 4096])
                sqa = sb1(f"sqa{blk}", [128, 4096])
                pscr = sb1(f"pscr{blk}", [128, 3072])

                def scrp(par, i):
                    return pscr[:, (par * 3 + i) * 512:(par * 3 + i + 1) * 512]

                for c in range(8):
                    p.dma('gpsimd', wA[:, c * 1024:(c + 1) * 1024], aWk[blk, c * 128:(c + 1) * 128, :])
                    gv = p.dma('gpsimd', wB[:, c * 1024:(c + 1) * 1024], aWv[blk, c * 128:(c + 1) * 128, :])
                p.op('gpsimd', lambda e: e.memset(kvacc[:], 0.0), 'GE')
                ge = p.op('gpsimd', lambda e: e.memset(ksacc[:], 0.0), 'GE')
                p.wait('tensor', 'GD', gv)
                p.wait('vector', 'GE', ge)

                def load_u(dst, tc, eng='gpsimd', sem=None):
                    for dc in range(8):
                        gv = p.dma(eng, dst[:, dc * 512:(dc + 1) * 512],
                                   ucur[dc * 128:(dc + 1) * 128, tc * 512:(tc + 1) * 512],
                                   sem=sem)
                    return gv

                ubsem = ['ubA', 'ubB']
                ub_ld = [None, None]
                ub_te = [0, 0]
                ub_ld[0] = load_u(ub2[0], 0, sem=ubsem[0])
                ve_par = [0, 0]
                for tc in range(8):
                    bi = tc // 4
                    par_u = tc % 2
                    if tc + 1 < 8:
                        p.wait('gpsimd', 'TE', ub_te[1 - par_u])
                        ub_ld[1 - par_u] = load_u(ub2[1 - par_u], tc + 1, sem=ubsem[1 - par_u])
                    ubf = ub2[par_u]
                    p.wait('tensor', ubsem[par_u], ub_ld[par_u])
                    for oc in range(8):
                        par = oc % 2
                        bk_k = par * 2
                        bk_v = bk_k + 1
                        p.use_bank(bk_k)
                        for dc in range(8):
                            te = p.op('tensor', lambda e, dc=dc, oc=oc, bk_k=bk_k, ubf=ubf: e.matmul(
                                bank[bk_k][:], wA[:, dc * 1024 + oc * 128: dc * 1024 + (oc + 1) * 128],
                                ubf[:, dc * 512:(dc + 1) * 512], start=(dc == 0), stop=(dc == 7)), 'TE')
                        te_k = te
                        p.use_bank(bk_v)
                        for dc in range(8):
                            te = p.op('tensor', lambda e, dc=dc, oc=oc, bk_v=bk_v, ubf=ubf: e.matmul(
                                bank[bk_v][:], wB[:, dc * 1024 + oc * 128: dc * 1024 + (oc + 1) * 128],
                                ubf[:, dc * 512:(dc + 1) * 512], start=(dc == 0), stop=(dc == 7)), 'TE')
                        te_v = te
                        ub_te[par_u] = te_v
                        p.wait('scalar', 'VE', ve_par[par])
                        p.wait('scalar', 'TE', te_k)
                        se = p.op('scalar', lambda e, oc=oc, bk_k=bk_k, par=par: e.activation(
                            scrp(par, 0), bank[bk_k][:], AF.Relu, bias=vk[:, b8 + oc:b8 + oc + 1]), 'SE')
                        se = p.op('scalar', lambda e, oc=oc, bk_k=bk_k, par=par: e.activation(
                            scrp(par, 1), bank[bk_k][:], AF.Relu, bias=vnk[:, b8 + oc:b8 + oc + 1],
                            scale=-1.0), 'SE')
                        p.set_bank(bk_k, 'SE')
                        se = p.op('scalar', lambda e, par=par: e.activation(scrp(par, 1), scrp(par, 1),
                                                                            AF.Exp, scale=-1.0), 'SE')
                        p.wait('scalar', 'TE', te_v)
                        se = p.op('scalar', lambda e, oc=oc, bk_v=bk_v, par=par: e.activation(
                            scrp(par, 2), bank[bk_v][:], AF.Identity, bias=vv[:, b8 + oc:b8 + oc + 1]), 'SE')
                        p.set_bank(bk_v, 'SE')
                        p.wait('vector', 'SE', se)
                        ve = p.op('vector', lambda e, par=par: e.tensor_tensor(
                            scrp(par, 0), scrp(par, 0), scrp(par, 1), op=ALU.add), 'VE')
                        ve = p.op('vector', lambda e, par=par: e.reduce_sum(r128b[:], scrp(par, 0),
                                                                            axis=AX), 'VE')
                        ve = p.op('vector', lambda e, oc=oc, bi=bi: e.tensor_tensor(
                            ksacc[:, bi * 8 + oc:bi * 8 + oc + 1], ksacc[:, bi * 8 + oc:bi * 8 + oc + 1],
                            r128b[:], op=ALU.add), 'VE')
                        ve = p.op('vector', lambda e, par=par: e.tensor_tensor(
                            scrp(par, 0), scrp(par, 0), scrp(par, 2), op=ALU.mult), 'VE')
                        ve = p.op('vector', lambda e, par=par: e.reduce_sum(r128a[:], scrp(par, 0),
                                                                            axis=AX), 'VE')
                        ve = p.op('vector', lambda e, oc=oc, bi=bi: e.tensor_tensor(
                            kvacc[:, bi * 8 + oc:bi * 8 + oc + 1], kvacc[:, bi * 8 + oc:bi * 8 + oc + 1],
                            r128a[:], op=ALU.add), 'VE')
                        ve_par[par] = ve
                p.wait('gpsimd', 'TE', p.cur('TE'))
                for c in range(8):
                    p.dma('gpsimd', wA[:, c * 1024:(c + 1) * 1024], aWq[blk, c * 128:(c + 1) * 128, :])
                    gv = p.dma('gpsimd', wB[:, c * 1024:(c + 1) * 1024], aWo[blk, c * 128:(c + 1) * 128, :])
                p.wait('vector', 'VE', p.cur('VE'))
                p.wait('tensor', 'GD', gv)
                ufsem = ['ufA', 'ufB']
                stsem = ['stA', 'stB']
                ub_ld = [None, None]
                uf_ld = [None, None]
                ub_te = [0, 0]
                uf_st = [0, 0]
                uf_ve = [0, 0]
                ub_ld[0] = load_u(ub2[0], 0, sem=ubsem[0])
                uf_ld[0] = load_u(uf2[0], 0, 'sync', sem=ufsem[0])
                ve_par = [p.cur('VE'), p.cur('VE')]
                ln_tail_te = 0
                for tc in range(8):
                    bi = tc // 4
                    par_u = tc % 2
                    if tc + 1 < 8:
                        p.wait('gpsimd', 'TE', ub_te[1 - par_u])
                        ub_ld[1 - par_u] = load_u(ub2[1 - par_u], tc + 1, sem=ubsem[1 - par_u])
                        p.wait('sync', stsem[1 - par_u], uf_st[1 - par_u])
                        p.wait('sync', 'VE', uf_ve[1 - par_u])
                        uf_ld[1 - par_u] = load_u(uf2[1 - par_u], tc + 1, 'sync',
                                                  sem=ufsem[1 - par_u])
                    ubf = ub2[par_u]
                    ufp = uf2[par_u]
                    p.wait('tensor', ubsem[par_u], ub_ld[par_u])
                    for oc in range(8):
                        par = oc % 2
                        bk_q = par
                        p.use_bank(bk_q)
                        for dc in range(8):
                            te = p.op('tensor', lambda e, dc=dc, oc=oc, bk_q=bk_q, ubf=ubf: e.matmul(
                                bank[bk_q][:], wA[:, dc * 1024 + oc * 128: dc * 1024 + (oc + 1) * 128],
                                ubf[:, dc * 512:(dc + 1) * 512], start=(dc == 0), stop=(dc == 7)), 'TE')
                        ub_te[par_u] = te
                        p.wait('scalar', 'VE', ve_par[par])
                        p.wait('scalar', 'TE', te)
                        se = p.op('scalar', lambda e, oc=oc, bk_q=bk_q, par=par: e.activation(
                            scrp(par, 0), bank[bk_q][:], AF.Relu, bias=vq[:, b8 + oc:b8 + oc + 1]), 'SE')
                        se = p.op('scalar', lambda e, oc=oc, bk_q=bk_q, par=par: e.activation(
                            scrp(par, 1), bank[bk_q][:], AF.Relu, bias=vnq[:, b8 + oc:b8 + oc + 1],
                            scale=-1.0), 'SE')
                        p.set_bank(bk_q, 'SE')
                        se = p.op('scalar', lambda e, par=par: e.activation(scrp(par, 1), scrp(par, 1),
                                                                            AF.Exp, scale=-1.0), 'SE')
                        p.wait('vector', 'SE', se)
                        ve = p.op('vector', lambda e, par=par: e.tensor_tensor(
                            scrp(par, 0), scrp(par, 0), scrp(par, 1), op=ALU.add), 'VE')
                        ve = p.op('vector', lambda e, oc=oc, bi=bi, par=par: e.tensor_scalar(
                            scrp(par, 1), scrp(par, 0), ksacc[:, bi * 8 + oc:bi * 8 + oc + 1], 1e-6,
                            op0=ALU.mult, op1=ALU.add), 'VE')
                        ve = p.op('vector', lambda e, par=par: e.reciprocal(scrp(par, 2),
                                                                            scrp(par, 1)), 'VE')
                        ve = p.op('vector', lambda e, oc=oc, bi=bi, par=par: e.scalar_tensor_tensor(
                            vbf[:, oc * 512:(oc + 1) * 512], scrp(par, 0),
                            kvacc[:, bi * 8 + oc:bi * 8 + oc + 1], scrp(par, 2),
                            op0=ALU.mult, op1=ALU.mult), 'VE')
                        ve_par[par] = ve
                    p.wait('tensor', 'VE', ve)
                    p.wait('vector', ufsem[par_u], uf_ld[par_u])
                    for dc in range(8):
                        par = dc % 2
                        bk_a = 2 + par
                        p.use_bank(bk_a)
                        for ac in range(8):
                            te = p.op('tensor', lambda e, ac=ac, dc=dc, bk_a=bk_a: e.matmul(
                                bank[bk_a][:], wB[:, ac * 1024 + dc * 128: ac * 1024 + (dc + 1) * 128],
                                vbf[:, ac * 512:(ac + 1) * 512], start=(ac == 0), stop=(ac == 7)), 'TE')
                        p.wait('scalar', 'VE', ve_par[par])
                        p.wait('scalar', 'TE', te)
                        se = p.op('scalar', lambda e, dc=dc, bk_a=bk_a, par=par: e.activation(
                            scrp(par, 0), bank[bk_a][:], AF.Gelu_apprx_tanh,
                            bias=vo[:, b8 + dc:b8 + dc + 1]), 'SE')
                        p.set_bank(bk_a, 'SE')
                        se = p.op('scalar', lambda e, par=par: e.activation(
                            scrp(par, 1), scrp(par, 0), AF.Gelu_apprx_tanh), 'SE')
                        p.wait('vector', 'SE', se)
                        p.wait('vector', 'TE', ln_tail_te)
                        ve = p.op('vector', lambda e, dc=dc, par=par, ufp=ufp: e.tensor_tensor(
                            t1b[:, dc * 512:(dc + 1) * 512], scrp(par, 1),
                            ufp[:, dc * 512:(dc + 1) * 512], op=ALU.add), 'VE')
                        ve_par[par] = ve
                        uf_ve[par_u] = ve
                    ln_feat(lambda dc, t1b=t1b: t1b[:, dc * 512:(dc + 1) * 512], 512, vl1s, vl1b,
                            b8, LNE_A, lambda dc, ufp=ufp: ufp[:, dc * 512:(dc + 1) * 512], sqa)
                    ln_tail_te = p.cur('TE')
                    p.wait('sync', 'SE', p.cur('SE'))
                    for dc in range(8):
                        v = p.dma('sync', x1T[dc * 128:(dc + 1) * 128, tc * 512:(tc + 1) * 512],
                                  ufp[:, dc * 512:(dc + 1) * 512], sem=stsem[par_u])
                    uf_st[par_u] = v
                p.barrier()
            with ExitStack() as ps2:
                sb2 = lambda n, s, d=F32: ps2.enter_context(nc.sbuf_tensor(n, s, d))
                w1sb = sb2(f"w1sb{blk}", [128, 32768], BF16)
                w2sb = sb2(f"w2sb{blk}", [128, 32768], BF16)
                xb2 = [sb2(f"x1bf{blk}_{i}", [128, 2048], BF16) for i in range(2)]
                hbf = sb2(f"hbf{blk}", [128, 8192], BF16)
                ufp = sb2(f"ufp2{blk}", [128, 2048])
                t1b = sb2(f"t1b2{blk}", [128, 2048])
                sqa = sb2(f"sqa2{blk}", [128, 2048])
                p2scr = sb2(f"p2scr{blk}", [128, 512])
                for c in range(8):
                    gv = p.dma('gpsimd', w1sb[:, c * 4096:(c + 1) * 4096],
                               aW1[blk, c * 128:(c + 1) * 128, :])
                for c in range(32):
                    gv = p.dma('gpsimd', w2sb[:, c * 1024:(c + 1) * 1024],
                               aW2[blk, c * 128:(c + 1) * 128, :])
                p.wait('tensor', 'GD', gv)

                def load_x1(dst, tc, eng, sem=None, ncols=256):
                    for dc in range(8):
                        gv = p.dma(eng, dst[:, dc * ncols:(dc + 1) * ncols],
                                   x1T[dc * 128:(dc + 1) * 128, tc * ncols:(tc + 1) * ncols],
                                   sem=sem)
                    return gv

                xbsem = ['xbA', 'xbB']
                xb_ld = [None, None]
                xb_te = [0, 0]
                xb_ld[0] = load_x1(xb2[0], 0, 'gpsimd', sem=xbsem[0])
                uf_ld = load_x1(ufp, 0, 'sync', sem='ufl')
                uf_st = 0
                uf_ve = 0
                ve_par = [0, 0]
                ln_tail_te = 0
                hbf_te = 0
                for tc in range(16):
                    par_x = tc % 2
                    if tc + 1 < 16:
                        p.wait('gpsimd', 'TE', xb_te[1 - par_x])
                        xb_ld[1 - par_x] = load_x1(xb2[1 - par_x], tc + 1, 'gpsimd',
                                                   sem=xbsem[1 - par_x])
                    x1bf = xb2[par_x]
                    p.wait('tensor', xbsem[par_x], xb_ld[par_x])
                    for fc in range(32):
                        bk_h = fc % 2
                        p.use_bank(bk_h)
                        for dc in range(8):
                            te = p.op('tensor', lambda e, dc=dc, fc=fc, bk_h=bk_h, x1bf=x1bf: e.matmul(
                                bank[bk_h][:, 0:256],
                                w1sb[:, dc * 4096 + fc * 128: dc * 4096 + (fc + 1) * 128],
                                x1bf[:, dc * 256:(dc + 1) * 256], start=(dc == 0), stop=(dc == 7)), 'TE')
                        xb_te[par_x] = te
                        p.wait('scalar', 'TE', te)
                        if fc == 0:
                            p.wait('scalar', 'TE', hbf_te)
                        se = p.op('scalar', lambda e, fc=fc, bk_h=bk_h: e.activation(
                            hbf[:, fc * 256:(fc + 1) * 256], bank[bk_h][:, 0:256], AF.Gelu_apprx_tanh,
                            bias=v1[:, b32 + fc:b32 + fc + 1]), 'SE')
                        p.set_bank(bk_h, 'SE')
                    p.wait('tensor', 'SE', p.cur('SE'))
                    p.wait('vector', 'ufl', uf_ld)
                    for dc in range(8):
                        par = dc % 2
                        bk_f = 2 + par
                        p.use_bank(bk_f)
                        for fc in range(32):
                            te = p.op('tensor', lambda e, fc=fc, dc=dc, bk_f=bk_f: e.matmul(
                                bank[bk_f][:, 0:256],
                                w2sb[:, fc * 1024 + dc * 128: fc * 1024 + (dc + 1) * 128],
                                hbf[:, fc * 256:(fc + 1) * 256], start=(fc == 0), stop=(fc == 31)), 'TE')
                        if dc == 7:
                            hbf_te = te
                        p.wait('scalar', 'VE', ve_par[par])
                        p.wait('scalar', 'TE', te)
                        se = p.op('scalar', lambda e, dc=dc, bk_f=bk_f, par=par: e.activation(
                            p2scr[:, par * 256:(par + 1) * 256], bank[bk_f][:, 0:256],
                            AF.Gelu_apprx_tanh, bias=v2[:, b8 + dc:b8 + dc + 1]), 'SE')
                        p.set_bank(bk_f, 'SE')
                        p.wait('vector', 'SE', se)
                        p.wait('vector', 'TE', ln_tail_te)
                        ve = p.op('vector', lambda e, dc=dc, par=par: e.tensor_tensor(
                            t1b[:, dc * 256:(dc + 1) * 256], p2scr[:, par * 256:(par + 1) * 256],
                            ufp[:, dc * 256:(dc + 1) * 256], op=ALU.add), 'VE')
                        ve_par[par] = ve
                        uf_ve = ve
                    ln_feat(lambda dc, t1b=t1b: t1b[:, dc * 256:(dc + 1) * 256], 256, vl2s, vl2b,
                            b8, LNE_A, lambda dc, ufp=ufp: ufp[:, dc * 256:(dc + 1) * 256], sqa)
                    ln_tail_te = p.cur('TE')
                    p.wait('sync', 'SE', p.cur('SE'))
                    for dc in range(8):
                        v = p.dma('sync', unxt[dc * 128:(dc + 1) * 128, tc * 256:(tc + 1) * 256],
                                  ufp[:, dc * 256:(dc + 1) * 256], sem='ufs')
                    uf_st = v
                    if tc + 1 < 16:
                        p.wait('sync', 'ufs', uf_st)
                        uf_ld = load_x1(ufp, tc + 1, 'sync', sem='ufl')
                p.barrier()
            ucur, unxt = unxt, ucur
            if stage == f"blk{blk}":
                ov = out.rearrange("t d -> (t d)").rearrange("(d t) -> d t", d=D)
                for dc in range(8):
                    p.dma('sync', ov[dc * 128:(dc + 1) * 128, :], ucur[dc * 128:(dc + 1) * 128, :])
                p.wait('sync', 'LD', p.cur('LD'))
                return finish(nc, ges, p)

    with ExitStack() as psm:
        sbm = lambda n, s, d=F32: psm.enter_context(nc.sbuf_tensor(n, s, d))
        xa = sbm("mxa", [128, 1024]); xb = sbm("mxb", [128, 1024])
        ua = sbm("mua", [128, 1024]); tm = sbm("mtm", [128, 2048])
        ustg = sbm("mustg", [128, 2048])
        for tc in range(32):
            vxa = p.dma('sync', xa[:], xin[tc * 128:(tc + 1) * 128, :], sem='pmx')
            for dc in range(8):
                gv = p.dma('gpsimd', ustg[:, dc * 128:(dc + 1) * 128],
                           ucur[dc * 128:(dc + 1) * 128, tc * 128:(tc + 1) * 128], sem='pmg')
            p.wait('tensor', 'pmg', gv)
            for g in range(2):
                p.use_bank(g)
                for jj in range(4):
                    dc = g * 4 + jj
                    te = p.op('tensor', lambda e, dc=dc, jj=jj, g=g: e.transpose(
                        bank[g][:, jj * 128:(jj + 1) * 128],
                        ustg[:, dc * 128:(dc + 1) * 128], ident[:]), 'TE')
                p.wait('scalar', 'TE', te)
                for jj in range(4):
                    dc = g * 4 + jj
                    se = p.op('scalar', lambda e, dc=dc, jj=jj, g=g: e.copy(
                        xb[:, dc * 128:(dc + 1) * 128], bank[g][:, jj * 128:(jj + 1) * 128]), 'SE')
                p.set_bank(g, 'SE')
            p.wait('vector', 'SE', se)
            p.wait('vector', 'pmx', vxa)
            ve = p.op('vector', lambda e: e.tensor_tensor(xa[:], xa[:], xb[:], op=ALU.add), 'VE')
            p.wait('sync', 'VE', ve)
            v = p.dma('sync', xmid[tc * 128:(tc + 1) * 128, :], xa[:], sem='pms')
            ve = ln_tok(xa, ua, LNE_T, tm)
            p.wait('sync', 'VE', ve)
            v = p.dma('sync', u2tok[tc * 128:(tc + 1) * 128, :], ua[:], sem='pms')
            p.wait('tensor', 'VE', ve)
            for g in range(2):
                bk = 2 + g
                p.use_bank(bk)
                for jj in range(4):
                    dc = g * 4 + jj
                    te = p.op('tensor', lambda e, dc=dc, jj=jj, bk=bk: e.transpose(
                        bank[bk][:, jj * 128:(jj + 1) * 128],
                        ua[:, dc * 128:(dc + 1) * 128], ident[:]), 'TE')
                p.wait('scalar', 'TE', te)
                for jj in range(4):
                    dc = g * 4 + jj
                    se = p.op('scalar', lambda e, dc=dc, jj=jj, bk=bk: e.copy(
                        ustg[:, dc * 128:(dc + 1) * 128], bank[bk][:, jj * 128:(jj + 1) * 128]), 'SE')
                p.set_bank(bk, 'SE')
            p.wait('sync', 'SE', se)
            for dc in range(8):
                v = p.dma('sync', u2T[dc * 128:(dc + 1) * 128, tc * 128:(tc + 1) * 128],
                          ustg[:, dc * 128:(dc + 1) * 128], sem='pms')
            p.wait('gpsimd', 'pms', v)
            p.barrier(('sync', 'gpsimd', 'tensor', 'scalar', 'vector'))
        p.barrier()

    if stage == "pmid":
        with ExitStack() as pst_:
            xa = pst_.enter_context(nc.sbuf_tensor("dxa", [128, 1024], F32))
            for tc in range(32):
                v = p.dma('sync', xa[:], u2tok[tc * 128:(tc + 1) * 128, :])
                p.wait('sync', 'LD', v)
                v = p.dma('sync', out[tc * 128:(tc + 1) * 128, :], xa[:])
                p.wait('sync', 'LD', v)
        return finish(nc, ges, p)

    with ExitStack() as psf:
        sbf = lambda n, s, d=F32: psf.enter_context(nc.sbuf_tensor(n, s, d))
        qTb = sbf("qTb", [128, 16384], BF16)
        kTb = sbf("kTb", [128, 16384], BF16)
        vtok = sbf("vtok", [128, 16640], BF16)
        l8 = sbf("l8", [128, 8]); e8 = sbf("e8", [128, 8])
        w8a = sbf("w8a", [128, 8]); w8b = sbf("w8b", [128, 8])

        for tt in range(16):
            for h in range(NH):
                ge = p.op('gpsimd', lambda e, tt=tt, h=h: e.memset(
                    vtok[:, tt * 1040 + h * 65 + 64: tt * 1040 + h * 65 + 65], 1.0), 'GE')
        ge_ones = p.cur('GE')

        def transpose_w(dst, src, wstg):
            for rc in range(8):
                gv = p.dma('gpsimd', wstg[:], src[rc * 128:(rc + 1) * 128, :], sem='fw')
                p.wait('tensor', 'fw', gv)
                for cc in range(8):
                    te = p.op('tensor', lambda e, cc=cc: e.transpose(
                        pbf[:, (cc % 4) * 128:(cc % 4 + 1) * 128],
                        wstg[:, cc * 128:(cc + 1) * 128], identb[:]), 'TE')
                    p.wait('scalar', 'TE', te)
                    se = p.op('scalar', lambda e, cc=cc, rc=rc, dst=dst: e.copy(
                        dst[:, cc * 1024 + rc * 128: cc * 1024 + (rc + 1) * 128],
                        pbf[:, (cc % 4) * 128:(cc % 4 + 1) * 128]), 'SE')
                    p.wait('tensor', 'SE', se)
                p.wait('gpsimd', 'SE', se)

        if True:
            for b in range(2):
              with ExitStack() as psf1:
                sbf1 = lambda n, s, d=F32: psf1.enter_context(nc.sbuf_tensor(n, s, d))
                wqT = sbf1(f"wqT{b}", [128, 8192], BF16)
                wkT = sbf1(f"wkT{b}", [128, 8192], BF16)
                wvT = sbf1(f"wvT{b}", [128, 8192], BF16)
                wstg = sbf1(f"wstg{b}", [128, 1024], BF16)
                u2bf = sbf1(f"u2bf{b}", [128, 4096], BF16)
                fbv_row = sbf1(f"fbv_row{b}", [1, 1024])
                p.dma('sync', fbv_row[:], fbv.rearrange("(a b) -> a b", a=1))
                transpose_w(wqT, fWq, wstg)
                transpose_w(wkT, fWk, wstg)
                transpose_w(wvT, fWv, wstg)
                p.barrier()
                for tc in range(4):
                    for dc in range(8):
                        gv = p.dma('gpsimd', u2bf[:, dc * 512:(dc + 1) * 512],
                                   u2T[dc * 128:(dc + 1) * 128, (b * 4 + tc) * 512:(b * 4 + tc + 1) * 512],
                                   sem='fu')
                    p.wait('tensor', 'fu', gv)
                    for pi, wT, bias_c, dstT in ((0, wqT, fvq, qTb), (1, wkT, fvk, kTb)):
                        for oc in range(8):
                            bk_x = pi * 2 + oc % 2
                            for dc in range(8):
                                te = p.op('tensor', lambda e, dc=dc, oc=oc, wT=wT, bk_x=bk_x: e.matmul(
                                    bank[bk_x][:], wT[:, dc * 1024 + oc * 128: dc * 1024 + (oc + 1) * 128],
                                    u2bf[:, dc * 512:(dc + 1) * 512], start=(dc == 0), stop=(dc == 7)), 'TE')
                            p.wait('scalar', 'TE', te)
                            se = p.op('scalar', lambda e, oc=oc, tc=tc, bias_c=bias_c, dstT=dstT, bk_x=bk_x:
                                      e.activation(dstT[:, oc * 2048 + tc * 512: oc * 2048 + (tc + 1) * 512],
                                                   bank[bk_x][:], AF.Identity, bias=bias_c[:, oc:oc + 1]), 'SE')
                            p.wait('tensor', 'SE', se)
                    p.wait('scalar', 'GE', ge_ones)
                    for ts_ in range(4):
                        tt = tc * 4 + ts_
                        for half in range(2):
                            bk_x = 4 + half
                            for dc in range(8):
                                te = p.op('tensor', lambda e, dc=dc, ts_=ts_, half=half, bk_x=bk_x: e.matmul(
                                    bank[bk_x][:], u2bf[:, dc * 512 + ts_ * 128: dc * 512 + (ts_ + 1) * 128],
                                    wvT[:, dc * 1024 + half * 512: dc * 1024 + (half + 1) * 512],
                                    start=(dc == 0), stop=False), 'TE')
                            te = p.op('tensor', lambda e, half=half, bk_x=bk_x: e.matmul(
                                bank[bk_x][:], onesrow[0:1, :], fbv_row[0:1, half * 512:(half + 1) * 512],
                                start=False, stop=True), 'TE')
                            p.wait('scalar', 'TE', te)
                            for j in range(8):
                                h = half * 8 + j
                                se = p.op('scalar', lambda e, tt=tt, h=h, j=j, bk_x=bk_x: e.copy(
                                    vtok[:, tt * 1040 + h * 65: tt * 1040 + h * 65 + 64],
                                    bank[bk_x][:, j * 64:(j + 1) * 64]), 'SE')
                            p.wait('tensor', 'SE', se)
                    p.barrier(('gpsimd', 'tensor', 'scalar', 'vector', 'sync'))
              with ExitStack() as psf2:
                sbf2 = lambda n, s, d=F32: psf2.enter_context(nc.sbuf_tensor(n, s, d))
                pbuf = sbf2(f"pbuf{b}", [128, 8192], BF16)
                hstg2 = sbf2(f"hstg{b}", [128, 8192])
                masks = sbf2(f"masks{b}", [128, 2048], BF16)
                zstg = sbf2(f"zstg{b}", [1, 1024]); z16 = sbf2(f"z16{b}", [16, 1024])
                r16 = sbf2(f"r16{b}", [16, 1024])
                rstage = sbf2(f"rstage{b}", [1, 1024])
                rbc = scr[1]
                v = p.dma('sync', masks[:], c_mask)
                p.wait('vector', 'LD', v)
                hstg_te = [0, 0]
                hstg_fh = [0, 0]
                z_ve = [0, 0]
                r_te = [0, 0]
                for qb in range(4):
                    nreal = 4 * qb + 4
                    par = qb % 2
                    hoff = par * 4096
                    zoff = par * 512
                    exp_last = 0
                    for h in range(NH):
                        hb = (64 * h) % 128
                        hc = h // 2
                        bk_pv = 2 + h % 2
                        for tt in range(nreal):
                            d_off = 512 * qb - 128 * tt
                            bk_s = tt % 2
                            p.use_bank(bk_s)
                            te = p.op('tensor', lambda e, tt=tt, qb=qb, bk_s=bk_s, hb=hb, hc=hc: e.matmul(
                                bank[bk_s][:], kTb[hb:hb + 64, hc * 2048 + tt * 128: hc * 2048 + (tt + 1) * 128],
                                qTb[hb:hb + 64, hc * 2048 + qb * 512: hc * 2048 + (qb + 1) * 512],
                                start=True, stop=True), 'TE')
                            if d_off >= 128:
                                p.wait('scalar', 'TE', te)
                                se = p.op('scalar', lambda e, tt=tt, bk_s=bk_s: e.activation(
                                    pbuf[:, tt * 512:(tt + 1) * 512], bank[bk_s][:], AF.Exp,
                                    scale=1.0 / 32.0), 'SE')
                                p.set_bank(bk_s, 'SE')
                            else:
                                c = (-d_off) // 128
                                sp = tt % 2
                                p.wait('vector', 'TE', te)
                                p.wait('vector', 'SE', exp_last)
                                ve = p.op('vector', lambda e, c=c, bk_s=bk_s, sp=sp: e.tensor_tensor(
                                    scr[sp * 2][:], bank[bk_s][:], masks[:, c * 512:(c + 1) * 512],
                                    op=ALU.mult), 'VE')
                                p.set_bank(bk_s, 'VE')
                                p.wait('scalar', 'VE', ve)
                                se = p.op('scalar', lambda e, tt=tt, sp=sp: e.activation(
                                    pbuf[:, tt * 512:(tt + 1) * 512], scr[sp * 2][:], AF.Exp,
                                    scale=1.0 / 32.0), 'SE')
                            exp_last = se
                        p.use_bank(bk_pv)
                        order = list(range(nreal, 16)) + list(range(nreal))
                        for i, tt in enumerate(order):
                            if i == (16 - nreal):
                                p.wait('tensor', 'SE', exp_last)
                            if tt < nreal:
                                rhsf = (lambda e, tt=tt: pbuf[:, tt * 512:(tt + 1) * 512])
                            else:
                                rhsf = (lambda e: allonesb[:])
                            te = p.op('tensor', lambda e, tt=tt, h=h, rhsf=rhsf, i=i, bk_pv=bk_pv,
                                      nr=nreal: e.matmul(
                                bank[bk_pv][0:65, :], vtok[:, tt * 1040 + h * 65: tt * 1040 + h * 65 + 65],
                                rhsf(e), start=(i == 0), stop=(i == 15)), 'TE')
                        p.wait('scalar', 'TE', te)
                        if h == 0:
                            p.wait('scalar', 'TE', hstg_te[par])
                            p.wait('scalar', 'fh', hstg_fh[par])
                            p.wait('scalar', 'VE', z_ve[par])
                        se = p.op('scalar', lambda e, zoff=zoff, bk_pv=bk_pv: e.copy(
                            zstg[0:1, zoff:zoff + 512], bank[bk_pv][64:65, :]), 'SE')
                        se2 = p.op('scalar', lambda e, hb=hb, hc=hc, hoff=hoff, bk_pv=bk_pv: e.copy(
                            hstg2[hb:hb + 64, hoff + hc * 512: hoff + (hc + 1) * 512],
                            bank[bk_pv][0:64, :]), 'SE')
                        p.set_bank(bk_pv, 'SE')
                        p.wait('gpsimd', 'SE', se2)
                        gv = p.dma('gpsimd', z16[h:h + 1, zoff:zoff + 512],
                                   zstg[0:1, zoff:zoff + 512], sem='fz')
                        p.wait('scalar', 'fz', gv)
                    p.wait('vector', 'fz', p.cur('fz'))
                    ve = p.op('vector', lambda e, zoff=zoff: e.reciprocal(
                        r16[:, zoff:zoff + 512], z16[:, zoff:zoff + 512]), 'VE')
                    z_ve[par] = ve
                    p.wait('gpsimd', 'VE', ve)
                    for h in range(NH):
                        hb = (64 * h) % 128
                        hc = h // 2
                        p.wait('gpsimd', 'TE', r_te[par])
                        gv = p.dma('gpsimd', rstage[0:1, zoff:zoff + 512],
                                   r16[h:h + 1, zoff:zoff + 512], sem='fr')
                        p.wait('tensor', 'fr', gv)
                        p.use_bank(4)
                        te = p.op('tensor', lambda e, hb=hb, zoff=zoff: e.matmul(
                            bank[4][hb:hb + 64, :], onesrow[0:1, 0:64],
                            rstage[0:1, zoff:zoff + 512], start=True, stop=True), 'TE')
                        r_te[par] = te
                        p.wait('scalar', 'TE', te)
                        se = p.op('scalar', lambda e, hb=hb: e.copy(rbc[hb:hb + 64, :],
                                                                    bank[4][hb:hb + 64, :]), 'SE')
                        p.set_bank(4, 'SE')
                        p.wait('vector', 'SE', se)
                        ve = p.op('vector', lambda e, hb=hb, hc=hc, hoff=hoff: e.tensor_tensor(
                            hstg2[hb:hb + 64, hoff + hc * 512: hoff + (hc + 1) * 512],
                            hstg2[hb:hb + 64, hoff + hc * 512: hoff + (hc + 1) * 512],
                            rbc[hb:hb + 64, :], op=ALU.mult), 'VE')
                        p.wait('scalar', 'VE', ve)
                    nrm_ve = p.cur('VE')
                    p.wait('gpsimd', 'VE', nrm_ve)
                    for dc in range(8):
                        gv = p.dma('gpsimd', hidT[dc * 128:(dc + 1) * 128,
                                                  (b * 4 + qb) * 512:(b * 4 + qb + 1) * 512],
                                   hstg2[:, hoff + dc * 512: hoff + (dc + 1) * 512], sem='fh')
                    hstg_fh[par] = gv
                    for ts_ in range(4):
                        gtc = b * 16 + qb * 4 + ts_
                        p.use_bank(5)
                        if ts_ == 0:
                            p.wait('tensor', 'VE', nrm_ve)
                        for dc in range(8):
                            te = p.op('tensor', lambda e, dc=dc, ts_=ts_, hoff=hoff: e.matmul(
                                bank[5][:, 0:8],
                                hstg2[:, hoff + dc * 512 + ts_ * 128: hoff + dc * 512 + (ts_ + 1) * 128],
                                wswT[:, dc * 8:(dc + 1) * 8], start=(dc == 0), stop=False), 'TE')
                        te = p.op('tensor', lambda e: e.matmul(bank[5][:, 0:8], onesrow[0:1, :],
                                                               fbsw_row[0:1, :], start=False, stop=True), 'TE')
                        hstg_te[par] = te
                        p.wait('vector', 'TE', te)
                        ve = p.op('vector', lambda e: e.tensor_copy(l8[:], bank[5][:, 0:8]), 'VE')
                        p.set_bank(5, 'VE')
                        ve = p.op('vector', lambda e: e.reduce_max(r128a[:], l8[:], axis=AX, negate=True), 'VE')
                        p.wait('scalar', 'VE', ve)
                        se = p.op('scalar', lambda e: e.activation(e8[:], l8[:], AF.Exp, bias=r128a[:]), 'SE')
                        p.wait('vector', 'SE', se)
                        ve = p.op('vector', lambda e: e.reduce_sum(r128b[:], e8[:], axis=AX), 'VE')
                        ve = p.op('vector', lambda e: e.reciprocal(r128c[:], r128b[:]), 'VE')
                        ve = p.op('vector', lambda e, gtc=gtc: e.tensor_copy(rp_all[:, gtc:gtc + 1],
                                                                             r128c[:]), 'VE')
                        ve = p.op('vector', lambda e: e.tensor_scalar(r128d[:], r128a[:], -1.0, None,
                                                                      op0=ALU.mult), 'VE')
                        ve = p.op('vector', lambda e: e.tensor_scalar(w8a[:], l8[:], r128d[:], None,
                                                                      op0=ALU.is_ge), 'VE')
                        ve = p.op('vector', lambda e: e.tensor_tensor(w8b[:], w8a[:], rm8[:], op=ALU.mult), 'VE')
                        ve = p.op('vector', lambda e: e.tensor_scalar(w8b[:], w8b[:], -1.0, 8.0,
                                                                      op0=ALU.mult, op1=ALU.add), 'VE')
                        ve = p.op('vector', lambda e: e.tensor_reduce(r128d[:], w8b[:], axis=AX,
                                                                      op=ALU.min), 'VE')
                        ve = p.op('vector', lambda e, gtc=gtc: e.tensor_scalar(
                            oh_all[:, gtc * 8:(gtc + 1) * 8], io8[:], r128d[:], None,
                            op0=ALU.is_equal), 'VE')
                        p.wait('tensor', 'VE', ve)
                        p.use_bank(6)
                        te = p.op('tensor', lambda e, gtc=gtc: e.matmul(
                            bank[6][:, 0:8], sltri[:], oh_all[:, gtc * 8:(gtc + 1) * 8],
                            start=True, stop=False), 'TE')
                        te = p.op('tensor', lambda e: e.matmul(bank[6][:, 0:8], onesrow[0:1, :],
                                                               carry[0:1, :], start=False, stop=True), 'TE')
                        p.wait('vector', 'TE', te)
                        ve = p.op('vector', lambda e, gtc=gtc: e.tensor_copy(
                            rank_all[:, gtc * 8:(gtc + 1) * 8], bank[6][:, 0:8]), 'VE')
                        p.set_bank(6, 'VE')
                        p.wait('tensor', 'VE', ve)
                        p.use_bank(6)
                        te = p.op('tensor', lambda e, gtc=gtc: e.matmul(
                            bank[6][0:1, 0:8], e127[:], rank_all[:, gtc * 8:(gtc + 1) * 8],
                            start=True, stop=False), 'TE')
                        te = p.op('tensor', lambda e, gtc=gtc: e.matmul(
                            bank[6][0:1, 0:8], e127[:], oh_all[:, gtc * 8:(gtc + 1) * 8],
                            start=False, stop=True), 'TE')
                        p.wait('vector', 'TE', te)
                        ve = p.op('vector', lambda e: e.tensor_copy(carry[0:1, :],
                                                                    bank[6][0:1, 0:8]), 'VE')
                        p.set_bank(6, 'VE')
                p.barrier()
        p.barrier()

    v = p.dma('sync', agin.ap(), carry[0:1, :])
    p.wait('gpsimd', 'LD', v)
    cc = p.op('gpsimd', lambda e: e.collective_compute(
        "AllGather", ALU.bypass, replica_groups=[list(range(NCORES))],
        ins=[agin.ap().opt()], outs=[agout.ap().opt()]), 'CC')
    p.wait('gpsimd', 'CC', 1)
    gv = p.dma('gpsimd', cnt8[:], agout.ap())
    p.wait('tensor', 'GD', gv)
    te = p.op('tensor', lambda e: e.matmul(bank[5][0:8, 0:1], cnt8[:], rmask[:],
                                           start=True, stop=True), 'TE')
    p.wait('scalar', 'TE', te)
    se = p.op('scalar', lambda e: e.copy(offcol[:], bank[5][0:8, 0:1]), 'SE')
    p.wait('gpsimd', 'SE', se)
    gv = p.dma('gpsimd', offs_row[:], offcol[:])
    p.barrier()

    with ExitStack() as ps3:
        sb3 = lambda n, s, d=F32: ps3.enter_context(nc.sbuf_tensor(n, s, d))
        weT = [sb3(f"weT{i}", [128, 16384], BF16) for i in range(4)]
        wstg3 = sb3("wstg3", [128, 1024], BF16)
        hbf_t2 = [sb3(f"hbf_t{i}", [128, 1024], BF16) for i in range(2)]
        htok = sb3("htok", [128, 1024], BF16)
        yacc2 = [sb3(f"yacc{i}", [128, 1024]) for i in range(2)]
        u2t_t = sb3("u2t_t", [128, 1024])
        xm_t = sb3("xm_t", [128, 1024])
        ua3 = sb3("ua3", [128, 1024])
        tm3 = sb3("tm3", [128, 2048])
        gam_bc = sb3("gam_bc", [128, 1024]); bet_bc = sb3("bet_bc", [128, 1024])
        fbe_sb = sb3("fbe_sb", [8, 1024])
        ct82 = [sb3(f"ct8_{i}", [8, 128]) for i in range(2)]
        keptf = sb3("keptf", [128, 8])
        ctile2 = [sb3(f"ctile{i}", [128, 8]) for i in range(2)]
        offs_bc = sb3("offs_bc", [128, 8])

        p.dma('sync', fbe_sb[:], fbe)
        v = p.dma('sync', strow[0:1, 0:1024], gam.rearrange("(a b) -> a b", a=1))
        p.wait('tensor', 'LD', v)
        te = p.op('tensor', lambda e: e.matmul(bank[0][:], onesrow[0:1, :], strow[0:1, 0:512],
                                               start=True, stop=True), 'TE')
        te = p.op('tensor', lambda e: e.matmul(bank[1][:], onesrow[0:1, :], strow[0:1, 512:1024],
                                               start=True, stop=True), 'TE')
        p.wait('scalar', 'TE', te)
        p.op('scalar', lambda e: e.copy(gam_bc[:, 0:512], bank[0][:]), 'SE')
        se = p.op('scalar', lambda e: e.copy(gam_bc[:, 512:1024], bank[1][:]), 'SE')
        p.wait('sync', 'SE', se)
        v = p.dma('sync', strow[0:1, 0:1024], bet.rearrange("(a b) -> a b", a=1))
        p.wait('tensor', 'LD', v)
        p.wait('tensor', 'SE', se)
        te = p.op('tensor', lambda e: e.matmul(bank[0][:], onesrow[0:1, :], strow[0:1, 0:512],
                                               start=True, stop=True), 'TE')
        te = p.op('tensor', lambda e: e.matmul(bank[1][:], onesrow[0:1, :], strow[0:1, 512:1024],
                                               start=True, stop=True), 'TE')
        p.wait('scalar', 'TE', te)
        p.op('scalar', lambda e: e.copy(bet_bc[:, 0:512], bank[0][:]), 'SE')
        se = p.op('scalar', lambda e: e.copy(bet_bc[:, 512:1024], bank[1][:]), 'SE')
        p.wait('tensor', 'SE', se)
        for e_i in range(NE):
            dst = weT[e_i // 2]
            off = (e_i % 2) * 8192
            for rc in range(8):
                gv = p.dma('gpsimd', wstg3[:], fWe[e_i, rc * 128:(rc + 1) * 128, :])
                p.wait('tensor', 'GD', gv)
                for cc_ in range(8):
                    te = p.op('tensor', lambda e, cc_=cc_: e.transpose(
                        pbf[:, (cc_ % 4) * 128:(cc_ % 4 + 1) * 128],
                        wstg3[:, cc_ * 128:(cc_ + 1) * 128], identb[:]), 'TE')
                    p.wait('scalar', 'TE', te)
                    se = p.op('scalar', lambda e, cc_=cc_, rc=rc, dst=dst, off=off: e.copy(
                        dst[:, off + cc_ * 1024 + rc * 128: off + cc_ * 1024 + (rc + 1) * 128],
                        pbf[:, (cc_ % 4) * 128:(cc_ % 4 + 1) * 128]), 'SE')
                    p.wait('tensor', 'SE', se)
                p.wait('gpsimd', 'SE', se)
        p.barrier()

        te = p.op('tensor', lambda e: e.matmul(bank[6][:, 0:8], onesrow[0:1, :], offs_row[0:1, :],
                                               start=True, stop=True), 'TE')
        p.wait('scalar', 'TE', te)
        se = p.op('scalar', lambda e: e.copy(offs_bc[:], bank[6][:, 0:8]), 'SE')
        p.set_bank(6, 'SE')
        p.wait('vector', 'SE', se)

        def f3_load(tc, par):
            for dc in range(8):
                gv = p.dma('gpsimd', hbf_t2[par][:, dc * 128:(dc + 1) * 128],
                           hidT[dc * 128:(dc + 1) * 128, tc * 128:(tc + 1) * 128],
                           sem='f3h' + 'AB'[par])
            v = p.dma('sync', u2t_t[:], u2tok[tc * 128:(tc + 1) * 128, :], sem='f3u')
            v = p.dma('sync', xm_t[:], xmid[tc * 128:(tc + 1) * 128, :], sem='f3u')
            return gv, v

        def f3_load_u(tc):
            v = p.dma('sync', u2t_t[:], u2tok[tc * 128:(tc + 1) * 128, :], sem='f3u')
            v = p.dma('sync', xm_t[:], xmid[tc * 128:(tc + 1) * 128, :], sem='f3u')
            return v

        ld_h = [None, None]
        te_h = [0, 0]
        st_o = 0
        yinit_ve = 0
        ld_h[0], _u0 = f3_load(0, 0)
        ld_u = _u0
        for tc in range(32):
            par = tc % 2
            if tc + 1 < 32:
                p.wait('gpsimd', 'TE', te_h[1 - par])
                for dc in range(8):
                    gv = p.dma('gpsimd', hbf_t2[1 - par][:, dc * 128:(dc + 1) * 128],
                               hidT[dc * 128:(dc + 1) * 128, (tc + 1) * 128:(tc + 2) * 128],
                               sem='f3h' + 'AB'[1 - par])
                ld_h[1 - par] = gv
            hbf_t = hbf_t2[par]; yacc = yacc2[par]
            ct8 = ct82[par]; ctile = ctile2[par]
            p.wait('tensor', 'f3h' + 'AB'[par], ld_h[par])
            p.wait('scalar', 'VE', yinit_ve)
            for g in range(2):
                p.use_bank('pbf')
                for j in range(4):
                    dc = g * 4 + j
                    te = p.op('tensor', lambda e, dc=dc, j=j, hbf_t=hbf_t: e.transpose(
                        pbf[:, j * 128:(j + 1) * 128],
                        hbf_t[:, dc * 128:(dc + 1) * 128], identb[:]), 'TE')
                p.wait('scalar', 'TE', te)
                for j in range(4):
                    dc = g * 4 + j
                    se = p.op('scalar', lambda e, dc=dc, j=j, htok=htok: e.copy(
                        htok[:, dc * 128:(dc + 1) * 128], pbf[:, j * 128:(j + 1) * 128]), 'SE')
                p.set_bank('pbf', 'SE')
            se_htok = se
            ve = p.op('vector', lambda e, tc=tc, ctile=ctile: e.tensor_tensor(
                keptf[:], rank_all[:, tc * 8:(tc + 1) * 8], offs_bc[:], op=ALU.add), 'VE')
            ve = p.op('vector', lambda e: e.tensor_scalar(keptf[:], keptf[:], float(CAP), None,
                                                          op0=ALU.is_lt), 'VE')
            ve = p.op('vector', lambda e, tc=tc, ctile=ctile: e.tensor_tensor(
                ctile[:], keptf[:], oh_all[:, tc * 8:(tc + 1) * 8], op=ALU.mult), 'VE')
            ve = p.op('vector', lambda e, ctile=ctile: e.reduce_sum(r128a[:], ctile[:], axis=AX), 'VE')
            ve = p.op('vector', lambda e, tc=tc, ctile=ctile: e.tensor_scalar(
                ctile[:], ctile[:], rp_all[:, tc:tc + 1], None, op0=ALU.mult), 'VE')
            ve = p.op('vector', lambda e: e.tensor_scalar(r128a[:], r128a[:], -1.0, 1.0,
                                                          op0=ALU.mult, op1=ALU.add), 'VE')
            ve = p.op('vector', lambda e, tc=tc: e.tensor_tensor(
                r128b[:], r128a[:], rp_all[:, tc:tc + 1], op=ALU.mult), 'VE')
            p.wait('tensor', 'VE', ve)
            p.use_bank(6)
            te = p.op('tensor', lambda e, ctile=ctile: e.transpose(bank[6][0:8, 0:128], ctile[:],
                                                                   ident[:]), 'TE')
            p.wait('scalar', 'TE', te)
            p.wait('scalar', 'VE', ve)
            se = p.op('scalar', lambda e, ct8=ct8: e.copy(ct8[:], bank[6][0:8, 0:128]), 'SE')
            p.set_bank(6, 'SE')
            p.wait('vector', 'SE', se)
            ve = p.op('vector', lambda e, yacc=yacc: e.tensor_scalar(
                yacc[:], htok[:], r128b[:], None, op0=ALU.mult), 'VE')
            yinit_ve = ve
            p.wait('tensor', 'VE', ve)
            p.use_bank(5)
            te = p.op('tensor', lambda e, ct8=ct8: e.matmul(bank[5][:], ct8[:], fbe_sb[:, 0:512],
                                                            start=True, stop=True), 'TE')
            p.use_bank(6)
            te = p.op('tensor', lambda e, ct8=ct8: e.matmul(bank[6][:, 0:512], ct8[:],
                                                            fbe_sb[:, 512:1024],
                                                            start=True, stop=True), 'TE')
            p.wait('vector', 'TE', te)
            ve = p.op('vector', lambda e, yacc=yacc: e.tensor_tensor(yacc[:, 0:512], yacc[:, 0:512],
                                                                     bank[5][:], op=ALU.add), 'VE')
            p.set_bank(5, 'VE')
            ve = p.op('vector', lambda e, yacc=yacc: e.tensor_tensor(yacc[:, 512:1024],
                                                                     yacc[:, 512:1024],
                                                                     bank[6][:, 0:512], op=ALU.add), 'VE')
            p.set_bank(6, 'VE')
            for e_i in range(NE):
                wsrc = weT[e_i // 2]
                off = (e_i % 2) * 8192
                bk0 = (e_i % 2) * 2
                for half in range(2):
                    p.use_bank(bk0 + half)
                    for dc in range(8):
                        te = p.op('tensor', lambda e, dc=dc, half=half, off=off, wsrc=wsrc,
                                  bk0=bk0, hbf_t=hbf_t: e.matmul(
                            bank[bk0 + half][:], hbf_t[:, dc * 128:(dc + 1) * 128],
                            wsrc[:, off + dc * 1024 + half * 512: off + dc * 1024 + (half + 1) * 512],
                            start=(dc == 0), stop=(dc == 7)), 'TE')
                    if e_i == NE - 1 and half == 1:
                        te_h[par] = te
                    p.wait('vector', 'TE', te)
                    ve = p.op('vector', lambda e, e_i=e_i, half=half, bk0=bk0, yacc=yacc,
                              ctile=ctile: e.scalar_tensor_tensor(
                        yacc[:, half * 512:(half + 1) * 512], bank[bk0 + half][:],
                        ctile[:, e_i:e_i + 1], yacc[:, half * 512:(half + 1) * 512],
                        op0=ALU.mult, op1=ALU.add), 'VE')
                    p.set_bank(bk0 + half, 'VE')
            p.wait('vector', 'f3u', ld_u)
            ve = p.op('vector', lambda e, yacc=yacc: e.tensor_tensor(
                yacc[:], yacc[:], u2t_t[:], op=ALU.add), 'VE')
            p.wait('vector', 'fo', st_o)
            ve = ln_tok(yacc, ua3, LNE_T, tm3, gamma=gam_bc, beta=bet_bc, resid=xm_t)
            p.wait('sync', 'VE', ve)
            v = p.dma('sync', out[tc * 128:(tc + 1) * 128, :], ua3[:], sem='fo')
            st_o = v
            if tc + 1 < 32:
                p.wait('sync', 'VE', ve)
                ld_u = f3_load_u(tc + 1)
    p.wait('sync', 'fo', p.cur('fo'))
    return finish(nc, ges, p)


def finish(nc, ges, p):
    def replay(e, which):
        for eng, name, a, k, semh, n in p.ops:
            if eng != which:
                continue
            r = getattr(e, name)(*a, **k)
            if semh is not None:
                r.then_inc(semh, n)

    block = ges.enter_context(nc.Block())

    @block.sync
    def _(e):
        replay(e, 'sync')

    @block.gpsimd
    def _(e):
        replay(e, 'gpsimd')

    @block.scalar
    def _(e):
        replay(e, 'scalar')

    @block.vector
    def _(e):
        replay(e, 'vector')

    @block.tensor
    def _(e):
        replay(e, 'tensor')

    ges.close()
    return nc


def _consts():
    ident = np.eye(128, dtype=np.float32)
    sl = np.triu(np.ones((128, 128), np.float32), 1)
    masks = np.zeros((128, 4, 512), np.float32)
    tp = np.arange(128)[:, None]
    qf = np.arange(512)[None, :]
    for c in range(4):
        masks[:, c, :] = (tp <= qf - 128 * c).astype(np.float32)
    io8 = np.broadcast_to(np.arange(8, dtype=np.float32), (128, 8)).copy()
    e127 = np.zeros((128, 1), np.float32); e127[127, 0] = 1.0
    return (ident, sl, masks.reshape(128, 2048).astype(ml_dtypes.bfloat16), io8,
            (8.0 - io8).astype(np.float32), e127)


_CACHE = {}


def kernel(L0_attn_x, L0_attn_Wq, L0_attn_bq, L0_attn_Wk, L0_attn_bk, L0_attn_Wv,
           L0_attn_bv, L0_attn_Wo, L0_attn_bo, L0_attn_ln1_s, L0_attn_ln1_b,
           L0_attn_W1, L0_attn_b1, L0_attn_W2, L0_attn_b2, L0_attn_ln2_s,
           L0_attn_ln2_b, L0_ffn_Wq, L0_ffn_bq, L0_ffn_Wk, L0_ffn_bk, L0_ffn_Wv,
           L0_ffn_bv, L0_ffn_We, L0_ffn_be, L0_ffn_Wsw, L0_ffn_bsw, L0_ffn_gamma,
           L0_ffn_beta, _stage=None):
    x = np.ascontiguousarray(np.asarray(L0_attn_x, dtype=np.float32))
    ident, sl, masks, io8, rm8, e127c = _consts()
    f32 = lambda a: np.ascontiguousarray(np.asarray(a, dtype=np.float32))
    common = dict(
        aWq=f32(L0_attn_Wq), aWk=f32(L0_attn_Wk), aWv=f32(L0_attn_Wv), aWo=f32(L0_attn_Wo),
        aW1=f32(L0_attn_W1), aW2=f32(L0_attn_W2),
        abq=f32(L0_attn_bq), abk=f32(L0_attn_bk), abv=f32(L0_attn_bv), abo=f32(L0_attn_bo),
        ab1=f32(L0_attn_b1), ab2=f32(L0_attn_b2),
        aln1s=f32(L0_attn_ln1_s), aln1b=f32(L0_attn_ln1_b),
        aln2s=f32(L0_attn_ln2_s), aln2b=f32(L0_attn_ln2_b),
        fWq=f32(L0_ffn_Wq).reshape(1024, D), fWk=f32(L0_ffn_Wk).reshape(1024, D),
        fWv=f32(L0_ffn_Wv).reshape(1024, D),
        fbq=f32(L0_ffn_bq).reshape(-1), fbk=f32(L0_ffn_bk).reshape(-1),
        fbv=f32(L0_ffn_bv).reshape(-1),
        fWe=f32(L0_ffn_We), fbe=f32(L0_ffn_be), fWsw=f32(L0_ffn_Wsw),
        fbsw=f32(L0_ffn_bsw), gam=f32(L0_ffn_gamma), bet=f32(L0_ffn_beta),
        c_id=ident, c_sl=sl, c_mask=masks, c_io8=io8, c_rm8=rm8, c_e127=e127c,
    )
    if _stage not in _CACHE:
        _CACHE[_stage] = build_program(stage=_stage)
    nc = _CACHE[_stage]
    in_maps = []
    for c in range(NCORES):
        m = dict(common)
        m["xin"] = np.ascontiguousarray(x[2 * c:2 * c + 2].reshape(T, D))
        rmk = np.zeros((8, 1), np.float32)
        rmk[:c, 0] = 1.0
        m["c_rmask"] = rmk
        in_maps.append(m)
    res = run_bass_kernel_spmd(nc, in_maps, core_ids=list(range(NCORES)))
    return np.concatenate([np.asarray(r["out"]).reshape(2, S, D) for r in res.results], axis=0)
